# version 02
# baseline (speedup 1.0000x reference)
import math
import numpy as np
import concourse.bass as bass
import concourse.mybir as mybir
from concourse.bass_utils import run_bass_kernel_spmd

F32 = mybir.dt.float32
BF16 = mybir.dt.bfloat16
AF = mybir.ActivationFunctionType
ALU = mybir.AluOpType
AX = mybir.AxisListType

B = 2
S = 4096
T_SL = 1024
D = 2048
DC = D // 128
NEG = -1e30
EPS = 1e-5



class Sem:
    def __init__(self, handle):
        self.h = handle
        self.n = 0

    def tok(self):
        return (self, self.n)


class LB:

    ENGINES = ("tensor", "vector", "scalar", "gpsimd", "sync")
    AUTOCHAIN = {"vector": True, "scalar": True, "gpsimd": True,
                 "tensor": False, "sync": False}

    def __init__(self, name):
        self.nc = bass.Bass("TRN2", target_bir_lowering=False, debug=False)
        self.name = name
        self.q = {e: [] for e in self.ENGINES}
        self._ctx = []
        self._mem = []
        self._waited = {e: {} for e in self.ENGINES}
        self.auto = {}
        self._all_sems = []
        self._uniq = 0

    def scope(self):
        from contextlib import contextmanager

        @contextmanager
        def _s():
            mark = len(self._mem)
            yield
            for g in reversed(self._mem[mark:]):
                g.__exit__(None, None, None)
            del self._mem[mark:]
        return _s()

    def barrier(self):
        toks = [s.tok() for s in self._all_sems]
        for e in self.ENGINES:
            self.wait(e, toks)

    def eauto(self, eng):
        if eng not in self.auto:
            self.auto[eng] = self.sem(f"s_auto_{eng}")
        return self.auto[eng]

    def dram(self, name, shape, dtype, kind="Internal"):
        return self.nc.dram_tensor(name, list(shape), dtype, kind=kind).ap()

    def din(self, name, shape, dtype=F32):
        return self.dram(name, shape, dtype, kind="ExternalInput")

    def dout(self, name, shape, dtype=F32):
        return self.dram(name, shape, dtype, kind="ExternalOutput")

    def sbuf(self, name, shape, dtype):
        self._uniq += 1
        g = self.nc.sbuf_tensor(f"{name}_u{self._uniq}", list(shape), dtype)
        t = g.__enter__()
        self._mem.append(g)
        return t

    def psum(self, name, shape, dtype=F32):
        self._uniq += 1
        g = self.nc.psum_tensor(f"{name}_u{self._uniq}", list(shape), dtype)
        t = g.__enter__()
        self._mem.append(g)
        return t

    def sem(self, name):
        self._uniq += 1
        g = self.nc.semaphore(f"{name}_u{self._uniq}")
        h = g.__enter__()
        self._ctx.append(g)
        s = Sem(h)
        self._all_sems.append(s)
        return s

    def add(self, eng, fn):
        self.q[eng].append(fn)

    def wait(self, eng, tok):
        if tok is None:
            return
        if isinstance(tok, list):
            for t in tok:
                self.wait(eng, t)
            return
        sem, val = tok
        if val <= 0:
            return
        prev = self._waited[eng].get(sem, 0)
        if val <= prev:
            return
        self._waited[eng][sem] = val
        self.add(eng, lambda E, s=sem.h, v=val: E.wait_ge(s, v))

    def op(self, eng, fn, sem=None, n=1, chain=None):
        a = self.eauto(eng)
        if chain is None:
            chain = self.AUTOCHAIN[eng]
        if chain:
            self.wait(eng, a.tok())
        a.n += 1
        self.add(eng, lambda E, f=fn, s=a.h, n=a.n: f(E).then_inc(s, 1))
        return a.tok()

    def dma(self, eng, out, in_, sem, **kw):
        sem.n += 16
        self.add(eng, lambda E, o=out, i=in_, s=sem.h, kw=kw:
                 E.dma_start(out=o, in_=i, **kw).then_inc(s, 16))
        return sem.tok()

    def build(self):
        with self.nc.Block() as block:
            for e in self.ENGINES:
                ops = self.q[e]
                if not ops:
                    continue
                getattr(block, e)(lambda E, ops=ops: [f(E) for f in ops])
        for g in reversed(self._mem):
            g.__exit__(None, None, None)
        self._mem = []
        for g in reversed(self._ctx):
            g.__exit__(None, None, None)
        return self.nc


def make_consts(kb):
    g = kb.sem("s_consts")
    out = {}
    ident = kb.sbuf("c_ident", [128, 128], BF16)
    t0 = kb.op("gpsimd", lambda E, t=ident: E.memset(t[:, :], 0.0), g)
    kb.wait("gpsimd", t0)
    kb.op("gpsimd", lambda E, t=ident: E.affine_select(
        out=t[:, :], in_=t[:, :], compare_op=ALU.not_equal, fill=1.0,
        base=0, pattern=[[-1, 128]], channel_multiplier=1), g)
    out["ident"] = ident
    ones_col = kb.sbuf("c_ones_col", [128, 1], BF16)
    kb.op("gpsimd", lambda E, t=ones_col: E.memset(t[:, :], 1.0), g)
    out["ones_col"] = ones_col
    ones_row = kb.sbuf("c_ones_row", [1, 128], BF16)
    out["ones_row"] = ones_row
    kb.op("gpsimd", lambda E, t=ones_row: E.memset(t[:, :], 1.0), g)
    ones128 = kb.sbuf("c_ones128", [128, 128], BF16)
    out["ones128"] = ones128
    kb.op("gpsimd", lambda E, t=ones128: E.memset(t[:, :], 1.0), g)
    epsc = kb.sbuf("c_eps", [128, 1], F32)
    out["eps"] = epsc
    kb.op("gpsimd", lambda E, t=epsc: E.memset(t[:, :], EPS), g)
    out["tok"] = g.tok()
    return out


def load_wT(kb, sb, hbm, in_dim, sem, eng="gpsimd", cols=None):
    kc = in_dim // 128
    for k in range(kc):
        src = hbm[k * 128:(k + 1) * 128, :] if cols is None else \
            hbm[k * 128:(k + 1) * 128, cols[0]:cols[1]]
        tok = kb.dma(eng, sb[:, k, :], src, sem)
    return tok


def emit_transpose_fm(kb, dst, dst_cols, src_ap, ident, ps_tp, pp,
                      s_pe, s_ve, pre_waits_t=None, eng_ev="vector", s_ev=None):
    if pre_waits_t:
        kb.wait("tensor", pre_waits_t)
    tok_t = kb.op("tensor", lambda E: E.transpose(
        ps_tp[:, pp, 0:src_ap.shape[0]], src_ap, ident[:, :]), s_pe)
    kb.wait(eng_ev, tok_t)
    tok_v = kb.op(eng_ev, lambda E: E.tensor_copy(
        dst[:, dst_cols[0]:dst_cols[1]], ps_tp[:, pp, 0:src_ap.shape[0]]),
        s_ev or s_ve)
    return tok_t, tok_v



def build_pre0(stage="all", kb=None, io=None):
    own = kb is None
    if own:
        kb = LB("pre0")
        io = {}
    x_sl = io.get("x_sl") if "x_sl" in io else kb.din("x_sl", [T_SL, D])
    w_cq = io.get("w_cq") if "w_cq" in io else kb.din("w_cq", [D, 1536])
    w_ckv = io.get("w_ckv") if "w_ckv" in io else kb.din("w_ckv", [D, 512])
    w_kr = io.get("w_kr") if "w_kr" in io else kb.din("w_kr", [D, 64])
    qnw_in = io.get("q_norm_w") if "q_norm_w" in io else kb.din("q_norm_w", [1536])
    kvnw_in = io.get("kv_norm_w") if "kv_norm_w" in io else kb.din("kv_norm_w", [512])
    nq_out = io.get("nqT") if "nqT" in io else kb.dout("nqT", [1536, T_SL], BF16)
    nkv_out = io.get("nkvT") if "nkvT" in io else kb.dout("nkvT", [512, T_SL], BF16)
    kr_out = io.get("kropeT") if "kropeT" in io else kb.dout("kropeT", [64, T_SL], BF16)

    s_ld = kb.sem("s_ld")
    s_ldx = kb.sem("s_ldx")
    s_pe = kb.eauto("tensor")
    s_ve = kb.eauto("vector")
    s_se = kb.eauto("scalar")
    s_st = kb.sem("s_st")

    C = make_consts(kb)
    ident, ones128 = C["ident"], C["ones128"]

    xtm = kb.sbuf("xtm", [128, 8, D], BF16)
    xT = kb.sbuf("xT", [128, DC, T_SL], BF16)
    wcq_t = kb.sbuf("wcq_t", [128, DC, 1536], BF16)
    wckv_t = kb.sbuf("wckv_t", [128, DC, 512], BF16)
    wkr_t = kb.sbuf("wkr_t", [128, DC, 64], BF16)
    qnw = kb.sbuf("qnw", [128, 12], F32)
    kvnw = kb.sbuf("kvnw", [128, 4], F32)
    pre = kb.sbuf("pre", [128, 12, T_SL], BF16)
    tmp_ln = kb.sbuf("tmp_ln", [128, 512], F32)
    sq = kb.sbuf("sq", [128, 12, 512], BF16)
    rstd_bc = kb.sbuf("rstd_bc", [128, 2, 512], BF16)
    outbuf = kb.sbuf("outbuf", [128, 12, T_SL], BF16)
    ps_tp = kb.psum("ps_tp", [128, 2, 1024], BF16)
    ps_mm = kb.psum("ps_mm", [128, 2, 512], F32)
    ps_st = kb.psum("ps_st", [128, 2, 512], F32)

    t_x = None
    for i in range(8):
        t_x = kb.dma("gpsimd", xtm[:, i, :], x_sl[i * 128:(i + 1) * 128, :], s_ldx)
    load_wT(kb, wcq_t, w_cq, D, s_ld)
    load_wT(kb, wckv_t, w_ckv, D, s_ld)
    load_wT(kb, wkr_t, w_kr, D, s_ld)
    for j in range(12):
        kb.dma("gpsimd", qnw[:, j:j + 1], qnw_in[j * 128:(j + 1) * 128][:, None], s_ld)
    for j in range(4):
        kb.dma("gpsimd", kvnw[:, j:j + 1], kvnw_in[j * 128:(j + 1) * 128][:, None], s_ld)
    all_loads = s_ld.tok()

    kb.wait("tensor", t_x)
    kb.wait("tensor", C["tok"])
    ev_toks = [None, None]
    for idx in range(0 if stage == "lv6" else 8 * DC):
        i, k = divmod(idx, DC)
        pp = idx % 2
        kb.wait("tensor", ev_toks[pp])
        tok_t = kb.op("tensor", lambda E, i=i, k=k, pp=pp: E.transpose(
            ps_tp[:, pp, 0:128], xtm[:, i, k * 128:(k + 1) * 128], ident[:, :]), s_pe)
        kb.wait("vector", tok_t)
        ev_toks[pp] = kb.op("vector", lambda E, i=i, k=k, pp=pp: E.tensor_copy(
            xT[:, k, i * 128:(i + 1) * 128], ps_tp[:, pp, 0:128]), s_ve)
    xT_ready = s_ve.tok()

    def proj(wt, OC, nw_tile, out_ext, do_norm, mw=128, level=0):
        snap = [s_pe.tok(), s_ve.tok(), s_se.tok(), s_st.tok()]
        for e in ("tensor", "vector", "scalar"):
            kb.wait(e, snap)
        kb.wait("tensor", [xT_ready, all_loads])
        DIMN = OC * mw
        bank_free = [None, None]
        for si in range(2):
            tok_sq = None
            for j in range(OC):
                pp = j % 2
                kb.wait("tensor", bank_free[pp])
                def mm(E, j=j, si=si, pp=pp, wt=wt, mw=mw):
                    for k in range(DC - 1):
                        E.matmul(ps_mm[0:mw, pp, :],
                                 wt[:, k, j * mw:(j + 1) * mw],
                                 xT[:, k, si * 512:(si + 1) * 512],
                                 start=(k == 0), stop=False)
                    return E.matmul(ps_mm[0:mw, pp, :],
                                    wt[:, DC - 1, j * mw:(j + 1) * mw],
                                    xT[:, DC - 1, si * 512:(si + 1) * 512],
                                    start=False, stop=True)
                tok_mm = kb.op("tensor", mm)
                if do_norm:
                    kb.wait("scalar", tok_mm)
                    if level == 4:
                        tok_sq = kb.op("scalar", lambda E, j=j, pp=pp: E.copy(
                            sq[:, j, :], ps_mm[:, pp, :]))
                    elif level == 5:
                        tok_sq = tok_mm
                    else:
                        tok_sq = kb.op("scalar", lambda E, j=j, pp=pp: E.activation(
                            sq[:, j, :], ps_mm[:, pp, :], AF.Square))
                    kb.wait("vector", tok_sq)
                    tok_ev = kb.op("vector", lambda E, j=j, si=si, pp=pp:
                                   E.tensor_copy(pre[:, j, si * 512:(si + 1) * 512],
                                                 ps_mm[:, pp, :]))
                    bank_free[pp] = [tok_sq, tok_ev]
                else:
                    kb.wait("vector", tok_mm)
                    tok_ev = kb.op("vector", lambda E, si=si, pp=pp, mw=mw:
                                   E.tensor_copy(
                        outbuf[0:mw, 0, si * 512:(si + 1) * 512],
                        ps_mm[0:mw, pp, :]))
                    bank_free[pp] = [tok_ev]
            if not do_norm or level >= 3:
                continue
            kb.wait("tensor", tok_sq)
            def stats(E, si=si, OC=OC):
                for j in range(OC - 1):
                    E.matmul(ps_st[:, si, :], ones128[:, :], sq[:, j, :],
                             start=(j == 0), stop=False)
                return E.matmul(ps_st[:, si, :], ones128[:, :], sq[:, OC - 1, :],
                                start=(OC == 1), stop=True)
            tok_stats = kb.op("tensor", stats)
            kb.wait("scalar", tok_stats)
            kb.op("scalar", lambda E, si=si, DIMN=DIMN: E.activation(
                tmp_ln[:, :], ps_st[:, si, :], AF.Ln,
                bias=C["eps"][:, 0:1], scale=1.0 / DIMN))
            tok_r = kb.op("scalar", lambda E, si=si: E.activation(
                rstd_bc[:, si, :], tmp_ln[:, :], AF.Exp, scale=-0.5))
            kb.wait("vector", tok_r)
            for j in range(OC):
                if level >= 2:
                    kb.op("vector", lambda E, j=j, si=si: E.tensor_copy(
                        outbuf[:, j, si * 512:(si + 1) * 512],
                        pre[:, j, si * 512:(si + 1) * 512]))
                elif level == 1:
                    kb.op("vector", lambda E, j=j, si=si: E.tensor_mul(
                        outbuf[:, j, si * 512:(si + 1) * 512],
                        pre[:, j, si * 512:(si + 1) * 512],
                        rstd_bc[:, si, :]))
                else:
                    kb.op("vector", lambda E, j=j, si=si: E.scalar_tensor_tensor(
                        outbuf[:, j, si * 512:(si + 1) * 512],
                        pre[:, j, si * 512:(si + 1) * 512],
                        nw_tile[:, j:j + 1],
                        rstd_bc[:, si, :], ALU.mult, ALU.mult))
        tok_fin = kb.eauto("vector").tok()
        kb.wait("sync", tok_fin)
        t = None
        if do_norm:
            for j in range(OC):
                t = kb.dma("sync", out_ext[j * 128:(j + 1) * 128, :],
                           outbuf[:, j, 0:T_SL], s_st)
        else:
            t = kb.dma("sync", out_ext[:, :], outbuf[0:mw, 0, 0:T_SL], s_st)
        return t

    if stage == "tp":
        kb.wait("sync", xT_ready)
        kb.dma("sync", nq_out[0:128, :], xT[:, 0, 0:T_SL], s_st)
    elif stage == "krope":
        proj(wkr_t, 1, None, kr_out, False, mw=64)
    elif stage == "proj1":
        proj(wcq_t, 12, qnw, nq_out, True)
    elif stage == "lv6":
        proj(wcq_t, 12, qnw, nq_out, True, level=4)
    elif stage.startswith("lv"):
        proj(wcq_t, 12, qnw, nq_out, True, level=int(stage[2:]))
    else:
        proj(wcq_t, 12, qnw, nq_out, True)
        proj(wckv_t, 4, kvnw, nkv_out, True)
        proj(wkr_t, 1, None, kr_out, False, mw=64)
    kb.wait("sync", s_st.tok())
    return kb


def _v3(ap, rows=None):
    if len(ap.shape) == 3:
        return ap
    if ap.shape[1] == S:
        return ap.rearrange("a (r t) -> a r t", r=4)
    v = ap.rearrange("(r a) t -> a r t", r=4)
    return v if rows is None else v[rows[0]:rows[1]]



ASCALE = 1.0 / math.sqrt(192.0)
VSCALE = 1.0 / math.sqrt(2048.0)


def build_l0attn(n_heads=4, n_qt=32, kb=None, io=None):
    own = kb is None
    if own:
        kb = LB("l0attn")
        io = {}
    nq_in = _v3(io["nqT_full"] if "nqT_full" in io else
                kb.din("nqT_full", [1536, S], BF16))
    nkv_in = _v3(io["nkvT_full"] if "nkvT_full" in io else
                 kb.din("nkvT_full", [512, S], BF16))
    kr_in = _v3(io["kropeT_full"] if "kropeT_full" in io else
                kb.din("kropeT_full", [64, S], BF16))
    wdqn = io.get("wdqn_sl") if "wdqn_sl" in io else kb.din("wdqn_sl", [1536, 512])
    wdqr = io.get("wdqr_sl") if "wdqr_sl" in io else kb.din("wdqr_sl", [1536, 256])
    wdkn = io.get("wdkn_sl") if "wdkn_sl" in io else kb.din("wdkn_sl", [512, 512])
    wdv = io.get("wdv_sl") if "wdv_sl" in io else kb.din("wdv_sl", [512, 512])
    rc_in = io.get("rc") if "rc" in io else kb.din("rc", [4, 64, 64])
    rsp_in = io.get("rsp") if "rsp" in io else kb.din("rsp", [4, 64, 64])
    rsn_in = io.get("rsn") if "rsn" in io else kb.din("rsn", [4, 64, 64])
    attn_out = io.get("attnT") if "attnT" in io else kb.dout("attnT", [512, S], BF16)

    s_ldw = kb.sem("s_ldw")
    s_ldkv = kb.sem("s_ldkv")
    s_nq = [kb.sem("s_nq0"), kb.sem("s_nq1")]
    s_st = kb.sem("s_st")
    C = make_consts(kb)
    ident = C["ident"]
    cmask = kb.sbuf("cmask", [128, 128], BF16)
    kb.op("gpsimd", lambda E: E.memset(cmask[:, :], 0.0))
    tok_cm = kb.op("gpsimd", lambda E: E.affine_select(
        out=cmask[:, :], in_=cmask[:, :], compare_op=ALU.is_ge, fill=NEG,
        base=0, pattern=[[-1, 128]], channel_multiplier=1))

    nkvT = kb.sbuf("nkvT", [128, 4, S], BF16)
    kropeT = kb.sbuf("kropeT", [64, S], BF16)
    nqs = kb.sbuf("nqs", [128, 2, 12, 512], BF16)
    wdqn_t = kb.sbuf("wdqn_t", [128, 12, 512], BF16)
    wdqr_t = kb.sbuf("wdqr_t", [128, 12, 256], BF16)
    wdkn_t = kb.sbuf("wdkn_t", [128, 4, 512], BF16)
    wdv_t = kb.sbuf("wdv_t", [128, 4, 512], BF16)
    rcs = kb.sbuf("rcs", [64, 3, 4, 64], F32)
    rt = kb.sbuf("rt", [64, 4, 64], BF16)
    qnT = kb.sbuf("qnT", [128, S], BF16)
    qrT = kb.sbuf("qrT", [64, S], BF16)
    qr_raw = kb.sbuf("qr_raw", [64, S], BF16)
    knT = kb.sbuf("knT", [128, S], BF16)
    krh = kb.sbuf("krh", [64, S], BF16)
    v_tm = kb.sbuf("v_tm", [128, 32, 128], BF16)
    sc = kb.sbuf("sc", [128, S], F32)
    probs = kb.sbuf("probs", [128, S], BF16)
    probs2 = kb.sbuf("probs2", [128, S], BF16)
    pT = kb.sbuf("pT", [128, 32, 128], BF16)
    attn_h = kb.sbuf("attn_h", [128, S], BF16)
    cmax = kb.sbuf("cmax", [128, 8], F32)
    gmax = kb.sbuf("gmax", [128, 1], F32)
    negmax = kb.sbuf("negmax", [128, 1], F32)
    sums = kb.sbuf("sums", [128, 1], F32)
    rsum = kb.sbuf("rsum", [128, 1], F32)
    ps_mm = kb.psum("ps_mm", [128, 2, 512], F32)
    ps_tp = kb.psum("ps_tp", [128, 2, 1024], BF16)
    ps_pv = kb.psum("ps_pv", [128, 2, 512], F32)
    ps_x1 = kb.psum("ps_x1", [128, 1, 512], F32)

    for c4 in range(4):
        kb.dma("gpsimd", nkvT[:, c4, :].rearrange("p (r t) -> p r t", r=4),
               nkv_in[c4 * 128:(c4 + 1) * 128], s_ldkv)
    kb.dma("gpsimd", kropeT[:, :].rearrange("p (r t) -> p r t", r=4),
           kr_in[0:64], s_ldkv)
    load_wT(kb, wdqn_t, wdqn, 1536, s_ldw)
    load_wT(kb, wdqr_t, wdqr, 1536, s_ldw)
    load_wT(kb, wdkn_t, wdkn, 512, s_ldw)
    load_wT(kb, wdv_t, wdv, 512, s_ldw)
    for i, t in enumerate((rc_in, rsp_in, rsn_in)):
        for h in range(4):
            kb.dma("gpsimd", rcs[:, i, h, :], t[h, :, :], s_ldw)
    ldw_tok = s_ldw.tok()
    ldkv_tok = s_ldkv.tok()

    kb.wait("vector", ldw_tok)
    for h in range(4):
        kb.op("vector", lambda E, h=h: E.tensor_add(
            rt[:, h, :], rcs[:, 0, h, :], rcs[:, 1, h, :]))
        kb.op("vector", lambda E, h=h: E.tensor_sub(
            rt[:, h, :], rt[:, h, :], rcs[:, 2, h, :]))
    rt_tok = kb.eauto("vector").tok()

    nq_span_idx = 0
    nq_used = {}

    def head_attention(h):
        nonlocal nq_span_idx
        s_t, s_v, s_s = kb.eauto("tensor"), kb.eauto("vector"), kb.eauto("scalar")

        kb.wait("tensor", [ldw_tok, ldkv_tok, rt_tok, tok_cm])
        for sp in range(8):
            slot = nq_span_idx % 2
            prev = nq_used.get(slot)
            if prev is not None:
                kb.wait("sync", prev)
            for j in range(12):
                kb.dma("sync", nqs[:, slot, j, :],
                       nq_in[j * 128:(j + 1) * 128, sp // 2,
                             (sp % 2) * 512:(sp % 2) * 512 + 512],
                       s_nq[slot])
            kb.wait("tensor", s_nq[slot].tok())
            pp = sp % 2
            kb.wait("tensor", kb._l0_bank.get(("mm", pp)))
            def mmq(E, h=h, sp=sp, slot=slot, pp=pp):
                for j in range(11):
                    E.matmul(ps_mm[:, pp, :], wdqn_t[:, j, h * 128:(h + 1) * 128],
                             nqs[:, slot, j, :], start=(j == 0), stop=False)
                return E.matmul(ps_mm[:, pp, :], wdqn_t[:, 11, h * 128:(h + 1) * 128],
                                nqs[:, slot, 11, :], start=False, stop=True)
            tok_q = kb.op("tensor", mmq)
            kb.wait("vector", tok_q)
            ev_q = kb.op("vector", lambda E, sp=sp, pp=pp: E.tensor_scalar_mul(
                qnT[:, sp * 512:(sp + 1) * 512], ps_mm[:, pp, :], ASCALE))
            kb._l0_bank[("mm", pp)] = [ev_q]
            pp2 = sp % 2
            kb.wait("tensor", kb._l0_bank.get(("pv", pp2)))
            def mmqr(E, h=h, sp=sp, slot=slot, pp2=pp2):
                for j in range(11):
                    E.matmul(ps_pv[0:64, pp2, :], wdqr_t[:, j, h * 64:(h + 1) * 64],
                             nqs[:, slot, j, :], start=(j == 0), stop=False)
                return E.matmul(ps_pv[0:64, pp2, :], wdqr_t[:, 11, h * 64:(h + 1) * 64],
                                nqs[:, slot, 11, :], start=False, stop=True)
            tok_qr = kb.op("tensor", mmqr)
            nq_used[slot] = tok_qr
            nq_span_idx += 1
            kb.wait("vector", tok_qr)
            ev_qr = kb.op("vector", lambda E, sp=sp, pp2=pp2: E.tensor_copy(
                qr_raw[:, sp * 512:(sp + 1) * 512], ps_pv[0:64, pp2, :]))
            kb._l0_bank[("pv", pp2)] = [ev_qr]
        for sp in range(8):
            pp2 = sp % 2
            kb.wait("tensor", kb._l0_bank.get(("pv", pp2)))
            kb.wait("tensor", kb.eauto("vector").tok())
            tok_r = kb.op("tensor", lambda E, h=h, sp=sp, pp2=pp2: E.matmul(
                ps_pv[0:64, pp2, :], rt[:, h, :],
                qr_raw[:, sp * 512:(sp + 1) * 512], start=True, stop=True))
            kb.wait("vector", tok_r)
            ev = kb.op("vector", lambda E, sp=sp, pp2=pp2: E.tensor_scalar_mul(
                qrT[:, sp * 512:(sp + 1) * 512], ps_pv[0:64, pp2, :], ASCALE))
            kb._l0_bank[("pv", pp2)] = [ev]
        for sp in range(8):
            pp = sp % 2
            kb.wait("tensor", kb._l0_bank.get(("mm", pp)))
            def mmk(E, h=h, sp=sp, pp=pp):
                for j in range(3):
                    E.matmul(ps_mm[:, pp, :], wdkn_t[:, j, h * 128:(h + 1) * 128],
                             nkvT[:, j, sp * 512:(sp + 1) * 512],
                             start=(j == 0), stop=False)
                return E.matmul(ps_mm[:, pp, :], wdkn_t[:, 3, h * 128:(h + 1) * 128],
                                nkvT[:, 3, sp * 512:(sp + 1) * 512],
                                start=False, stop=True)
            tok_k = kb.op("tensor", mmk)
            kb.wait("vector", tok_k)
            ev_k = kb.op("vector", lambda E, sp=sp, pp=pp: E.tensor_copy(
                knT[:, sp * 512:(sp + 1) * 512], ps_mm[:, pp, :]))
            kb._l0_bank[("mm", pp)] = [ev_k]
            pp2 = sp % 2
            kb.wait("tensor", kb._l0_bank.get(("pv", pp2)))
            tok_kr = kb.op("tensor", lambda E, h=h, sp=sp, pp2=pp2: E.matmul(
                ps_pv[0:64, pp2, :], rt[:, h, :],
                kropeT[:, sp * 512:(sp + 1) * 512], start=True, stop=True))
            kb.wait("vector", tok_kr)
            ev_kr = kb.op("vector", lambda E, sp=sp, pp2=pp2: E.tensor_copy(
                krh[:, sp * 512:(sp + 1) * 512], ps_pv[0:64, pp2, :]))
            kb._l0_bank[("pv", pp2)] = [ev_kr]
        for tt in range(32):
            kb.wait("tensor", kb._l0_bank.get(("x1", 0)))
            def mmv(E, h=h, tt=tt):
                for j in range(3):
                    E.matmul(ps_x1[:, 0, 0:128], nkvT[:, j, tt * 128:(tt + 1) * 128],
                             wdv_t[:, j, h * 128:(h + 1) * 128],
                             start=(j == 0), stop=False)
                return E.matmul(ps_x1[:, 0, 0:128], nkvT[:, 3, tt * 128:(tt + 1) * 128],
                                wdv_t[:, 3, h * 128:(h + 1) * 128],
                                start=False, stop=True)
            tok_v = kb.op("tensor", mmv)
            kb.wait("vector", tok_v)
            ev_v = kb.op("vector", lambda E, tt=tt: E.tensor_scalar_mul(
                v_tm[:, tt, :], ps_x1[:, 0, 0:128], VSCALE))
            kb._l0_bank[("x1", 0)] = [ev_v]

        tok_exp_prev = None
        tok_n_prev = None
        tok_tp_prev = None
        for qt in range(n_qt):
            klen = (qt + 1) * 128
            nch = (klen + 511) // 512
            if tok_exp_prev is not None:
                kb.wait("vector", tok_exp_prev)
            for c in range(nch):
                w = min(512, klen - c * 512)
                pp = c % 2
                kb.wait("tensor", kb._l0_bank.get(("mm", pp)))
                def mmsc(E, h=h, qt=qt, c=c, w=w, pp=pp):
                    E.matmul(ps_mm[:, pp, 0:w],
                             qnT[:, qt * 128:(qt + 1) * 128],
                             knT[:, c * 512:c * 512 + w], start=True, stop=False)
                    return E.matmul(ps_mm[:, pp, 0:w],
                                    qrT[0:64, qt * 128:(qt + 1) * 128],
                                    krh[0:64, c * 512:c * 512 + w],
                                    start=False, stop=True)
                tok_s = kb.op("tensor", mmsc)
                kb.wait("vector", tok_s)
                if c == nch - 1:
                    if w > 128:
                        kb.op("vector", lambda E, c=c, w=w, pp=pp: E.tensor_copy(
                            sc[:, c * 512:c * 512 + w - 128], ps_mm[:, pp, 0:w - 128]))
                    kb.op("vector", lambda E, c=c, w=w, pp=pp: E.tensor_add(
                        sc[:, c * 512 + w - 128:c * 512 + w],
                        ps_mm[:, pp, w - 128:w], cmask[:, :]))
                else:
                    kb.op("vector", lambda E, c=c, w=w, pp=pp: E.tensor_copy(
                        sc[:, c * 512:c * 512 + w], ps_mm[:, pp, 0:w]))
                ev_s = kb.op("vector", lambda E, c=c, w=w: E.reduce_max(
                    cmax[:, c:c + 1], sc[:, c * 512:c * 512 + w], AX.X))
                kb._l0_bank[("mm", pp)] = [ev_s]
            kb.op("vector", lambda E, nch=nch: E.reduce_max(
                gmax[:, 0:1], cmax[:, 0:nch], AX.X))
            tok_nm = kb.op("vector", lambda E: E.tensor_scalar_mul(
                negmax[:, 0:1], gmax[:, 0:1], -1.0))
            kb.wait("scalar", tok_nm)
            kb.wait("scalar", tok_n_prev)
            tok_exp = kb.op("scalar", lambda E, klen=klen: E.activation(
                probs[:, 0:klen], sc[:, 0:klen], AF.Exp,
                bias=negmax[:, 0:1], accum_out=sums[:, 0:1]))
            tok_exp_prev = tok_exp
            kb.wait("vector", tok_exp)
            kb.op("vector", lambda E: E.reciprocal(rsum[:, 0:1], sums[:, 0:1]))
            kb.wait("vector", tok_tp_prev)
            tok_n = kb.op("vector", lambda E, klen=klen: E.tensor_scalar_mul(
                probs2[:, 0:klen], probs[:, 0:klen], rsum[:, 0:1]))
            tok_n_prev = tok_n
            for kt in range(qt + 1):
                ppt = kt % 2
                kb.wait("tensor", kb._l0_bank.get(("tp", ppt)))
                kb.wait("tensor", tok_n)
                tok_t = kb.op("tensor", lambda E, kt=kt, ppt=ppt: E.transpose(
                    ps_tp[:, ppt, 0:128], probs2[:, kt * 128:(kt + 1) * 128],
                    ident[:, :]))
                tok_tp_prev = tok_t
                kb.wait("vector", tok_t)
                ev_t = kb.op("vector", lambda E, kt=kt, ppt=ppt: E.tensor_copy(
                    pT[:, kt, :], ps_tp[:, ppt, 0:128]))
                kb._l0_bank[("tp", ppt)] = [ev_t]
            ppv = qt % 2
            kb.wait("tensor", kb._l0_bank.get(("pv", ppv)))
            kb.wait("tensor", kb.eauto("vector").tok())
            def mmpv(E, qt=qt, ppv=ppv):
                r = None
                for kt in range(qt + 1):
                    r = E.matmul(ps_pv[:, ppv, 0:128], v_tm[:, kt, :],
                                 pT[:, kt, :], start=(kt == 0), stop=(kt == qt))
                return r
            tok_pv = kb.op("tensor", mmpv)
            kb.wait("vector", tok_pv)
            ev_pv = kb.op("vector", lambda E, qt=qt, ppv=ppv: E.tensor_copy(
                attn_h[:, qt * 128:(qt + 1) * 128], ps_pv[:, ppv, 0:128]))
            kb._l0_bank[("pv", ppv)] = [ev_pv]
        kb.wait("sync", kb.eauto("vector").tok())
        kb.dma("sync", attn_out[h * 128:(h + 1) * 128, 0:n_qt * 128], attn_h[:, 0:n_qt * 128], s_st)
        kb._l0_bank[("attn_h",)] = s_st.tok()

    kb._l0_bank = {}
    for h in range(n_heads):
        prev_st = kb._l0_bank.get(("attn_h",))
        if prev_st is not None:
            kb.wait("vector", prev_st)
        head_attention(h)
    kb.wait("sync", s_st.tok())
    return kb



def build_post(name, with_proj, x_token_major, final=False, gating=False,
               n_parts=4, kb=None, io=None):
    own = kb is None
    if own:
        kb = LB(name)
        io = {}
    if with_proj:
        in_sl_d = io.get("inT_sl") if "inT_sl" in io else \
            kb.din("inT_sl", [D, T_SL], BF16)
        w_in = io.get("w") if "w" in io else kb.din("w", [D, D])
    else:
        parts = io.get("parts") if "parts" in io else \
            kb.din("parts", [n_parts, D, T_SL])
    if gating:
        gw_in = io.get("gate_w") if "gate_w" in io else kb.din("gate_w", [8, D])
        a2a_comb = io.get("a2a_combine")
        if a2a_comb is None:
            comb_out = io.get("combine") if "combine" in io else \
                kb.dout("combine", [T_SL, 8])
    if x_token_major:
        x_in = io.get("x_sl") if "x_sl" in io else kb.din("x_sl", [T_SL, D])
    else:
        x_in = io.get("xT_prev") if "xT_prev" in io else \
            kb.din("xT_prev", [D, T_SL])
    if final:
        out_f = io.get("x_out") if "x_out" in io else kb.dout("x_out", [T_SL, D])
    else:
        out_f = io.get("xT_new") if "xT_new" in io else \
            kb.dout("xT_new", [D, T_SL])
        out_b = io.get("xT_new_bf") if "xT_new_bf" in io else \
            kb.dout("xT_new_bf", [D, T_SL], BF16)

    s_ldw = kb.sem("s_ldw")
    s_lin = kb.sem("s_lin")
    s_st = kb.sem("s_st")
    s_xs = [kb.sem("s_xs0"), kb.sem("s_xs1")]
    C = make_consts(kb)
    ident, ones128 = C["ident"], C["ones128"]

    if with_proj:
        w_t = kb.sbuf("w_t", [128, DC, D], BF16)
        in_sl = kb.sbuf("in_sl", [128, DC, T_SL], BF16)
    else:
        parts_sb = kb.sbuf("parts_sb", [128, 2, 4, 512], F32)
    x_sb = kb.sbuf("x_sb", [128, 2, 512], F32)
    if x_token_major:
        xtm_s = kb.sbuf("xtm_s", [128, 2, 4, 128], F32)
        identf = kb.sbuf("identf", [128, 128], F32)
        t0 = kb.op("gpsimd", lambda E: E.memset(identf[:, :], 0.0))
        kb.wait("gpsimd", t0)
        kb.op("gpsimd", lambda E: E.affine_select(
            out=identf[:, :], in_=identf[:, :], compare_op=ALU.not_equal,
            fill=1.0, base=0, pattern=[[-1, 128]], channel_multiplier=1))
        ps_tpx = kb.psum("ps_tpx", [128, 2, 512], F32)
    if final:
        identf = kb.sbuf("identf", [128, 128], F32)
        t0 = kb.op("gpsimd", lambda E: E.memset(identf[:, :], 0.0))
        kb.wait("gpsimd", t0)
        kb.op("gpsimd", lambda E: E.affine_select(
            out=identf[:, :], in_=identf[:, :], compare_op=ALU.not_equal,
            fill=1.0, base=0, pattern=[[-1, 128]], channel_multiplier=1))
        ps_tpf = kb.psum("ps_tpf", [128, 2, 512], F32)
        otm = kb.sbuf("otm", [128, 4, 128], F32)
    consts_tok = kb.eauto("gpsimd").tok()
    r_f = kb.sbuf("r_f", [128, DC, 512], F32)
    r_bf = kb.sbuf("r_bf", [128, DC, 512], BF16)
    sq = kb.sbuf("sq2", [128, DC, 512], BF16)
    mu = kb.sbuf("mu", [128, 512], F32)
    ms = kb.sbuf("ms", [128, 512], F32)
    var = kb.sbuf("var", [128, 512], F32)
    rstd = kb.sbuf("rstd", [128, 512], F32)
    scr = kb.sbuf("scr", [128, 512], F32)
    obf = kb.sbuf("obf", [128, 512], BF16)
    ps_mm = kb.psum("ps_mm2", [128, 2, 512], F32)
    ps_st = kb.psum("ps_st2", [128, 2, 512], F32)

    if gating:
        gw_sb = kb.sbuf("gw_sb", [8, D], F32)
        gwT = kb.sbuf("gwT", [128, DC, 8], F32)
        identg = kb.sbuf("identg", [128, 128], F32)
        tg = kb.op("gpsimd", lambda E: E.memset(identg[:, :], 0.0))
        kb.wait("gpsimd", tg)
        kb.op("gpsimd", lambda E: E.affine_select(
            out=identg[:, :], in_=identg[:, :], compare_op=ALU.not_equal,
            fill=1.0, base=0, pattern=[[-1, 128]], channel_multiplier=1))
        gconsts = kb.eauto("gpsimd").tok()
        s_gw = kb.sem("s_gw")
        kb.dma("gpsimd", gw_sb[:, :], gw_in[:, :], s_gw)
        lg = kb.sbuf("lg", [128, 8], F32)
        m1 = kb.sbuf("m1", [128, 1], F32)
        m2 = kb.sbuf("m2", [128, 1], F32)
        eqm = kb.sbuf("eqm", [128, 8], F32)
        l2t = kb.sbuf("l2t", [128, 8], F32)
        selt = kb.sbuf("selt", [128, 8], F32)
        negm1 = kb.sbuf("negm1", [128, 1], F32)
        expt = kb.sbuf("expt", [128, 8], F32)
        sumt = kb.sbuf("sumt", [128, 1], F32)
        rst = kb.sbuf("rst", [128, 1], F32)
        comb_sb = kb.sbuf("comb_sb", [128, 8], F32)
        ps_g = kb.psum("ps_g", [128, 1, 512], F32)
        kb.wait("tensor", [s_gw.tok(), gconsts])
        for k in range(DC):
            tok_t = kb.op("tensor", lambda E, k=k: E.transpose(
                ps_g[:, 0, 0:8], gw_sb[0:8, k * 128:(k + 1) * 128],
                identg[0:8, 0:8]))
            kb.wait("vector", tok_t)
            ev = kb.op("vector", lambda E, k=k: E.tensor_copy(
                gwT[:, k, :], ps_g[:, 0, 0:8]))
            kb.wait("tensor", ev)
        gwT_tok = kb.eauto("vector").tok()

        def emit_gating(si, u, r_f, bank, r_f_done):
            kb.wait("vector", s_st.tok())
            kb.wait("tensor", r_f_done)
            kb.wait("tensor", [gwT_tok, bank.get(("gt",))])
            def mmg(E, si=si, u=u):
                for k in range(DC - 1):
                    E.matmul(ps_g[:, 0, 0:8], r_f[:, k, u * 128:(u + 1) * 128],
                             gwT[:, k, :], start=(k == 0), stop=False)
                return E.matmul(ps_g[:, 0, 0:8],
                                r_f[:, DC - 1, u * 128:(u + 1) * 128],
                                gwT[:, DC - 1, :], start=False, stop=True)
            tok_g = kb.op("tensor", mmg)
            kb.wait("vector", tok_g)
            kb.op("vector", lambda E: E.tensor_copy(lg[:, :], ps_g[:, 0, 0:8]))
            kb.op("vector", lambda E: E.reduce_max(m1[:, 0:1], lg[:, :], AX.X))
            kb.op("vector", lambda E: E.tensor_single_scalar(
                eqm[:, :], lg[:, :], m1[:, 0:1], ALU.is_equal))
            kb.op("vector", lambda E: E.scalar_tensor_tensor(
                l2t[:, :], eqm[:, :], -1e30, lg[:, :], ALU.mult, ALU.add))
            kb.op("vector", lambda E: E.reduce_max(m2[:, 0:1], l2t[:, :], AX.X))
            kb.op("vector", lambda E: E.tensor_single_scalar(
                selt[:, :], lg[:, :], m2[:, 0:1], ALU.is_ge))
            tok_v = kb.op("vector", lambda E: E.tensor_scalar_mul(
                negm1[:, 0:1], m1[:, 0:1], -1.0))
            kb.wait("scalar", tok_v)
            tok_e = kb.op("scalar", lambda E: E.activation(
                expt[:, :], lg[:, :], AF.Exp, bias=negm1[:, 0:1]))
            kb.wait("vector", tok_e)
            kb.op("vector", lambda E: E.tensor_mul(l2t[:, :], expt[:, :],
                                                   selt[:, :]))
            kb.op("vector", lambda E: E.reduce_sum(sumt[:, 0:1], l2t[:, :], AX.X))
            kb.op("vector", lambda E: E.reciprocal(rst[:, 0:1], sumt[:, 0:1]))
            tok_c = kb.op("vector", lambda E: E.tensor_scalar_mul(
                comb_sb[:, :], l2t[:, :], rst[:, 0:1]))
            bank[("gt",)] = [tok_g, tok_c]
            kb.wait("sync", tok_c)
            if a2a_comb is not None:
                for jj in range(4):
                    kb.dma("sync",
                           a2a_comb[1024 * jj + si * 512 + u * 128:
                                    1024 * jj + si * 512 + (u + 1) * 128, :],
                           comb_sb[:, 2 * jj:2 * jj + 2], s_st)
            else:
                kb.dma("sync",
                       comb_out[si * 512 + u * 128:si * 512 + (u + 1) * 128, :],
                       comb_sb[:, :], s_st)

    if with_proj:
        load_wT(kb, w_t, w_in, D, s_ldw)
        for k in range(DC):
            kb.dma("gpsimd", in_sl[:, k, :], in_sl_d[k * 128:(k + 1) * 128, :],
                   s_lin)
    ldw_tok = s_ldw.tok()
    lin_tok = s_lin.tok()

    bank = {}
    x_used = {}
    for si in range(2):
        for j in range(DC):
            slot = (si * DC + j) % 2
            prev = x_used.get(slot)
            if prev is not None:
                kb.wait("sync", prev)
            if x_token_major:
                for u in range(4):
                    kb.dma("sync", xtm_s[:, slot, u, :],
                           x_in[si * 512 + u * 128:si * 512 + (u + 1) * 128,
                                j * 128:(j + 1) * 128], s_xs[slot])
            else:
                kb.dma("sync", x_sb[:, slot, 0:512],
                       x_in[j * 128:(j + 1) * 128, si * 512:(si + 1) * 512],
                       s_xs[slot])
            if not with_proj:
                for p4 in range(n_parts):
                    src_ap = parts[p4, j * 128:(j + 1) * 128,
                                   si * 512:(si + 1) * 512] \
                        if len(parts.shape) == 3 else \
                        parts[j * 128:(j + 1) * 128, si * 512:(si + 1) * 512]
                    kb.dma("sync", parts_sb[:, slot, p4, :], src_ap, s_xs[slot])
            if x_token_major:
                kb.wait("tensor", s_xs[slot].tok())
                kb.wait("tensor", consts_tok)
                for u in range(4):
                    ppt = u % 2
                    kb.wait("tensor", bank.get(("tpx", ppt)))
                    tok_t = kb.op("tensor", lambda E, slot=slot, u=u, ppt=ppt:
                                  E.transpose(ps_tpx[:, ppt, 0:128],
                                              xtm_s[:, slot, u, :], identf[:, :]))
                    kb.wait("vector", tok_t)
                    ev = kb.op("vector", lambda E, slot=slot, u=u, ppt=ppt:
                               E.tensor_copy(x_sb[:, slot, u * 128:(u + 1) * 128],
                                             ps_tpx[:, ppt, 0:128]))
                    bank[("tpx", ppt)] = [ev]
                x_ready = kb.eauto("vector").tok()
            else:
                kb.wait("vector", s_xs[slot].tok())
                x_ready = None
            if with_proj:
                pp = (si * DC + j) % 2
                kb.wait("tensor", [lin_tok, ldw_tok])
                kb.wait("tensor", bank.get(("mm", pp)))
                def mmp(E, j=j, si=si, pp=pp):
                    for k in range(DC - 1):
                        E.matmul(ps_mm[:, pp, :], w_t[:, k, j * 128:(j + 1) * 128],
                                 in_sl[:, k, si * 512:(si + 1) * 512],
                                 start=(k == 0), stop=False)
                    return E.matmul(ps_mm[:, pp, :],
                                    w_t[:, DC - 1, j * 128:(j + 1) * 128],
                                    in_sl[:, DC - 1, si * 512:(si + 1) * 512],
                                    start=False, stop=True)
                tok_mm = kb.op("tensor", mmp)
                kb.wait("vector", [tok_mm, x_ready])
                tok_r = kb.op("vector", lambda E, j=j, slot=slot, pp=pp:
                              E.tensor_add(r_f[:, j, :], ps_mm[:, pp, :],
                                           x_sb[:, slot, 0:512]))
                bank[("mm", pp)] = [tok_r]
            else:
                kb.wait("vector", s_xs[slot].tok())
                if x_ready is not None:
                    kb.wait("vector", x_ready)
                if n_parts == 1:
                    tok_r = kb.op("vector", lambda E, j=j, slot=slot: E.tensor_add(
                        r_f[:, j, :], x_sb[:, slot, 0:512],
                        parts_sb[:, slot, 0, :]))
                else:
                    kb.op("vector", lambda E, slot=slot: E.tensor_add(
                        scr[:, :], x_sb[:, slot, 0:512], parts_sb[:, slot, 0, :]))
                    kb.op("vector", lambda E, j=j, slot=slot: E.tensor_add(
                        r_f[:, j, :], scr[:, :], parts_sb[:, slot, 1, :]))
                    kb.op("vector", lambda E, j=j, slot=slot: E.tensor_add(
                        scr[:, :], r_f[:, j, :], parts_sb[:, slot, 2, :]))
                    tok_r = kb.op("vector", lambda E, j=j, slot=slot: E.tensor_add(
                        r_f[:, j, :], scr[:, :], parts_sb[:, slot, 3, :]))
            x_used[slot] = tok_r
            kb.op("vector", lambda E, j=j: E.tensor_copy(r_bf[:, j, :],
                                                         r_f[:, j, :]))
            kb.wait("scalar", kb.eauto("vector").tok())
            kb.op("scalar", lambda E, j=j: E.activation(sq[:, j, :], r_f[:, j, :],
                                                        AF.Square))
        sq_tok = kb.eauto("scalar").tok()
        rbf_tok = kb.eauto("vector").tok()
        kb.wait("tensor", [sq_tok, rbf_tok])
        def statsA(E):
            for j in range(DC - 1):
                E.matmul(ps_st[:, 0, :], ones128[:, :], r_bf[:, j, :],
                         start=(j == 0), stop=False)
            return E.matmul(ps_st[:, 0, :], ones128[:, :], r_bf[:, DC - 1, :],
                            start=False, stop=True)
        tokA = kb.op("tensor", statsA)
        def statsB(E):
            for j in range(DC - 1):
                E.matmul(ps_st[:, 1, :], ones128[:, :], sq[:, j, :],
                         start=(j == 0), stop=False)
            return E.matmul(ps_st[:, 1, :], ones128[:, :], sq[:, DC - 1, :],
                            start=False, stop=True)
        tokB = kb.op("tensor", statsB)
        kb.wait("vector", tokA)
        kb.op("vector", lambda E: E.tensor_scalar_mul(mu[:, :], ps_st[:, 0, :],
                                                      1.0 / D))
        kb.wait("vector", tokB)
        tok_ms = kb.op("vector", lambda E: E.tensor_scalar_mul(
            ms[:, :], ps_st[:, 1, :], 1.0 / D))
        kb.op("vector", lambda E: E.tensor_mul(scr[:, :], mu[:, :], mu[:, :]))
        tok_var = kb.op("vector", lambda E: E.tensor_sub(var[:, :], ms[:, :],
                                                         scr[:, :]))
        bank[("st",)] = [tok_ms]
        kb.wait("scalar", tok_var)
        kb.op("scalar", lambda E: E.activation(ms[:, :], var[:, :], AF.Ln,
                                               bias=C["eps"][:, 0:1]))
        tok_rstd = kb.op("scalar", lambda E: E.activation(
            rstd[:, :], ms[:, :], AF.Exp, scale=-0.5))
        kb.wait("vector", tok_rstd)
        for j in range(DC):
            kb.op("vector", lambda E, j=j: E.tensor_sub(scr[:, :], r_f[:, j, :],
                                                        mu[:, :]))
            tok_o = kb.op("vector", lambda E, j=j: E.tensor_mul(
                r_f[:, j, :], scr[:, :], rstd[:, :]))
            if final:
                for u in range(4):
                    ppt = u % 2
                    kb.wait("tensor", bank.get(("tpo", ppt)))
                    kb.wait("tensor", [tok_o, consts_tok])
                    tok_t = kb.op("tensor", lambda E, j=j, u=u, ppt=ppt:
                                  E.transpose(ps_tpf[:, ppt, 0:128],
                                              r_f[:, j, u * 128:(u + 1) * 128],
                                              identf[:, :]))
                    kb.wait("vector", tok_t)
                    ev = kb.op("vector", lambda E, j=j, u=u, ppt=ppt:
                               E.tensor_copy(otm[:, u, :], ps_tpf[:, ppt, 0:128]))
                    bank[("tpo", ppt)] = [ev]
                kb.wait("sync", kb.eauto("vector").tok())
                for u in range(4):
                    kb.dma("sync",
                           out_f[si * 512 + u * 128:si * 512 + (u + 1) * 128,
                                 j * 128:(j + 1) * 128], otm[:, u, :], s_st)
                kb.wait("vector", s_st.tok())
            else:
                tok_ob = kb.op("vector", lambda E, j=j: E.tensor_copy(
                    obf[:, :], r_f[:, j, :]))
                kb.wait("sync", tok_ob)
                kb.dma("sync", out_f[j * 128:(j + 1) * 128,
                                     si * 512:(si + 1) * 512], r_f[:, j, :], s_st)
                kb.dma("sync", out_b[j * 128:(j + 1) * 128,
                                     si * 512:(si + 1) * 512], obf[:, :], s_st)
                kb.wait("vector", s_st.tok())
        if gating:
            r_f_done = kb.eauto("vector").tok()
            for u in range(4):
                emit_gating(si, u, r_f, bank, r_f_done)
            kb.wait("vector", bank.get(("gt",)))
        kb.wait("tensor", bank.get(("st",)))
        kb.wait("vector", s_st.tok())
    kb.wait("sync", s_st.tok())
    return kb



def build_moe(n_spans=8, kb=None, io=None):
    own = kb is None
    if own:
        kb = LB("moe")
        io = {}
    x_in = _v3(io["xT_bf_full"] if "xT_bf_full" in io else
               kb.din("xT_bf_full", [D, S], BF16))
    w1_in = io.get("w1_sl") if "w1_sl" in io else kb.din("w1_sl", [2, 2816, D])
    w2_in = io.get("w2_sl") if "w2_sl" in io else kb.din("w2_sl", [2, D, 1408])
    comb_in = io.get("combine_sl") if "combine_sl" in io else \
        kb.din("combine_sl", [S, 8])
    esel_in = io.get("esel") if "esel" in io else kb.din("esel", [8, 2])
    out_p = _v3(io["moeP"] if "moeP" in io else kb.dout("moeP", [D, S]))

    s_cast = kb.sem("s_cast")
    s_w1 = kb.sem("s_w1")
    s_w2 = kb.sem("s_w2")
    s_x = [kb.sem("s_x0"), kb.sem("s_x1")]
    s_cb = kb.sem("s_cb")
    s_st = kb.sem("s_st")
    C = make_consts(kb)
    ident, ones_row = C["ident"], C["ones_row"]

    identf = kb.sbuf("identf_m", [128, 128], F32)
    tg = kb.op("gpsimd", lambda E: E.memset(identf[:, :], 0.0))
    kb.wait("gpsimd", tg)
    kb.op("gpsimd", lambda E: E.affine_select(
        out=identf[:, :], in_=identf[:, :], compare_op=ALU.not_equal,
        fill=1.0, base=0, pattern=[[-1, 128]], channel_multiplier=1))
    identf_tok = kb.eauto("gpsimd").tok()

    w1_bf = kb.dram("w1_bf", [2, 2816, D], BF16)
    w2_bf = kb.dram("w2_bf", [2, D, 1408], BF16)

    xs_sb = kb.sbuf("xs_sb", [128, 2, DC, 512], BF16)
    w1T = kb.sbuf("w1T", [128, DC, 1408], BF16)
    w2T = kb.sbuf("w2T", [128, 2, 11, D], BF16)
    sg = kb.sbuf("sg", [128, 11, 512], BF16)
    act = kb.sbuf("act", [128, 2, 11, 512], BF16)
    cb_sb = kb.sbuf("cb_sb", [128, 4, 8], F32)
    cbT_bf = kb.sbuf("cbT_bf", [8, 128], BF16)
    esel_bf = kb.sbuf("esel_bf", [8, 2], BF16)
    ce_row = kb.sbuf("ce_row", [1, 2, 512], BF16)
    cbc = kb.sbuf("cbc", [128, 2, 512], BF16)
    u_tmp = kb.sbuf("u_tmp", [128, 512], BF16)
    stg = kb.sbuf("stg", [128, 2, 512], F32)
    ps_gu = kb.psum("ps_gu", [128, 2, 512], F32)
    ps_dn = kb.psum("ps_dn", [128, 2, 512], F32)
    ps_cb = kb.psum("ps_cb", [128, 1, 512], F32)

    s_es = kb.sem("s_es")
    kb.dma("gpsimd", esel_bf[:, :], esel_in[:, :], s_es)
    esel_tok = s_es.tok()
    for e in range(2):
        for o4 in range(4):
            kb.dma("gpsimd", w1_bf[e, o4 * 704:(o4 + 1) * 704, :],
                   w1_in[e, o4 * 704:(o4 + 1) * 704, :], s_cast)
        for o4 in range(4):
            kb.dma("gpsimd", w2_bf[e, o4 * 512:(o4 + 1) * 512, :],
                   w2_in[e, o4 * 512:(o4 + 1) * 512, :], s_cast)
    cast_tok = s_cast.tok()
    kb.wait("sync", cast_tok)
    for e in range(2):
        for i in range(11):
            kb.dma("sync", w2T[:, e, i, :],
                   w2_bf[e, :, i * 128:(i + 1) * 128], s_w2, transpose=True)
    w2_tok = s_w2.tok()

    bank = {}
    st_idx = 0
    for sp in range(n_spans):
        slot = sp % 2
        prev = bank.get(("xuse", slot))
        if prev is not None:
            kb.wait("sync", prev)
        for k in range(DC):
            kb.dma("sync", xs_sb[:, slot, k, :],
                   x_in[k * 128:(k + 1) * 128, sp // 2,
                        (sp % 2) * 512:(sp % 2) * 512 + 512], s_x[slot])
        x_tok = s_x[slot].tok()
        kb.wait("sync", bank.get(("cbuse",)))
        kb.dma("sync", cb_sb[:, :, :],
               comb_in[sp * 512:(sp + 1) * 512, :].rearrange(
                   "(u p) e -> p u e", p=128), s_cb)
        cb_tok = s_cb.tok()
        kb.wait("tensor", [cb_tok, C["tok"], identf_tok, esel_tok])
        for u in range(4):
            tok_t = kb.op("tensor", lambda E, u=u: E.transpose(
                ps_cb[0:8, 0, 0:128], cb_sb[:, u, :], identf[:, :]))
            kb.wait("vector", tok_t)
            ev = kb.op("vector", lambda E, u=u: E.tensor_copy(
                cbT_bf[:, :], ps_cb[0:8, 0, 0:128]))
            kb.wait("tensor", ev)
            for e in range(2):
                tok_s = kb.op("tensor", lambda E, e=e: E.matmul(
                    ps_cb[0:1, 0, 0:128], esel_bf[:, e:e + 1], cbT_bf[:, :],
                    start=True, stop=True))
                kb.wait("vector", tok_s)
                ev2 = kb.op("vector", lambda E, e=e, u=u: E.tensor_copy(
                    ce_row[0:1, e, u * 128:(u + 1) * 128], ps_cb[0:1, 0, 0:128]))
                kb.wait("tensor", ev2)
        row_tok = kb.eauto("vector").tok()
        kb.wait("tensor", row_tok)
        for e in range(2):
            tok_b = kb.op("tensor", lambda E, e=e: E.matmul(
                ps_cb[:, 0, :], ones_row[:, :], ce_row[0:1, e, :],
                start=True, stop=True))
            kb.wait("vector", tok_b)
            evb = kb.op("vector", lambda E, e=e: E.tensor_copy(
                cbc[:, e, :], ps_cb[:, 0, :]))
            kb.wait("tensor", evb)
        cbc_tok = kb.eauto("vector").tok()
        bank[("cbuse",)] = cbc_tok

        for e in range(2):
            for half in range(2):
                kb.wait("sync", bank.get(("w1use",)))
                for k in range(DC):
                    kb.dma("sync", w1T[:, k, :],
                           w1_bf[e, half * 1408:(half + 1) * 1408,
                                 k * 128:(k + 1) * 128], s_w1, transpose=True)
                w1_tok = s_w1.tok()
                if half == 1:
                    kb.wait("vector", kb.eauto("scalar").tok())
                kb.wait("tensor", [w1_tok, x_tok, cbc_tok])
                for j in range(11):
                    pp = j % 2
                    kb.wait("tensor", bank.get(("gu", pp)))
                    def mmgu(E, j=j, pp=pp, slot=slot):
                        for k in range(DC - 1):
                            E.matmul(ps_gu[:, pp, :],
                                     w1T[:, k, j * 128:(j + 1) * 128],
                                     xs_sb[:, slot, k, :],
                                     start=(k == 0), stop=False)
                        return E.matmul(ps_gu[:, pp, :],
                                        w1T[:, DC - 1, j * 128:(j + 1) * 128],
                                        xs_sb[:, slot, DC - 1, :],
                                        start=False, stop=True)
                    tok_mm = kb.op("tensor", mmgu)
                    if half == 0:
                        kb.wait("scalar", tok_mm)
                        ev = kb.op("scalar", lambda E, j=j, pp=pp: E.activation(
                            sg[:, j, :], ps_gu[:, pp, :], AF.Silu))
                        bank[("gu", pp)] = [ev]
                    else:
                        kb.wait("vector", tok_mm)
                        kb.op("vector", lambda E, j=j, pp=pp:
                              E.tensor_mul(u_tmp[:, :], sg[:, j, :],
                                           ps_gu[:, pp, :]))
                        ev = kb.op("vector", lambda E, j=j, e=e: E.tensor_mul(
                            act[:, e, j, :], u_tmp[:, :], cbc[:, e, :]))
                        bank[("gu", pp)] = [ev]
                w1use = kb.eauto("scalar" if half == 0 else "vector").tok()
                bank[("w1use",)] = [w1use]
        act_tok = kb.eauto("vector").tok()
        sg_tok = kb.eauto("scalar").tok()

        kb.wait("tensor", [act_tok, w2_tok])
        for hc in range(DC):
            pp = hc % 2
            kb.wait("tensor", bank.get(("dn", pp)))
            def mmdn(E, hc=hc, pp=pp):
                r = None
                for e in range(2):
                    for i in range(11):
                        r = E.matmul(ps_dn[:, pp, :],
                                     w2T[:, e, i, hc * 128:(hc + 1) * 128],
                                     act[:, e, i, :],
                                     start=(e == 0 and i == 0),
                                     stop=(e == 1 and i == 10))
                return r
            tok_dn = kb.op("tensor", mmdn)
            kb.wait("vector", tok_dn)
            kb.wait("vector", s_st.tok())
            ev = kb.op("vector", lambda E, hc=hc, pp=pp: E.tensor_copy(
                stg[:, pp, :], ps_dn[:, pp, :]))
            bank[("dn", pp)] = [ev]
            kb.wait("sync", ev)
            kb.dma("sync", out_p[hc * 128:(hc + 1) * 128, sp // 2,
                                 (sp % 2) * 512:(sp % 2) * 512 + 512],
                   stg[:, pp, :], s_st)
        bank[("xuse", slot)] = kb.eauto("tensor").tok()
    kb.wait("sync", s_st.tok())
    return kb



def build_l1ffn(n_spans=8, kb=None, io=None):
    own = kb is None
    if own:
        kb = LB("l1ffn")
        io = {}
    x_in = _v3(io["xT_bf_full"] if "xT_bf_full" in io else
               kb.din("xT_bf_full", [D, S], BF16))
    wg_in = io.get("wg_sl") if "wg_sl" in io else kb.din("wg_sl", [D, D])
    wu_in = io.get("wu_sl") if "wu_sl" in io else kb.din("wu_sl", [D, D])
    wd_in = io.get("wd_sl") if "wd_sl" in io else kb.din("wd_sl", [D, D])
    out_p = _v3(io["ffnP"] if "ffnP" in io else kb.dout("ffnP", [D, S]))

    s_cast = kb.sem("s_cast")
    s_w = kb.sem("s_w")
    s_x = [kb.sem("s_x0"), kb.sem("s_x1")]
    s_st = kb.sem("s_st")
    C = make_consts(kb)

    wg_bf = kb.dram("wg_bf", [D, D], BF16)
    wu_bf = kb.dram("wu_bf", [D, D], BF16)
    wd_bf = kb.dram("wd_bf", [D, D], BF16)

    xs_sb = kb.sbuf("xs_sb", [128, 2, DC, 512], BF16)
    wT = kb.sbuf("wT", [128, DC, D], BF16)
    sg = kb.sbuf("sg", [128, DC, 512], BF16)
    act = kb.sbuf("act", [128, DC, 512], BF16)
    u_tmp = kb.sbuf("u_tmp", [128, 512], BF16)
    stg = kb.sbuf("stg", [128, 2, 512], F32)
    ps_mm = kb.psum("ps_mm", [128, 2, 512], F32)

    for w_in, w_bf in ((wg_in, wg_bf), (wu_in, wu_bf), (wd_in, wd_bf)):
        for o4 in range(4):
            kb.dma("gpsimd", w_bf[o4 * 512:(o4 + 1) * 512, :],
                   w_in[o4 * 512:(o4 + 1) * 512, :], s_cast)
    cast_tok = s_cast.tok()
    kb.wait("sync", cast_tok)

    bank = {}
    for sp in range(n_spans):
        slot = sp % 2
        prev = bank.get(("xuse", slot))
        if prev is not None:
            kb.wait("sync", prev)
        for k in range(DC):
            kb.dma("sync", xs_sb[:, slot, k, :],
                   x_in[k * 128:(k + 1) * 128, sp // 2,
                        (sp % 2) * 512:(sp % 2) * 512 + 512], s_x[slot])
        x_tok = s_x[slot].tok()

        for phase in range(3):
            w_bf = (wg_bf, wu_bf, wd_bf)[phase]
            kb.wait("sync", bank.get(("wuse",)))
            for k in range(DC):
                kb.dma("sync", wT[:, k, :], w_bf[:, k * 128:(k + 1) * 128],
                       s_w, transpose=True)
            w_tok = s_w.tok()
            kb.wait("tensor", [w_tok, x_tok])
            if phase == 2:
                kb.wait("tensor", kb.eauto("vector").tok())
            for j in range(DC):
                pp = j % 2
                kb.wait("tensor", bank.get(("mm", pp)))
                if phase < 2:
                    def mmgu(E, j=j, pp=pp, slot=slot):
                        for k in range(DC - 1):
                            E.matmul(ps_mm[:, pp, :],
                                     wT[:, k, j * 128:(j + 1) * 128],
                                     xs_sb[:, slot, k, :],
                                     start=(k == 0), stop=False)
                        return E.matmul(ps_mm[:, pp, :],
                                        wT[:, DC - 1, j * 128:(j + 1) * 128],
                                        xs_sb[:, slot, DC - 1, :],
                                        start=False, stop=True)
                    tok_mm = kb.op("tensor", mmgu)
                    if phase == 0:
                        kb.wait("scalar", tok_mm)
                        ev = kb.op("scalar", lambda E, j=j, pp=pp: E.activation(
                            sg[:, j, :], ps_mm[:, pp, :], AF.Silu))
                    else:
                        kb.wait("vector", tok_mm)
                        ev = kb.op("vector", lambda E, j=j, pp=pp: E.tensor_mul(
                            act[:, j, :], sg[:, j, :], ps_mm[:, pp, :]))
                    bank[("mm", pp)] = [ev]
                else:
                    def mmdn(E, j=j, pp=pp):
                        for k in range(DC - 1):
                            E.matmul(ps_mm[:, pp, :],
                                     wT[:, k, j * 128:(j + 1) * 128],
                                     act[:, k, :], start=(k == 0), stop=False)
                        return E.matmul(ps_mm[:, pp, :],
                                        wT[:, DC - 1, j * 128:(j + 1) * 128],
                                        act[:, DC - 1, :],
                                        start=False, stop=True)
                    tok_mm = kb.op("tensor", mmdn)
                    kb.wait("vector", tok_mm)
                    kb.wait("vector", s_st.tok())
                    ev = kb.op("vector", lambda E, j=j, pp=pp: E.tensor_copy(
                        stg[:, pp, :], ps_mm[:, pp, :]))
                    bank[("mm", pp)] = [ev]
                    kb.wait("sync", ev)
                    kb.dma("sync", out_p[j * 128:(j + 1) * 128, sp // 2,
                                         (sp % 2) * 512:(sp % 2) * 512 + 512],
                           stg[:, pp, :], s_st)
            if phase == 0:
                kb.wait("vector", kb.eauto("scalar").tok())
            bank[("wuse",)] = kb.eauto("tensor").tok()
        bank[("xuse", slot)] = kb.eauto("tensor").tok()
    kb.wait("sync", s_st.tok())
    return kb



def build_l1attn(n_heads=4, n_chunks=32, n_spans=8, kb=None, io=None):
    own = kb is None
    if own:
        kb = LB("l1attn")
        io = {}
    x_in = _v3(io["xT_bf_full"] if "xT_bf_full" in io else
               kb.din("xT_bf_full", [D, S], BF16))
    wq_in = io.get("wq_sl") if "wq_sl" in io else kb.din("wq_sl", [D, 512])
    wk_in = io.get("wk_sl") if "wk_sl" in io else kb.din("wk_sl", [D, 512])
    wv_in = io.get("wv_sl") if "wv_sl" in io else kb.din("wv_sl", [D, 512])
    wg_in = io.get("wg_sl") if "wg_sl" in io else kb.din("wg_sl", [D, 512])
    wa_in = io.get("wa_sl") if "wa_sl" in io else kb.din("wa_sl", [D, 4])
    wb_in = io.get("wb_sl") if "wb_sl" in io else kb.din("wb_sl", [D, 4])
    ba_in = io.get("ba_sl") if "ba_sl" in io else kb.din("ba_sl", [4])
    bb_in = io.get("bb_sl") if "bb_sl" in io else kb.din("bb_sl", [4])
    qcw_in = io.get("qcw_sl") if "qcw_sl" in io else kb.din("qcw_sl", [512, 4])
    qcb_in = io.get("qcb_sl") if "qcb_sl" in io else kb.din("qcb_sl", [512])
    kcw_in = io.get("kcw_sl") if "kcw_sl" in io else kb.din("kcw_sl", [512, 4])
    kcb_in = io.get("kcb_sl") if "kcb_sl" in io else kb.din("kcb_sl", [512])
    vcw_in = io.get("vcw_sl") if "vcw_sl" in io else kb.din("vcw_sl", [512, 4])
    vcb_in = io.get("vcb_sl") if "vcb_sl" in io else kb.din("vcb_sl", [512])
    lng_in = io.get("ln_g") if "ln_g" in io else kb.din("ln_g", [128])
    lnb_in = io.get("ln_b") if "ln_b" in io else kb.din("ln_b", [128])
    og_out = io.get("ogT") if "ogT" in io else kb.dout("ogT", [512, S], BF16)

    s_ldw = kb.sem("s_ldw")
    s_wh = kb.sem("s_wh")
    s_x = [kb.sem("s_x0"), kb.sem("s_x1")]
    s_st = kb.sem("s_st")
    C = make_consts(kb)
    ident, ones128 = C["ident"], C["ones128"]

    utonesf = kb.sbuf("utonesf", [128, 128], F32)
    maskA = kb.sbuf("maskA", [128, 128], F32)
    maskB = kb.sbuf("maskB", [128, 128], F32)
    zerof = kb.sbuf("zerof", [128, 128], F32)
    onef = kb.sbuf("onef", [128, 1], F32)
    identf = kb.sbuf("identfL", [128, 128], F32)
    gcst = kb.sem("s_gcst")
    for tile, fill, op, val in (
        (utonesf, 0.0, ALU.is_gt, 1.0),
        (maskA, 0.0, ALU.is_gt, NEG),
        (maskB, NEG, ALU.is_gt, 0.0),
    ):
        t0 = kb.op("gpsimd", lambda E, t=tile, f=fill: E.memset(t[:, :], f), gcst)
        kb.wait("gpsimd", t0)
        kb.op("gpsimd", lambda E, t=tile, o=op, v=val: E.affine_select(
            out=t[:, :], in_=t[:, :], compare_op=o, fill=v,
            base=0, pattern=[[-1, 128]], channel_multiplier=1), gcst)
    kb.op("gpsimd", lambda E: E.memset(zerof[:, :], 0.0), gcst)
    kb.op("gpsimd", lambda E: E.memset(onef[:, :], 1.0), gcst)
    t0i = kb.op("gpsimd", lambda E: E.memset(identf[:, :], 0.0), gcst)
    kb.wait("gpsimd", t0i)
    kb.op("gpsimd", lambda E: E.affine_select(
        out=identf[:, :], in_=identf[:, :], compare_op=ALU.not_equal,
        fill=1.0, base=0, pattern=[[-1, 128]], channel_multiplier=1), gcst)
    gconst_tok = kb.eauto("gpsimd").tok()

    xs = kb.sbuf("xs", [128, 2, DC, 512], BF16)
    wq_t = kb.sbuf("wq_t", [128, DC, 128], BF16)
    wk_t = kb.sbuf("wk_t", [128, DC, 128], BF16)
    wv_t = kb.sbuf("wv_t", [128, DC, 128], BF16)
    wg_t = kb.sbuf("wg_t", [128, DC, 128], BF16)
    wa_t = kb.sbuf("wa_t", [128, DC, 4], BF16)
    wb_t = kb.sbuf("wb_t", [128, DC, 4], BF16)
    bacol = kb.sbuf("bacol", [4, 1], F32)
    bbcol = kb.sbuf("bbcol", [4, 1], F32)
    qcw = kb.sbuf("qcw", [128, 4, 4], F32)
    qcb = kb.sbuf("qcb", [128, 4], F32)
    kcw = kb.sbuf("kcw", [128, 4, 4], F32)
    kcb = kb.sbuf("kcb", [128, 4], F32)
    vcw = kb.sbuf("vcw", [128, 4, 4], F32)
    vcb = kb.sbuf("vcb", [128, 4], F32)
    lngc = kb.sbuf("lngc", [128, 1], F32)
    lnbc = kb.sbuf("lnbc", [128, 1], F32)
    zq = kb.sbuf("zq", [4, 512], F32)
    zb = kb.sbuf("zb", [4, 512], F32)
    zs1 = kb.sbuf("zs1", [4, 512], F32)
    zs2 = kb.sbuf("zs2", [4, 512], F32)
    la_sp = kb.sbuf("la_sp", [4, 512], F32)
    b_sp = kb.sbuf("b_sp", [4, 512], F32)
    labtm = kb.sbuf("labtm", [128, 32, 8], F32)
    gtm = kb.sbuf("gtm", [128, 32, 4], F32)
    rawq = kb.sbuf("rawq", [128, S + 3], BF16)
    rawk = kb.sbuf("rawk", [128, S + 3], BF16)
    rawv = kb.sbuf("rawv", [128, S + 3], BF16)
    cq = kb.sbuf("cq", [128, S], BF16)
    ck = kb.sbuf("ck", [128, S], BF16)
    cv_ = kb.sbuf("cv_", [128, S], BF16)
    gateh = kb.sbuf("gateh", [128, S], BF16)
    ogsb = kb.sbuf("ogsb", [128, 2, S], BF16)
    cvs = kb.sbuf("cvs", [128, 2, 512], F32)
    kss = kb.sbuf("kss", [128, 512], BF16)
    gbc_s = kb.sbuf("gbc_s", [128, 128], F32)
    escr = kb.sbuf("escr", [128, 128], F32)
    dn_bf = kb.sbuf("dn_bf", [128, 128], BF16)
    nt_bf = kb.sbuf("nt_bf", [128, 128], BF16)
    t1n = kb.sbuf("t1n", [128, 2, 128], BF16)
    t2n = kb.sbuf("t2n", [128, 2, 128], BF16)
    xsol = kb.sbuf("xsol", [128, 2, 128], BF16)
    dft_bf = kb.sbuf("dft_bf", [128, 128], BF16)
    sht_bf = kb.sbuf("sht_bf", [128, 128], BF16)
    vtm_bf = kb.sbuf("vtm_bf", [128, 128], BF16)
    ktm_bf = kb.sbuf("ktm_bf", [128, 128], BF16)
    qg_bf = kb.sbuf("qg_bf", [128, 128], BF16)
    gambc = kb.sbuf("gambc", [128, 128], BF16)
    uyl_bf = kb.sbuf("uyl_bf", [128, 128], BF16)
    r_bf = kb.sbuf("r_bf", [128, 128], BF16)
    scrR = kb.sbuf("scrR", [128, 128], F32)
    colv = kb.sbuf("colv", [128, 8], F32)
    labc = kb.sbuf("labc", [128, 128], F32)
    st_f = kb.sbuf("st_f", [128, 4, 2, 128], F32)
    st_bf = kb.sbuf("st_bf", [128, 128], BF16)
    o_s = kb.sbuf("o_s", [128, 128], F32)
    sqo = kb.sbuf("sqo", [128, 128], BF16)
    o_bf = kb.sbuf("o_bf", [128, 128], BF16)
    muo = kb.sbuf("muo", [128, 128], F32)
    mso = kb.sbuf("mso", [128, 128], F32)
    varo = kb.sbuf("varo", [128, 128], F32)
    rstdo = kb.sbuf("rstdo", [128, 128], F32)
    oscr = kb.sbuf("oscr", [128, 128], F32)
    ps_pj = kb.psum("ps_pj", [128, 2, 512], F32)
    ps_c = kb.psum("ps_c", [128, 1, 512], F32)
    ps_d = kb.psum("ps_d", [128, 1, 512], F32)
    ps_e = kb.psum("ps_e", [128, 1, 512], F32)
    ps_tp = kb.psum("ps_tpL", [128, 2, 1024], BF16)
    ps_x = kb.psum("ps_x", [128, 1, 512], F32)

    load_wT(kb, wa_t, wa_in, D, s_ldw)
    load_wT(kb, wb_t, wb_in, D, s_ldw)
    kb.dma("gpsimd", bacol[:, :], ba_in[:][:, None], s_ldw)
    kb.dma("gpsimd", bbcol[:, :], bb_in[:][:, None], s_ldw)
    for cw, cb, cwi, cbi in ((qcw, qcb, qcw_in, qcb_in),
                             (kcw, kcb, kcw_in, kcb_in),
                             (vcw, vcb, vcw_in, vcb_in)):
        for h in range(4):
            kb.dma("gpsimd", cw[:, h, :], cwi[h * 128:(h + 1) * 128, :], s_ldw)
            kb.dma("gpsimd", cb[:, h:h + 1], cbi[h * 128:(h + 1) * 128][:, None],
                   s_ldw)
    kb.dma("gpsimd", lngc[:, :], lng_in[:][:, None], s_ldw)
    kb.dma("gpsimd", lnbc[:, :], lnb_in[:][:, None], s_ldw)
    ldw_tok = s_ldw.tok()

    bank = {}

    for sp in range(n_spans):
        slot = sp % 2
        prev = bank.get(("xuse", slot))
        if prev is not None:
            kb.wait("sync", prev)
        for k in range(DC):
            kb.dma("sync", xs[:, slot, k, :],
                   x_in[k * 128:(k + 1) * 128, sp // 2,
                        (sp % 2) * 512:(sp % 2) * 512 + 512], s_x[slot])
        x_tok = s_x[slot].tok()
        kb.wait("tensor", [x_tok, ldw_tok, gconst_tok, C["tok"]])
        for which, wt, zt in ((0, wa_t, zq), (1, wb_t, zb)):
            pp = which
            kb.wait("tensor", bank.get(("pj", pp)))
            def mmab(E, wt=wt, pp=pp, slot=slot):
                for k in range(DC - 1):
                    E.matmul(ps_pj[0:4, pp, :], wt[:, k, :], xs[:, slot, k, :],
                             start=(k == 0), stop=False)
                return E.matmul(ps_pj[0:4, pp, :], wt[:, DC - 1, :],
                                xs[:, slot, DC - 1, :], start=False, stop=True)
            tok_mm = kb.op("tensor", mmab)
            kb.wait("vector", tok_mm)
            ev = kb.op("vector", lambda E, zt=zt, pp=pp, bc=(bacol if which == 0
                                                            else bbcol):
                       E.tensor_single_scalar(zt[:, :], ps_pj[0:4, pp, :],
                                              bc[:, 0:1], ALU.add))
            bank[("pj", pp)] = [ev]
        kb.wait("scalar", kb.eauto("vector").tok())
        kb.op("scalar", lambda E: E.activation(zs1[:, :], zq[:, :], AF.Abs))
        kb.op("scalar", lambda E: E.activation(zs2[:, :], zs1[:, :], AF.Exp,
                                               scale=-1.0))
        tok_ln = kb.op("scalar", lambda E: E.activation(
            zs1[:, :], zs2[:, :], AF.Ln, bias=onef[0:4, 0:1]))
        kb.wait("vector", tok_ln)
        kb.op("vector", lambda E: E.tensor_scalar_min(zs2[:, :], zq[:, :], 0.0))
        tok_la = kb.op("vector", lambda E: E.tensor_sub(la_sp[:, :], zs2[:, :],
                                                        zs1[:, :]))
        tok_b = kb.op("scalar", lambda E: E.activation(
            b_sp[:, :], zb[:, :], AF.Sigmoid))
        kb.wait("tensor", [tok_la, tok_b])
        for u in range(4):
            ch = sp * 4 + u
            tok_t = kb.op("tensor", lambda E, u=u: E.transpose(
                ps_c[0:128, 0, 0:4], la_sp[0:4, u * 128:(u + 1) * 128],
                identf[0:4, 0:4]))
            kb.wait("vector", tok_t)
            ev = kb.op("vector", lambda E, ch=ch: E.tensor_copy(
                labtm[:, ch, 0:4], ps_c[0:128, 0, 0:4]))
            kb.wait("tensor", ev)
            tok_t = kb.op("tensor", lambda E, u=u: E.transpose(
                ps_c[0:128, 0, 0:4], b_sp[0:4, u * 128:(u + 1) * 128],
                identf[0:4, 0:4]))
            kb.wait("vector", tok_t)
            ev = kb.op("vector", lambda E, ch=ch: E.tensor_copy(
                labtm[:, ch, 4:8], ps_c[0:128, 0, 0:4]))
            kb.wait("tensor", ev)
            tok_g = kb.op("tensor", lambda E, ch=ch: E.matmul(
                ps_c[:, 0, 0:4], utonesf[:, :], labtm[:, ch, 0:4],
                start=True, stop=True))
            kb.wait("vector", tok_g)
            ev = kb.op("vector", lambda E, ch=ch: E.tensor_copy(
                gtm[:, ch, :], ps_c[:, 0, 0:4]))
            kb.wait("tensor", ev)
        bank[("xuse", slot)] = kb.eauto("tensor").tok()
    labtm_tok = kb.eauto("vector").tok()

    for h in range(n_heads):
        kb.wait("scalar", kb.eauto("vector").tok())
        kb.wait("gpsimd", bank.get(("whuse",)))
        for wt, wi in ((wq_t, wq_in), (wk_t, wk_in), (wv_t, wv_in), (wg_t, wg_in)):
            load_wT(kb, wt, wi, D, s_wh, cols=(h * 128, (h + 1) * 128))
        wh_tok = s_wh.tok()
        if h == 0:
            for raw in (rawq, rawk, rawv):
                kb.op("gpsimd", lambda E, raw=raw: E.memset(raw[:, 0:3], 0.0))
            pad_tok = kb.eauto("gpsimd").tok()
        for sp in range(n_spans):
            slot = sp % 2
            prev = bank.get(("xuse", slot))
            if prev is not None:
                kb.wait("sync", prev)
            for k in range(DC):
                kb.dma("sync", xs[:, slot, k, :],
                       x_in[k * 128:(k + 1) * 128, sp // 2,
                            (sp % 2) * 512:(sp % 2) * 512 + 512],
                       s_x[slot])
            x_tok = s_x[slot].tok()
            kb.wait("tensor", [x_tok, wh_tok, pad_tok])
            for wi, (wt, dst) in enumerate(((wq_t, rawq), (wk_t, rawk),
                                            (wv_t, rawv), (wg_t, None))):
                pp = wi % 2
                kb.wait("tensor", bank.get(("pj", pp)))
                def mmp(E, wt=wt, pp=pp, slot=slot):
                    for k in range(DC - 1):
                        E.matmul(ps_pj[:, pp, :], wt[:, k, :], xs[:, slot, k, :],
                                 start=(k == 0), stop=False)
                    return E.matmul(ps_pj[:, pp, :], wt[:, DC - 1, :],
                                    xs[:, slot, DC - 1, :],
                                    start=False, stop=True)
                tok_mm = kb.op("tensor", mmp)
                if dst is None:
                    kb.wait("scalar", tok_mm)
                    ev = kb.op("scalar", lambda E, sp=sp, pp=pp: E.activation(
                        gateh[:, sp * 512:(sp + 1) * 512], ps_pj[:, pp, :],
                        AF.Sigmoid))
                else:
                    kb.wait("vector", tok_mm)
                    ev = kb.op("vector", lambda E, dst=dst, sp=sp, pp=pp:
                               E.tensor_copy(dst[:, 3 + sp * 512:3 + (sp + 1) * 512],
                                             ps_pj[:, pp, :]))
                bank[("pj", pp)] = [ev]
            raw_tok = kb.eauto("vector").tok()
            for raw, cw, cb, cout in ((rawq, qcw, qcb, cq), (rawk, kcw, kcb, ck),
                                      (rawv, vcw, vcb, cv_)):
                kb.wait("vector", bank.get(("cvsilu",)))
                kb.op("vector", lambda E, raw=raw, cw=cw, sp=sp, h=h:
                      E.tensor_single_scalar(
                          cvs[:, 0, :], raw[:, sp * 512:sp * 512 + 512],
                          cw[:, h, 0:1], ALU.mult))
                for tau in range(1, 4):
                    src_pp, dst_pp = (tau - 1) % 2, tau % 2
                    kb.op("vector", lambda E, raw=raw, cw=cw, sp=sp, h=h,
                          tau=tau, a=src_pp, b2=dst_pp: E.scalar_tensor_tensor(
                              cvs[:, b2, :],
                              raw[:, sp * 512 + tau:sp * 512 + tau + 512],
                              cw[:, h, tau:tau + 1], cvs[:, a, :],
                              ALU.mult, ALU.add))
                tok_cv = kb.eauto("vector").tok()
                kb.wait("scalar", tok_cv)
                if cout is ck:
                    tok_sl = kb.op("scalar", lambda E, cb=cb, h=h: E.activation(
                        kss[:, :], cvs[:, 1, :], AF.Silu, bias=cb[:, h:h + 1]))
                    kb.wait("vector", tok_sl)
                    kb.op("vector", lambda E, sp=sp: E.tensor_scalar_mul(
                        ck[:, sp * 512:(sp + 1) * 512], kss[:, :],
                        1.0 / math.sqrt(128.0)))
                    bank[("cvsilu",)] = [tok_sl]
                else:
                    tok_sl = kb.op("scalar", lambda E, cout=cout, cb=cb, sp=sp, h=h:
                          E.activation(cout[:, sp * 512:(sp + 1) * 512],
                                       cvs[:, 1, :], AF.Silu,
                                       bias=cb[:, h:h + 1]))
                    bank[("cvsilu",)] = [tok_sl]
            bank[("xuse", slot)] = kb.eauto("tensor").tok()
        conv_tok = [kb.eauto("vector").tok(), kb.eauto("scalar").tok()]

        kb.wait("tensor", conv_tok + [labtm_tok])
        stpp = 0
        kb.op("vector", lambda E, h=h: E.memset(st_f[:, h, 0, :], 0.0))
        tok_st0 = kb.op("vector", lambda E: E.memset(st_bf[:, :], 0.0))
        st_tok = tok_st0
        for c in range(n_chunks):
            t0 = c * 128
            kb.wait("vector", bank.get(("colv",)))
            kb.op("vector", lambda E, c=c, h=h: E.tensor_copy(
                colv[:, 0:1], labtm[:, c, h:h + 1]))
            kb.op("vector", lambda E, c=c, h=h: E.tensor_copy(
                colv[:, 1:2], gtm[:, c, h:h + 1]))
            kb.op("vector", lambda E: E.tensor_sub(
                colv[:, 2:3], colv[:, 1:2], colv[:, 0:1]))
            tok_col = kb.eauto("vector").tok()
            kb.wait("scalar", tok_col)
            tok_gp = kb.op("scalar", lambda E: E.activation(
                colv[:, 3:4], colv[:, 2:3], AF.Exp))
            kb.wait("vector", tok_gp)
            tok_bgp = kb.op("vector", lambda E, c=c, h=h: E.tensor_mul(
                colv[:, 4:5], labtm[:, c, 4 + h:5 + h], colv[:, 3:4]))
            kb.wait("vector", bank.get(("labc",)))
            tok_labc = kb.op("vector", lambda E: E.tensor_single_scalar(
                labc[:, :], zerof[:, :], colv[:, 0:1], ALU.add))
            kb.wait("tensor", [tok_labc, bank.get(("ps_c",))])
            tok_gbc = kb.op("tensor", lambda E: E.matmul(
                ps_c[:, 0, 0:128], labc[:, :], utonesf[:, :],
                start=True, stop=True))
            bank[("labc",)] = [tok_gbc]
            kb.wait("vector", tok_gbc)
            tok_gbcs = kb.op("vector", lambda E: E.tensor_copy(
                gbc_s[:, :], ps_c[:, 0, 0:128]))
            bank[("ps_c",)] = [tok_gbcs]
            kb.op("vector", lambda E: E.tensor_sub(
                colv[:, 7:8], gbc_s[:, 127:128], colv[:, 1:2]))
            tok_l1 = kb.eauto("vector").tok()
            kb.wait("scalar", tok_l1)
            kb.op("scalar", lambda E: E.activation(colv[:, 5:6], colv[:, 7:8],
                                                   AF.Exp))
            tok_gamC = kb.op("scalar", lambda E: E.activation(
                colv[:, 6:7], gbc_s[:, 127:128], AF.Exp))
            bank[("colv",)] = [tok_gamC]
            kb.wait("scalar", bank.get(("gambc",)))
            tok_gam = kb.op("scalar", lambda E: E.activation(
                gambc[:, :], gbc_s[:, :], AF.Exp))
            kb.wait("tensor", bank.get(("pj", 0)))
            tok_gram = kb.op("tensor", lambda E, t0=t0: E.matmul(
                ps_pj[:, 0, 0:128], ck[:, t0:t0 + 128], ck[:, t0:t0 + 128],
                start=True, stop=True))
            kb.wait("tensor", bank.get(("pj", 1)))
            tok_kq = kb.op("tensor", lambda E, t0=t0: E.matmul(
                ps_pj[:, 1, 0:128], ck[:, t0:t0 + 128], cq[:, t0:t0 + 128],
                start=True, stop=True))
            kb.wait("vector", [tok_gbcs])
            kb.op("vector", lambda E: E.tensor_scalar(
                escr[:, :], gbc_s[:, :], -1.0, colv[:, 2:3],
                ALU.mult, ALU.add))
            tok_escr = kb.op("vector", lambda E: E.tensor_add(
                escr[:, :], escr[:, :], maskA[:, :]))
            kb.wait("scalar", tok_escr)
            tok_dn = kb.op("scalar", lambda E: E.activation(
                dn_bf[:, :], escr[:, :], AF.Exp))
            kb.wait("vector", [tok_dn, tok_gram, tok_bgp])
            tok_nt = kb.op("vector", lambda E: E.scalar_tensor_tensor(
                nt_bf[:, :], dn_bf[:, :], labtm[:, c, 4 + h:5 + h],
                ps_pj[:, 0, 0:128], ALU.mult, ALU.mult))
            bank[("pj", 0)] = [tok_nt]
            kb.wait("tensor", [tok_nt, bank.get(("tp", 0))])
            tok_tN = kb.op("tensor", lambda E: E.transpose(
                ps_tp[:, 0, 0:128], nt_bf[:, :], ident[:, :]))
            kb.wait("scalar", tok_tN)
            tok_t2 = kb.op("scalar", lambda E: E.activation(
                t2n[:, 0, :], ps_tp[:, 0, 0:128], AF.Identity, scale=-1.0))
            bank[("tp", 0)] = [tok_t2]
            kb.wait("vector", tok_nt)
            tok_t1 = kb.op("vector", lambda E: E.tensor_scalar_mul(
                t1n[:, 0, :], nt_bf[:, :], -1.0))
            kb.wait("vector", tok_dn)
            kb.op("vector", lambda E: E.tensor_scalar(
                escr[:, :], gbc_s[:, :], 1.0, colv[:, 1:2],
                ALU.mult, ALU.subtract))
            tok_es2 = kb.op("vector", lambda E: E.tensor_add(
                escr[:, :], escr[:, :], maskB[:, :]))
            kb.wait("scalar", tok_es2)
            tok_dft = kb.op("scalar", lambda E: E.activation(
                dft_bf[:, :], escr[:, :], AF.Exp))
            kb.wait("vector", [tok_dft, tok_kq])
            tok_sht = kb.op("vector", lambda E: E.tensor_mul(
                sht_bf[:, :], dft_bf[:, :], ps_pj[:, 1, 0:128]))
            bank[("pj", 1)] = [tok_sht]
            kb.wait("tensor", bank.get(("tp", 1)))
            tok_tV = kb.op("tensor", lambda E, t0=t0: E.transpose(
                ps_tp[:, 1, 0:128], cv_[:, t0:t0 + 128], ident[:, :]))
            kb.wait("vector", tok_tV)
            tok_vtm = kb.op("vector", lambda E: E.tensor_copy(
                vtm_bf[:, :], ps_tp[:, 1, 0:128]))
            bank[("tp", 1)] = [tok_vtm]
            kb.wait("tensor", bank.get(("tp", 0)))
            tok_tK = kb.op("tensor", lambda E, t0=t0: E.transpose(
                ps_tp[:, 0, 0:128], ck[:, t0:t0 + 128], ident[:, :]))
            kb.wait("vector", tok_tK)
            tok_ktm = kb.op("vector", lambda E: E.tensor_copy(
                ktm_bf[:, :], ps_tp[:, 0, 0:128]))
            bank[("tp", 0)] = [tok_ktm]
            kb.wait("tensor", [st_tok, bank.get(("ps_d",))])
            tok_kst = kb.op("tensor", lambda E, t0=t0: E.matmul(
                ps_d[:, 0, 0:128], ck[:, t0:t0 + 128], st_bf[:, :],
                start=True, stop=True))
            kb.wait("vector", tok_kst)
            kb.op("vector", lambda E: E.tensor_scalar_mul(
                scrR[:, :], ps_d[:, 0, 0:128], colv[:, 4:5]))
            bank[("ps_d",)] = [kb.eauto("vector").tok()]
            tok_rhs = kb.op("vector", lambda E, c=c, h=h: E.scalar_tensor_tensor(
                r_bf[:, :], vtm_bf[:, :], labtm[:, c, 4 + h:5 + h],
                scrR[:, :], ALU.mult, ALU.subtract))
            kb.op("vector", lambda E: E.tensor_copy(xsol[:, 0, :], r_bf[:, :]))
            x_cur, x_nxt = 0, 1
            p_cur, p_nxt = 0, 1
            kb.wait("tensor", kb.eauto("vector").tok())
            kb.wait("tensor", [tok_t2])
            for lvl in range(7):
                tok_xu = kb.op("tensor", lambda E, p_cur=p_cur, x_cur=x_cur:
                               E.matmul(ps_x[:, 0, 0:128], t2n[:, p_cur, :],
                                        xsol[:, x_cur, :],
                                        start=True, stop=True))
                kb.wait("vector", tok_xu)
                tok_xa = kb.op("vector", lambda E, x_cur=x_cur, x_nxt=x_nxt:
                               E.tensor_add(xsol[:, x_nxt, :],
                                            xsol[:, x_cur, :],
                                            ps_x[:, 0, 0:128]))
                if lvl < 6:
                    kb.wait("tensor", tok_xa)
                    tok_s1 = kb.op("tensor", lambda E, p_cur=p_cur: E.matmul(
                        ps_x[:, 0, 0:128], t1n[:, p_cur, :], t2n[:, p_cur, :],
                        start=True, stop=True))
                    kb.wait("scalar", tok_s1)
                    tok_e1 = kb.op("scalar", lambda E, p_nxt=p_nxt: E.copy(
                        t2n[:, p_nxt, :], ps_x[:, 0, 0:128]))
                    kb.wait("tensor", tok_e1)
                    tok_s2 = kb.op("tensor", lambda E, p_cur=p_cur: E.matmul(
                        ps_x[:, 0, 0:128], t2n[:, p_cur, :], t1n[:, p_cur, :],
                        start=True, stop=True))
                    kb.wait("scalar", tok_s2)
                    tok_e2 = kb.op("scalar", lambda E, p_nxt=p_nxt: E.copy(
                        t1n[:, p_nxt, :], ps_x[:, 0, 0:128]))
                    kb.wait("tensor", tok_e2)
                    p_cur, p_nxt = p_nxt, p_cur
                x_cur, x_nxt = x_nxt, x_cur
            uy_tok = kb.eauto("vector").tok()
            kb.wait("vector", [uy_tok, tok_gamC])
            tok_uyl = kb.op("vector", lambda E, x_cur=x_cur: E.tensor_scalar_mul(
                uyl_bf[:, :], xsol[:, x_cur, :], colv[:, 5:6]))
            kb.wait("vector", [tok_gam])
            tok_qg = kb.op("vector", lambda E, t0=t0: E.tensor_mul(
                qg_bf[:, :], cq[:, t0:t0 + 128], gambc[:, :]))
            bank[("gambc",)] = [tok_qg]
            kb.wait("tensor", [tok_qg, tok_sht, bank.get(("ps_c",)),
                               kb.eauto("vector").tok()])
            def mmo(E, x_cur=x_cur):
                E.matmul(ps_c[:, 0, 0:128], st_bf[:, :], qg_bf[:, :],
                         start=True, stop=False)
                return E.matmul(ps_c[:, 0, 0:128], xsol[:, x_cur, :],
                                sht_bf[:, :], start=False, stop=True)
            tok_o = kb.op("tensor", mmo)
            kb.wait("scalar", tok_o)
            tok_oev = kb.op("scalar", lambda E: E.copy(o_s[:, :],
                                                       ps_c[:, 0, 0:128]))
            bank[("ps_c",)] = [tok_oev]
            kb.wait("tensor", [tok_uyl, tok_ktm, bank.get(("ps_e",))])
            tok_smm = kb.op("tensor", lambda E: E.matmul(
                ps_e[:, 0, 0:128], ktm_bf[:, :], uyl_bf[:, :],
                start=True, stop=True))
            kb.wait("vector", [tok_smm, tok_gamC])
            tok_stn = kb.op("vector", lambda E, h=h, stpp=stpp: E.scalar_tensor_tensor(
                st_f[:, h, 1 - stpp, :], st_f[:, h, stpp, :], colv[:, 6:7],
                ps_e[:, 0, 0:128], ALU.mult, ALU.add))
            bank[("ps_e",)] = [tok_stn]
            kb.wait("vector", tok_o)
            tok_stbf = kb.op("vector", lambda E, h=h, stpp=stpp: E.tensor_copy(
                st_bf[:, :], st_f[:, h, 1 - stpp, :]))
            stpp = 1 - stpp
            st_tok = tok_stbf
            kb.wait("scalar", tok_oev)
            tok_sqo = kb.op("scalar", lambda E: E.activation(
                sqo[:, :], o_s[:, :], AF.Square))
            kb.wait("vector", tok_oev)
            tok_obf = kb.op("vector", lambda E: E.tensor_copy(o_bf[:, :],
                                                              o_s[:, :]))
            kb.wait("tensor", [tok_sqo, tok_obf, bank.get(("ps_d",)),
                               bank.get(("ps_e",))])
            tok_mA = kb.op("tensor", lambda E: E.matmul(
                ps_d[:, 0, 0:128], ones128[:, :], o_bf[:, :],
                start=True, stop=True))
            tok_mB = kb.op("tensor", lambda E: E.matmul(
                ps_e[:, 0, 128:256], ones128[:, :], sqo[:, :],
                start=True, stop=True))
            kb.wait("vector", tok_mA)
            kb.op("vector", lambda E: E.tensor_scalar_mul(
                muo[:, :], ps_d[:, 0, 0:128], 1.0 / 128.0))
            bank[("ps_d",)] = [kb.eauto("vector").tok()]
            kb.wait("vector", tok_mB)
            kb.op("vector", lambda E: E.tensor_scalar_mul(
                mso[:, :], ps_e[:, 0, 128:256], 1.0 / 128.0))
            bank[("ps_e",)] = [kb.eauto("vector").tok()]
            kb.op("vector", lambda E: E.tensor_mul(oscr[:, :], muo[:, :],
                                                   muo[:, :]))
            tok_var = kb.op("vector", lambda E: E.tensor_sub(
                varo[:, :], mso[:, :], oscr[:, :]))
            kb.wait("scalar", tok_var)
            kb.op("scalar", lambda E: E.activation(mso[:, :], varo[:, :],
                                                   AF.Ln, bias=C["eps"][:, 0:1]))
            tok_rstd = kb.op("scalar", lambda E: E.activation(
                rstdo[:, :], mso[:, :], AF.Exp, scale=-0.5))
            kb.wait("vector", tok_rstd)
            kb.op("vector", lambda E: E.tensor_sub(oscr[:, :], o_s[:, :],
                                                   muo[:, :]))
            kb.op("vector", lambda E: E.tensor_mul(varo[:, :], oscr[:, :],
                                                   rstdo[:, :]))
            kb.op("vector", lambda E: E.tensor_scalar(
                oscr[:, :], varo[:, :], lngc[:, 0:1], lnbc[:, 0:1],
                ALU.mult, ALU.add))
            tok_og = kb.op("vector", lambda E, t0=t0, h=h: E.tensor_mul(
                ogsb[:, h % 2, t0:t0 + 128], oscr[:, :],
                gateh[:, t0:t0 + 128]))
        kb.wait("sync", kb.eauto("vector").tok())
        kb.dma("sync", og_out[h * 128:(h + 1) * 128, 0:n_chunks * 128],
               ogsb[:, h % 2, 0:n_chunks * 128], s_st)
        bank[("whuse",)] = kb.eauto("tensor").tok()
        kb.wait("vector", s_st.tok())
    kb.wait("sync", s_st.tok())
    return kb




def _run(kb, in_maps, **kw):
    nc = kb.build()
    return run_bass_kernel_spmd(nc, in_maps, core_ids=list(range(8)), **kw)



def emit_projpart(kb, io, name):
    w_in = io["w_sl"]
    in_loc = io["inT_loc"]
    out_p = _v3(io["partial"])
    s_w = kb.sem(f"s_w_{name}")
    s_i = [kb.sem(f"s_i0_{name}"), kb.sem(f"s_i1_{name}")]
    s_st = kb.sem(f"s_st_{name}")
    wT = kb.sbuf("ppw", [128, 4, D], BF16)
    isb = kb.sbuf("ppi", [128, 2, 4, 512], BF16)
    stg = kb.sbuf("ppstg", [128, 2, 512], F32)
    ps = kb.psum("pp_ps", [128, 2, 512], F32)
    load_wT(kb, wT, w_in, 512, s_w)
    w_tok = s_w.tok()
    bank = {}
    for sp in range(8):
        slot = sp % 2
        prev = bank.get(("iuse", slot))
        if prev is not None:
            kb.wait("sync", prev)
        for k in range(4):
            kb.dma("sync", isb[:, slot, k, :],
                   in_loc[k * 128:(k + 1) * 128, sp * 512:(sp + 1) * 512],
                   s_i[slot])
        i_tok = s_i[slot].tok()
        kb.wait("tensor", [w_tok, i_tok])
        for j in range(DC):
            pp = j % 2
            kb.wait("tensor", bank.get(("mm", pp)))
            def mmp(E, j=j, pp=pp, slot=slot):
                for k in range(3):
                    E.matmul(ps[:, pp, :], wT[:, k, j * 128:(j + 1) * 128],
                             isb[:, slot, k, :], start=(k == 0), stop=False)
                return E.matmul(ps[:, pp, :], wT[:, 3, j * 128:(j + 1) * 128],
                                isb[:, slot, 3, :], start=False, stop=True)
            tok_mm = kb.op("tensor", mmp)
            kb.wait("vector", [tok_mm, s_st.tok()])
            ev = kb.op("vector", lambda E, pp=pp: E.tensor_copy(
                stg[:, pp, :], ps[:, pp, :]))
            bank[("mm", pp)] = [ev]
            kb.wait("sync", ev)
            kb.dma("sync", out_p[j * 128:(j + 1) * 128, sp // 2,
                                 (sp % 2) * 512:(sp % 2) * 512 + 512],
                   stg[:, pp, :], s_st)
        bank[("iuse", slot)] = kb.eauto("tensor").tok()
    kb.wait("sync", s_st.tok())


RG4 = [[0, 1, 2, 3], [4, 5, 6, 7]]


def build_fused():
    kb = LB("fused")
    d = kb.din
    ins = {
        "x_sl": d("x_sl", [T_SL, D]),
        "w_cq": d("w_cq", [D, 1536]), "w_ckv": d("w_ckv", [D, 512]),
        "w_kr": d("w_kr", [D, 64]),
        "q_norm_w": d("q_norm_w", [1536]), "kv_norm_w": d("kv_norm_w", [512]),
        "wdqn_sl": d("wdqn_sl", [1536, 512]), "wdqr_sl": d("wdqr_sl", [1536, 256]),
        "wdkn_sl": d("wdkn_sl", [512, 512]), "wdv_sl": d("wdv_sl", [512, 512]),
        "rc": d("rc", [4, 64, 64]), "rsp": d("rsp", [4, 64, 64]),
        "rsn": d("rsn", [4, 64, 64]),
        "wproj_sl": d("wproj_sl", [512, D]),
        "gate_w": d("gate_w", [8, D]),
        "w1_sl": d("w1_sl", [2, 2816, D]), "w2_sl": d("w2_sl", [2, D, 1408]),
        "esel": d("esel", [8, 2]),
        "wq_sl": d("wq_sl", [D, 512]), "wk_sl": d("wk_sl", [D, 512]),
        "wv_sl": d("wv_sl", [D, 512]), "wg_sl": d("wg_sl", [D, 512]),
        "wa_sl": d("wa_sl", [D, 4]), "wb_sl": d("wb_sl", [D, 4]),
        "ba_sl": d("ba_sl", [4]), "bb_sl": d("bb_sl", [4]),
        "qcw_sl": d("qcw_sl", [512, 4]), "qcb_sl": d("qcb_sl", [512]),
        "kcw_sl": d("kcw_sl", [512, 4]), "kcb_sl": d("kcb_sl", [512]),
        "vcw_sl": d("vcw_sl", [512, 4]), "vcb_sl": d("vcb_sl", [512]),
        "ln_g": d("ln_g", [128]), "ln_b": d("ln_b", [128]),
        "wo_sl": d("wo_sl", [512, D]),
        "wgate_sl": d("wgate_sl", [D, D]), "wup_sl": d("wup_sl", [D, D]),
        "wdown_sl": d("wdown_sl", [D, D]),
    }
    x_out = kb.dout("x_out", [T_SL, D])
    di = kb.dram
    ag1_in = di("ag1_in", [2112, T_SL], BF16)
    ag1_out = di("ag1_out", [4 * 2112, T_SL], BF16)
    attn_loc = di("attn_loc", [512, S], BF16)
    rs1_in = di("rs1_in", [4 * D, T_SL], F32)
    rs1_out = di("rs1_out", [D, T_SL], F32)
    x1 = di("x1", [D, T_SL], F32)
    ag2_in = di("ag2_in", [D, T_SL], BF16)
    ag2_out = di("ag2_out", [4 * D, T_SL], BF16)
    cb_ag_in = di("cb_ag_in", [T_SL, 8], F32)
    cb_ag_out = di("cb_ag_out", [S, 8], F32)
    rs2_in = di("rs2_in", [4 * D, T_SL], F32)
    rs2_out = di("rs2_out", [D, T_SL], F32)
    x2 = di("x2", [D, T_SL], F32)
    ag3_in = di("ag3_in", [D, T_SL], BF16)
    ag3_out = di("ag3_out", [4 * D, T_SL], BF16)
    og_loc = di("og_loc", [512, S], BF16)
    rs3_in = di("rs3_in", [4 * D, T_SL], F32)
    rs3_out = di("rs3_out", [D, T_SL], F32)
    x3 = di("x3", [D, T_SL], F32)
    ag4_in = di("ag4_in", [D, T_SL], BF16)
    ag4_out = di("ag4_out", [4 * D, T_SL], BF16)
    rs4_in = di("rs4_in", [4 * D, T_SL], F32)
    rs4_out = di("rs4_out", [D, T_SL], F32)

    def coll(kind, op, i, o):
        kb.barrier()
        kb.op("gpsimd", lambda E, i=i, o=o, kind=kind, op=op: E.collective_compute(
            kind, op, replica_groups=RG4, ins=[i[:, :].opt()],
            outs=[o[:, :].opt()]))
        kb.barrier()

    with kb.scope():
        build_pre0(kb=kb, io={
            "x_sl": ins["x_sl"], "w_cq": ins["w_cq"], "w_ckv": ins["w_ckv"],
            "w_kr": ins["w_kr"], "q_norm_w": ins["q_norm_w"],
            "kv_norm_w": ins["kv_norm_w"],
            "nqT": ag1_in[0:1536, :], "nkvT": ag1_in[1536:2048, :],
            "kropeT": ag1_in[2048:2112, :]})
    coll("AllGather", ALU.bypass, ag1_in, ag1_out)
    with kb.scope():
        build_l0attn(kb=kb, io={
            "nqT_full": _v3(ag1_out)[0:1536], "nkvT_full": _v3(ag1_out)[1536:2048],
            "kropeT_full": _v3(ag1_out)[2048:2112],
            "wdqn_sl": ins["wdqn_sl"], "wdqr_sl": ins["wdqr_sl"],
            "wdkn_sl": ins["wdkn_sl"], "wdv_sl": ins["wdv_sl"],
            "rc": ins["rc"], "rsp": ins["rsp"], "rsn": ins["rsn"],
            "attnT": attn_loc})
    kb.barrier()
    with kb.scope():
        emit_projpart(kb, {"w_sl": ins["wproj_sl"], "inT_loc": attn_loc,
                           "partial": rs1_in}, "p1")
    coll("ReduceScatter", ALU.add, rs1_in, rs1_out)
    with kb.scope():
        build_post("l0post", False, True, gating=True, n_parts=1, kb=kb, io={
            "parts": rs1_out, "x_sl": ins["x_sl"], "gate_w": ins["gate_w"],
            "combine": cb_ag_in, "xT_new": x1, "xT_new_bf": ag2_in})
    coll("AllGather", ALU.bypass, ag2_in, ag2_out)
    coll("AllGather", ALU.bypass, cb_ag_in, cb_ag_out)
    with kb.scope():
        build_moe(kb=kb, io={
            "xT_bf_full": ag2_out, "w1_sl": ins["w1_sl"], "w2_sl": ins["w2_sl"],
            "combine_sl": cb_ag_out, "esel": ins["esel"], "moeP": rs2_in})
    coll("ReduceScatter", ALU.add, rs2_in, rs2_out)
    with kb.scope():
        build_post("moepost", False, False, n_parts=1, kb=kb, io={
            "parts": rs2_out, "xT_prev": x1, "xT_new": x2, "xT_new_bf": ag3_in})
    coll("AllGather", ALU.bypass, ag3_in, ag3_out)
    with kb.scope():
        build_l1attn(kb=kb, io={
            "xT_bf_full": ag3_out,
            "wq_sl": ins["wq_sl"], "wk_sl": ins["wk_sl"], "wv_sl": ins["wv_sl"],
            "wg_sl": ins["wg_sl"], "wa_sl": ins["wa_sl"], "wb_sl": ins["wb_sl"],
            "ba_sl": ins["ba_sl"], "bb_sl": ins["bb_sl"],
            "qcw_sl": ins["qcw_sl"], "qcb_sl": ins["qcb_sl"],
            "kcw_sl": ins["kcw_sl"], "kcb_sl": ins["kcb_sl"],
            "vcw_sl": ins["vcw_sl"], "vcb_sl": ins["vcb_sl"],
            "ln_g": ins["ln_g"], "ln_b": ins["ln_b"], "ogT": og_loc})
    kb.barrier()
    with kb.scope():
        emit_projpart(kb, {"w_sl": ins["wo_sl"], "inT_loc": og_loc,
                           "partial": rs3_in}, "p3")
    coll("ReduceScatter", ALU.add, rs3_in, rs3_out)
    with kb.scope():
        build_post("l1post", False, False, n_parts=1, kb=kb, io={
            "parts": rs3_out, "xT_prev": x2, "xT_new": x3, "xT_new_bf": ag4_in})
    coll("AllGather", ALU.bypass, ag4_in, ag4_out)
    with kb.scope():
        build_l1ffn(kb=kb, io={
            "xT_bf_full": ag4_out, "wg_sl": ins["wgate_sl"],
            "wu_sl": ins["wup_sl"], "wd_sl": ins["wdown_sl"], "ffnP": rs4_in})
    coll("ReduceScatter", ALU.add, rs4_in, rs4_out)
    with kb.scope():
        build_post("finpost", False, False, final=True, n_parts=1, kb=kb, io={
            "parts": rs4_out, "xT_prev": x3, "x_out": x_out})
    kb.barrier()
    return kb


def _esel(r):
    m = np.zeros((8, 2), np.float32)
    m[2 * r, 0] = 1.0
    m[2 * r + 1, 1] = 1.0
    return m


def kernel_fused(**inputs):
    inp = inputs
    ca = np.ascontiguousarray
    x = inp["L0_attn_x"]
    fc, fs = inp["L0_attn_freqs_cos"], inp["L0_attn_freqs_sin"]
    nc = _prog("fused", build_fused)
    maps = []
    for c in range(8):
        g, r = c // 4, c % 4
        rcm, rsp, rsn = _rope_mats_host(fc, fs, [4 * r + i for i in range(4)])
        sl5 = slice(512 * r, 512 * (r + 1))
        sl4 = slice(4 * r, 4 * (r + 1))
        maps.append({
            "x_sl": ca(x[g, r * 1024:(r + 1) * 1024]),
            "w_cq": inp["L0_attn_w_cq"], "w_ckv": inp["L0_attn_w_ckv"],
            "w_kr": inp["L0_attn_w_k_rope"],
            "q_norm_w": inp["L0_attn_q_norm_w"],
            "kv_norm_w": inp["L0_attn_kv_norm_w"],
            "wdqn_sl": ca(inp["L0_attn_w_dq_nope"][:, sl5]),
            "wdqr_sl": ca(inp["L0_attn_w_dq_rope"][:, 256 * r:256 * (r + 1)]),
            "wdkn_sl": ca(inp["L0_attn_w_dk_nope"][:, sl5]),
            "wdv_sl": ca(inp["L0_attn_w_dv"][:, sl5]),
            "rc": rcm, "rsp": rsp, "rsn": rsn,
            "wproj_sl": ca(inp["L0_attn_w_proj"][sl5, :]),
            "gate_w": inp["L0_ffn_gate_w"],
            "w1_sl": ca(inp["L0_ffn_w1"][2 * r:2 * r + 2]),
            "w2_sl": ca(inp["L0_ffn_w2"][2 * r:2 * r + 2]),
            "esel": _esel(r),
            "wq_sl": ca(inp["L1_attn_Wq"][:, sl5]),
            "wk_sl": ca(inp["L1_attn_Wk"][:, sl5]),
            "wv_sl": ca(inp["L1_attn_Wv"][:, sl5]),
            "wg_sl": ca(inp["L1_attn_Wg"][:, sl5]),
            "wa_sl": ca(inp["L1_attn_Wa"][:, sl4]),
            "wb_sl": ca(inp["L1_attn_Wb"][:, sl4]),
            "ba_sl": ca(inp["L1_attn_ba"][sl4]),
            "bb_sl": ca(inp["L1_attn_bb"][sl4]),
            "qcw_sl": ca(inp["L1_attn_qconv_w"][sl5, 0, :]),
            "qcb_sl": ca(inp["L1_attn_qconv_b"][sl5]),
            "kcw_sl": ca(inp["L1_attn_kconv_w"][sl5, 0, :]),
            "kcb_sl": ca(inp["L1_attn_kconv_b"][sl5]),
            "vcw_sl": ca(inp["L1_attn_vconv_w"][sl5, 0, :]),
            "vcb_sl": ca(inp["L1_attn_vconv_b"][sl5]),
            "ln_g": inp["L1_attn_ln_g"], "ln_b": inp["L1_attn_ln_b"],
            "wo_sl": ca(inp["L1_attn_Wo"][sl5, :]),
            "wgate_sl": ca(inp["L1_ffn_w_gate"][2048 * r:2048 * (r + 1), :]),
            "wup_sl": ca(inp["L1_ffn_w_up"][2048 * r:2048 * (r + 1), :]),
            "wdown_sl": ca(inp["L1_ffn_w_down"][:, 2048 * r:2048 * (r + 1)]),
        })
    res = _spmd(nc, maps)
    out = np.empty((2, 4096, 2048), np.float32)
    for c in range(8):
        g, r = c // 4, c % 4
        out[g, r * 1024:(r + 1) * 1024] = res[c]["x_out"]
    return out


_PROG_CACHE = {}


def _prog(name, builder, *a, **kw):
    if name not in _PROG_CACHE:
        _PROG_CACHE[name] = builder(*a, **kw).build()
    return _PROG_CACHE[name]


def _rope_mats_host(fc, fs, heads):
    rc = np.zeros((4, 64, 64), np.float32)
    rsp = np.zeros((4, 64, 64), np.float32)
    rsn = np.zeros((4, 64, 64), np.float32)
    for i, gh in enumerate(heads):
        for j in range(32):
            rc[i, 2 * j, 2 * j] = fc[gh, j]
            rc[i, 2 * j + 1, 2 * j + 1] = fc[gh, j]
            rsp[i, 2 * j, 2 * j + 1] = fs[gh, j]
            rsn[i, 2 * j + 1, 2 * j] = fs[gh, j]
    return rc, rsp, rsn


def _spmd(nc, in_maps):
    return run_bass_kernel_spmd(nc, in_maps, core_ids=list(range(8))).results


def kernel(**inputs):
    _ = inputs.get("L0_attn_mask")
    return kernel_fused(**inputs)


def kernel_unfused(**inputs):
    inp = inputs
    ca = np.ascontiguousarray
    x = inp["L0_attn_x"]
    fc, fs = inp["L0_attn_freqs_cos"], inp["L0_attn_freqs_sin"]

    nc = _prog("pre0", build_pre0)
    maps = []
    for c in range(8):
        g, r = c // 4, c % 4
        maps.append({
            "x_sl": ca(x[g, r * 1024:(r + 1) * 1024]),
            "w_cq": inp["L0_attn_w_cq"], "w_ckv": inp["L0_attn_w_ckv"],
            "w_kr": inp["L0_attn_w_k_rope"],
            "q_norm_w": inp["L0_attn_q_norm_w"],
            "kv_norm_w": inp["L0_attn_kv_norm_w"],
        })
    r1 = _spmd(nc, maps)

    nq_full = [np.concatenate([r1[4 * g + i]["nqT"] for i in range(4)], axis=1)
               for g in range(2)]
    nkv_full = [np.concatenate([r1[4 * g + i]["nkvT"] for i in range(4)], axis=1)
                for g in range(2)]
    kr_full = [np.concatenate([r1[4 * g + i]["kropeT"] for i in range(4)], axis=1)
               for g in range(2)]
    nc = _prog("l0attn", build_l0attn)
    maps = []
    for c in range(8):
        g, r = c // 4, c % 4
        rc, rsp, rsn = _rope_mats_host(fc, fs, [4 * r + i for i in range(4)])
        maps.append({
            "nqT_full": ca(nq_full[g]), "nkvT_full": ca(nkv_full[g]),
            "kropeT_full": ca(kr_full[g]),
            "wdqn_sl": ca(inp["L0_attn_w_dq_nope"][:, 512 * r:512 * (r + 1)]),
            "wdqr_sl": ca(inp["L0_attn_w_dq_rope"][:, 256 * r:256 * (r + 1)]),
            "wdkn_sl": ca(inp["L0_attn_w_dk_nope"][:, 512 * r:512 * (r + 1)]),
            "wdv_sl": ca(inp["L0_attn_w_dv"][:, 512 * r:512 * (r + 1)]),
            "rc": rc, "rsp": rsp, "rsn": rsn,
        })
    r2 = _spmd(nc, maps)

    attn_full = [np.concatenate([r2[4 * g + i]["attnT"] for i in range(4)], axis=0)
                 for g in range(2)]
    nc = _prog("l0post", build_post, "l0post", True, True, gating=True)
    maps = []
    for c in range(8):
        g, r = c // 4, c % 4
        maps.append({
            "inT_sl": ca(attn_full[g][:, r * 1024:(r + 1) * 1024]),
            "w": inp["L0_attn_w_proj"],
            "x_sl": ca(x[g, r * 1024:(r + 1) * 1024]),
            "gate_w": inp["L0_ffn_gate_w"],
        })
    r3 = _spmd(nc, maps)

    x1bf_full = [np.concatenate([r3[4 * g + i]["xT_new_bf"] for i in range(4)],
                                axis=1) for g in range(2)]
    comb_full = [np.concatenate([r3[4 * g + i]["combine"] for i in range(4)],
                                axis=0) for g in range(2)]
    nc = _prog("moe", build_moe)
    maps = []
    for c in range(8):
        g, r = c // 4, c % 4
        maps.append({
            "xT_bf_full": ca(x1bf_full[g]),
            "w1_sl": ca(inp["L0_ffn_w1"][2 * r:2 * r + 2]),
            "w2_sl": ca(inp["L0_ffn_w2"][2 * r:2 * r + 2]),
            "combine_sl": ca(comb_full[g][:, 2 * r:2 * r + 2]),
        })
    r4 = _spmd(nc, maps)

    nc = _prog("sumpost", build_post, "sumpost", False, False)
    maps = []
    for c in range(8):
        g, r = c // 4, c % 4
        parts = np.stack([r4[4 * g + i]["moeP"][:, r * 1024:(r + 1) * 1024]
                          for i in range(4)], axis=0)
        maps.append({"parts": ca(parts), "xT_prev": ca(r3[c]["xT_new"])})
    r5 = _spmd(nc, maps)

    x2bf_full = [np.concatenate([r5[4 * g + i]["xT_new_bf"] for i in range(4)],
                                axis=1) for g in range(2)]
    nc = _prog("l1attn", build_l1attn)
    maps = []
    for c in range(8):
        g, r = c // 4, c % 4
        sl5 = slice(512 * r, 512 * (r + 1))
        sl4 = slice(4 * r, 4 * (r + 1))
        maps.append({
            "xT_bf_full": ca(x2bf_full[g]),
            "wq_sl": ca(inp["L1_attn_Wq"][:, sl5]),
            "wk_sl": ca(inp["L1_attn_Wk"][:, sl5]),
            "wv_sl": ca(inp["L1_attn_Wv"][:, sl5]),
            "wg_sl": ca(inp["L1_attn_Wg"][:, sl5]),
            "wa_sl": ca(inp["L1_attn_Wa"][:, sl4]),
            "wb_sl": ca(inp["L1_attn_Wb"][:, sl4]),
            "ba_sl": ca(inp["L1_attn_ba"][sl4]),
            "bb_sl": ca(inp["L1_attn_bb"][sl4]),
            "qcw_sl": ca(inp["L1_attn_qconv_w"][sl5, 0, :]),
            "qcb_sl": ca(inp["L1_attn_qconv_b"][sl5]),
            "kcw_sl": ca(inp["L1_attn_kconv_w"][sl5, 0, :]),
            "kcb_sl": ca(inp["L1_attn_kconv_b"][sl5]),
            "vcw_sl": ca(inp["L1_attn_vconv_w"][sl5, 0, :]),
            "vcb_sl": ca(inp["L1_attn_vconv_b"][sl5]),
            "ln_g": inp["L1_attn_ln_g"], "ln_b": inp["L1_attn_ln_b"],
        })
    r6 = _spmd(nc, maps)

    og_full = [np.concatenate([r6[4 * g + i]["ogT"] for i in range(4)], axis=0)
               for g in range(2)]
    nc = _prog("l1post", build_post, "l1post", True, False)
    maps = []
    for c in range(8):
        g, r = c // 4, c % 4
        maps.append({
            "inT_sl": ca(og_full[g][:, r * 1024:(r + 1) * 1024]),
            "w": inp["L1_attn_Wo"],
            "xT_prev": ca(r5[c]["xT_new"]),
        })
    r7 = _spmd(nc, maps)

    x3bf_full = [np.concatenate([r7[4 * g + i]["xT_new_bf"] for i in range(4)],
                                axis=1) for g in range(2)]
    nc = _prog("l1ffn", build_l1ffn)
    maps = []
    for c in range(8):
        g, r = c // 4, c % 4
        maps.append({
            "xT_bf_full": ca(x3bf_full[g]),
            "wg_sl": ca(inp["L1_ffn_w_gate"][2048 * r:2048 * (r + 1), :]),
            "wu_sl": ca(inp["L1_ffn_w_up"][2048 * r:2048 * (r + 1), :]),
            "wd_sl": ca(inp["L1_ffn_w_down"][:, 2048 * r:2048 * (r + 1)]),
        })
    r8 = _spmd(nc, maps)

    nc = _prog("finpost", build_post, "finpost", False, False, final=True)
    maps = []
    for c in range(8):
        g, r = c // 4, c % 4
        parts = np.stack([r8[4 * g + i]["ffnP"][:, r * 1024:(r + 1) * 1024]
                          for i in range(4)], axis=0)
        maps.append({"parts": ca(parts), "xT_prev": ca(r7[c]["xT_new"])})
    r9 = _spmd(nc, maps)

    out = np.empty((2, 4096, 2048), np.float32)
    for c in range(8):
        g, r = c // 4, c % 4
        out[g, r * 1024:(r + 1) * 1024] = r9[c]["x_out"]
    return out


# version 04
# speedup vs baseline: 1.1172x; 1.1172x over previous
import math
import numpy as np
import concourse.bass as bass
import concourse.mybir as mybir
from concourse.bass_utils import run_bass_kernel_spmd

F32 = mybir.dt.float32
BF16 = mybir.dt.bfloat16
AF = mybir.ActivationFunctionType
ALU = mybir.AluOpType
AX = mybir.AxisListType

B = 2
S = 4096
T_SL = 1024
D = 2048
DC = D // 128
NEG = -1e30
EPS = 1e-5



class Sem:
    def __init__(self, handle):
        self.h = handle
        self.n = 0

    def tok(self):
        return (self, self.n)


class LB:

    ENGINES = ("tensor", "vector", "scalar", "gpsimd", "sync")
    AUTOCHAIN = {"vector": True, "scalar": True, "gpsimd": True,
                 "tensor": False, "sync": False}

    def __init__(self, name):
        self.nc = bass.Bass("TRN2", target_bir_lowering=False, debug=False)
        self.name = name
        self.q = {e: [] for e in self.ENGINES}
        self._ctx = []
        self._mem = []
        self._waited = {e: {} for e in self.ENGINES}
        self.auto = {}
        self._all_sems = []
        self._uniq = 0

    def scope(self):
        from contextlib import contextmanager

        @contextmanager
        def _s():
            mark = len(self._mem)
            yield
            for g in reversed(self._mem[mark:]):
                g.__exit__(None, None, None)
            del self._mem[mark:]
        return _s()

    def barrier(self):
        toks = [s.tok() for s in self._all_sems]
        for e in self.ENGINES:
            self.wait(e, toks)

    def eauto(self, eng):
        if eng not in self.auto:
            self.auto[eng] = self.sem(f"s_auto_{eng}")
        return self.auto[eng]

    def dram(self, name, shape, dtype, kind="Internal"):
        return self.nc.dram_tensor(name, list(shape), dtype, kind=kind).ap()

    def din(self, name, shape, dtype=F32):
        return self.dram(name, shape, dtype, kind="ExternalInput")

    def dout(self, name, shape, dtype=F32):
        return self.dram(name, shape, dtype, kind="ExternalOutput")

    def sbuf(self, name, shape, dtype):
        self._uniq += 1
        g = self.nc.sbuf_tensor(f"{name}_u{self._uniq}", list(shape), dtype)
        t = g.__enter__()
        self._mem.append(g)
        return t

    def psum(self, name, shape, dtype=F32):
        self._uniq += 1
        g = self.nc.psum_tensor(f"{name}_u{self._uniq}", list(shape), dtype)
        t = g.__enter__()
        self._mem.append(g)
        return t

    def sem(self, name):
        self._uniq += 1
        g = self.nc.semaphore(f"{name}_u{self._uniq}")
        h = g.__enter__()
        self._ctx.append(g)
        s = Sem(h)
        self._all_sems.append(s)
        return s

    def add(self, eng, fn):
        self.q[eng].append(fn)

    def wait(self, eng, tok):
        if tok is None:
            return
        if isinstance(tok, list):
            for t in tok:
                self.wait(eng, t)
            return
        sem, val = tok
        if val <= 0:
            return
        prev = self._waited[eng].get(sem, 0)
        if val <= prev:
            return
        self._waited[eng][sem] = val
        self.add(eng, lambda E, s=sem.h, v=val: E.wait_ge(s, v))

    def op(self, eng, fn, sem=None, n=1, chain=None):
        a = self.eauto(eng)
        if chain is None:
            chain = self.AUTOCHAIN[eng]
        if chain:
            self.wait(eng, a.tok())
        a.n += 1
        self.add(eng, lambda E, f=fn, s=a.h, n=a.n: f(E).then_inc(s, 1))
        return a.tok()

    def dma(self, eng, out, in_, sem, **kw):
        sem.n += 16
        self.add(eng, lambda E, o=out, i=in_, s=sem.h, kw=kw:
                 E.dma_start(out=o, in_=i, **kw).then_inc(s, 16))
        return sem.tok()

    def build(self):
        with self.nc.Block() as block:
            for e in self.ENGINES:
                ops = self.q[e]
                if not ops:
                    continue
                getattr(block, e)(lambda E, ops=ops: [f(E) for f in ops])
        for g in reversed(self._mem):
            g.__exit__(None, None, None)
        self._mem = []
        for g in reversed(self._ctx):
            g.__exit__(None, None, None)
        return self.nc


def make_consts(kb):
    g = kb.sem("s_consts")
    out = {}
    ident = kb.sbuf("c_ident", [128, 128], BF16)
    t0 = kb.op("gpsimd", lambda E, t=ident: E.memset(t[:, :], 0.0), g)
    kb.wait("gpsimd", t0)
    kb.op("gpsimd", lambda E, t=ident: E.affine_select(
        out=t[:, :], in_=t[:, :], compare_op=ALU.not_equal, fill=1.0,
        base=0, pattern=[[-1, 128]], channel_multiplier=1), g)
    out["ident"] = ident
    ones_col = kb.sbuf("c_ones_col", [128, 1], BF16)
    kb.op("gpsimd", lambda E, t=ones_col: E.memset(t[:, :], 1.0), g)
    out["ones_col"] = ones_col
    ones_row = kb.sbuf("c_ones_row", [1, 128], BF16)
    out["ones_row"] = ones_row
    kb.op("gpsimd", lambda E, t=ones_row: E.memset(t[:, :], 1.0), g)
    ones128 = kb.sbuf("c_ones128", [128, 128], BF16)
    out["ones128"] = ones128
    kb.op("gpsimd", lambda E, t=ones128: E.memset(t[:, :], 1.0), g)
    epsc = kb.sbuf("c_eps", [128, 1], F32)
    out["eps"] = epsc
    kb.op("gpsimd", lambda E, t=epsc: E.memset(t[:, :], EPS), g)
    out["tok"] = g.tok()
    return out


def load_wT(kb, sb, hbm, in_dim, sem, eng="gpsimd", cols=None):
    kc = in_dim // 128
    for k in range(kc):
        src = hbm[k * 128:(k + 1) * 128, :] if cols is None else \
            hbm[k * 128:(k + 1) * 128, cols[0]:cols[1]]
        tok = kb.dma(eng, sb[:, k, :], src, sem)
    return tok


def emit_transpose_fm(kb, dst, dst_cols, src_ap, ident, ps_tp, pp,
                      s_pe, s_ve, pre_waits_t=None, eng_ev="vector", s_ev=None):
    if pre_waits_t:
        kb.wait("tensor", pre_waits_t)
    tok_t = kb.op("tensor", lambda E: E.transpose(
        ps_tp[:, pp, 0:src_ap.shape[0]], src_ap, ident[:, :]), s_pe)
    kb.wait(eng_ev, tok_t)
    tok_v = kb.op(eng_ev, lambda E: E.tensor_copy(
        dst[:, dst_cols[0]:dst_cols[1]], ps_tp[:, pp, 0:src_ap.shape[0]]),
        s_ev or s_ve)
    return tok_t, tok_v



def build_pre0(stage="all", kb=None, io=None):
    own = kb is None
    if own:
        kb = LB("pre0")
        io = {}
    x_sl = io.get("x_sl") if "x_sl" in io else kb.din("x_sl", [T_SL, D])
    w_cq = io.get("w_cq") if "w_cq" in io else kb.din("w_cq", [D, 1536])
    w_ckv = io.get("w_ckv") if "w_ckv" in io else kb.din("w_ckv", [D, 512])
    w_kr = io.get("w_kr") if "w_kr" in io else kb.din("w_kr", [D, 64])
    qnw_in = io.get("q_norm_w") if "q_norm_w" in io else kb.din("q_norm_w", [1536])
    kvnw_in = io.get("kv_norm_w") if "kv_norm_w" in io else kb.din("kv_norm_w", [512])
    nq_out = io.get("nqT") if "nqT" in io else kb.dout("nqT", [1536, T_SL], BF16)
    nkv_out = io.get("nkvT") if "nkvT" in io else kb.dout("nkvT", [512, T_SL], BF16)
    kr_out = io.get("kropeT") if "kropeT" in io else kb.dout("kropeT", [64, T_SL], BF16)

    s_ld = kb.sem("s_ld")
    s_ldx = kb.sem("s_ldx")
    s_pe = kb.eauto("tensor")
    s_ve = kb.eauto("vector")
    s_se = kb.eauto("scalar")
    s_st = kb.sem("s_st")

    C = make_consts(kb)
    ident, ones128 = C["ident"], C["ones128"]

    xtm = kb.sbuf("xtm", [128, 8, D], BF16)
    xT = kb.sbuf("xT", [128, DC, T_SL], BF16)
    wcq_t = kb.sbuf("wcq_t", [128, DC, 1536], BF16)
    wckv_t = kb.sbuf("wckv_t", [128, DC, 512], BF16)
    wkr_t = kb.sbuf("wkr_t", [128, DC, 64], BF16)
    qnw = kb.sbuf("qnw", [128, 12], F32)
    kvnw = kb.sbuf("kvnw", [128, 4], F32)
    pre = kb.sbuf("pre", [128, 12, T_SL], BF16)
    tmp_ln = kb.sbuf("tmp_ln", [128, 512], F32)
    sq = kb.sbuf("sq", [128, 12, 512], BF16)
    rstd_bc = kb.sbuf("rstd_bc", [128, 2, 512], BF16)
    outbuf = kb.sbuf("outbuf", [128, 12, T_SL], BF16)
    ps_tp = kb.psum("ps_tp", [128, 2, 1024], BF16)
    ps_mm = kb.psum("ps_mm", [128, 2, 512], F32)
    ps_st = kb.psum("ps_st", [128, 2, 512], F32)

    t_x = None
    for i in range(8):
        t_x = kb.dma("gpsimd", xtm[:, i, :], x_sl[i * 128:(i + 1) * 128, :], s_ldx)
    load_wT(kb, wcq_t, w_cq, D, s_ld)
    load_wT(kb, wckv_t, w_ckv, D, s_ld)
    load_wT(kb, wkr_t, w_kr, D, s_ld)
    for j in range(12):
        kb.dma("gpsimd", qnw[:, j:j + 1], qnw_in[j * 128:(j + 1) * 128][:, None], s_ld)
    for j in range(4):
        kb.dma("gpsimd", kvnw[:, j:j + 1], kvnw_in[j * 128:(j + 1) * 128][:, None], s_ld)
    all_loads = s_ld.tok()

    kb.wait("tensor", t_x)
    kb.wait("tensor", C["tok"])
    ev_toks = [None, None]
    for idx in range(0 if stage == "lv6" else 8 * DC):
        i, k = divmod(idx, DC)
        pp = idx % 2
        kb.wait("tensor", ev_toks[pp])
        tok_t = kb.op("tensor", lambda E, i=i, k=k, pp=pp: E.transpose(
            ps_tp[:, pp, 0:128], xtm[:, i, k * 128:(k + 1) * 128], ident[:, :]), s_pe)
        kb.wait("vector", tok_t)
        ev_toks[pp] = kb.op("vector", lambda E, i=i, k=k, pp=pp: E.tensor_copy(
            xT[:, k, i * 128:(i + 1) * 128], ps_tp[:, pp, 0:128]), s_ve)
    xT_ready = s_ve.tok()

    def proj(wt, OC, nw_tile, out_ext, do_norm, mw=128, level=0):
        snap = [s_pe.tok(), s_ve.tok(), s_se.tok(), s_st.tok()]
        for e in ("tensor", "vector", "scalar"):
            kb.wait(e, snap)
        kb.wait("tensor", [xT_ready, all_loads])
        DIMN = OC * mw
        bank_free = [None, None]
        for si in range(2):
            tok_sq = None
            for j in range(OC):
                pp = j % 2
                kb.wait("tensor", bank_free[pp])
                def mm(E, j=j, si=si, pp=pp, wt=wt, mw=mw):
                    for k in range(DC - 1):
                        E.matmul(ps_mm[0:mw, pp, :],
                                 wt[:, k, j * mw:(j + 1) * mw],
                                 xT[:, k, si * 512:(si + 1) * 512],
                                 start=(k == 0), stop=False)
                    return E.matmul(ps_mm[0:mw, pp, :],
                                    wt[:, DC - 1, j * mw:(j + 1) * mw],
                                    xT[:, DC - 1, si * 512:(si + 1) * 512],
                                    start=False, stop=True)
                tok_mm = kb.op("tensor", mm)
                if do_norm:
                    kb.wait("scalar", tok_mm)
                    if level == 4:
                        tok_sq = kb.op("scalar", lambda E, j=j, pp=pp: E.copy(
                            sq[:, j, :], ps_mm[:, pp, :]))
                    elif level == 5:
                        tok_sq = tok_mm
                    else:
                        tok_sq = kb.op("scalar", lambda E, j=j, pp=pp: E.activation(
                            sq[:, j, :], ps_mm[:, pp, :], AF.Square))
                    kb.wait("vector", tok_sq)
                    tok_ev = kb.op("vector", lambda E, j=j, si=si, pp=pp:
                                   E.tensor_copy(pre[:, j, si * 512:(si + 1) * 512],
                                                 ps_mm[:, pp, :]))
                    bank_free[pp] = [tok_sq, tok_ev]
                else:
                    kb.wait("vector", tok_mm)
                    tok_ev = kb.op("vector", lambda E, si=si, pp=pp, mw=mw:
                                   E.tensor_copy(
                        outbuf[0:mw, 0, si * 512:(si + 1) * 512],
                        ps_mm[0:mw, pp, :]))
                    bank_free[pp] = [tok_ev]
            if not do_norm or level >= 3:
                continue
            kb.wait("tensor", tok_sq)
            def stats(E, si=si, OC=OC):
                for j in range(OC - 1):
                    E.matmul(ps_st[:, si, :], ones128[:, :], sq[:, j, :],
                             start=(j == 0), stop=False)
                return E.matmul(ps_st[:, si, :], ones128[:, :], sq[:, OC - 1, :],
                                start=(OC == 1), stop=True)
            tok_stats = kb.op("tensor", stats)
            kb.wait("scalar", tok_stats)
            kb.op("scalar", lambda E, si=si, DIMN=DIMN: E.activation(
                tmp_ln[:, :], ps_st[:, si, :], AF.Ln,
                bias=C["eps"][:, 0:1], scale=1.0 / DIMN))
            tok_r = kb.op("scalar", lambda E, si=si: E.activation(
                rstd_bc[:, si, :], tmp_ln[:, :], AF.Exp, scale=-0.5))
            kb.wait("vector", tok_r)
            for j in range(OC):
                if level >= 2:
                    kb.op("vector", lambda E, j=j, si=si: E.tensor_copy(
                        outbuf[:, j, si * 512:(si + 1) * 512],
                        pre[:, j, si * 512:(si + 1) * 512]))
                elif level == 1:
                    kb.op("vector", lambda E, j=j, si=si: E.tensor_mul(
                        outbuf[:, j, si * 512:(si + 1) * 512],
                        pre[:, j, si * 512:(si + 1) * 512],
                        rstd_bc[:, si, :]))
                else:
                    kb.op("vector", lambda E, j=j, si=si: E.scalar_tensor_tensor(
                        outbuf[:, j, si * 512:(si + 1) * 512],
                        pre[:, j, si * 512:(si + 1) * 512],
                        nw_tile[:, j:j + 1],
                        rstd_bc[:, si, :], ALU.mult, ALU.mult))
        tok_fin = kb.eauto("vector").tok()
        kb.wait("sync", tok_fin)
        t = None
        if do_norm:
            for j in range(OC):
                t = kb.dma("sync", out_ext[j * 128:(j + 1) * 128, :],
                           outbuf[:, j, 0:T_SL], s_st)
        else:
            t = kb.dma("sync", out_ext[:, :], outbuf[0:mw, 0, 0:T_SL], s_st)
        return t

    if stage == "tp":
        kb.wait("sync", xT_ready)
        kb.dma("sync", nq_out[0:128, :], xT[:, 0, 0:T_SL], s_st)
    elif stage == "krope":
        proj(wkr_t, 1, None, kr_out, False, mw=64)
    elif stage == "proj1":
        proj(wcq_t, 12, qnw, nq_out, True)
    elif stage == "lv6":
        proj(wcq_t, 12, qnw, nq_out, True, level=4)
    elif stage.startswith("lv"):
        proj(wcq_t, 12, qnw, nq_out, True, level=int(stage[2:]))
    else:
        proj(wcq_t, 12, qnw, nq_out, True)
        proj(wckv_t, 4, kvnw, nkv_out, True)
        proj(wkr_t, 1, None, kr_out, False, mw=64)
    kb.wait("sync", s_st.tok())
    return kb


def _v3(ap, rows=None):
    if len(ap.shape) == 3:
        return ap
    if ap.shape[1] == S:
        return ap.rearrange("a (r t) -> a r t", r=4)
    v = ap.rearrange("(r a) t -> a r t", r=4)
    return v if rows is None else v[rows[0]:rows[1]]



ASCALE = 1.0 / math.sqrt(192.0)
VSCALE = 1.0 / math.sqrt(2048.0)


def build_l0attn(n_heads=4, n_qt=32, kb=None, io=None):
    own = kb is None
    if own:
        kb = LB("l0attn")
        io = {}
    nq_in = _v3(io["nqT_full"] if "nqT_full" in io else
                kb.din("nqT_full", [1536, S], BF16))
    nkv_in = _v3(io["nkvT_full"] if "nkvT_full" in io else
                 kb.din("nkvT_full", [512, S], BF16))
    kr_in = _v3(io["kropeT_full"] if "kropeT_full" in io else
                kb.din("kropeT_full", [64, S], BF16))
    wdqn = io.get("wdqn_sl") if "wdqn_sl" in io else kb.din("wdqn_sl", [1536, 512])
    wdqr = io.get("wdqr_sl") if "wdqr_sl" in io else kb.din("wdqr_sl", [1536, 256])
    wdkn = io.get("wdkn_sl") if "wdkn_sl" in io else kb.din("wdkn_sl", [512, 512])
    wdv = io.get("wdv_sl") if "wdv_sl" in io else kb.din("wdv_sl", [512, 512])
    rc_in = io.get("rc") if "rc" in io else kb.din("rc", [4, 64, 64])
    rsp_in = io.get("rsp") if "rsp" in io else kb.din("rsp", [4, 64, 64])
    rsn_in = io.get("rsn") if "rsn" in io else kb.din("rsn", [4, 64, 64])
    attn_out = io.get("attnT") if "attnT" in io else kb.dout("attnT", [512, S], BF16)

    s_ldw = kb.sem("s_ldw")
    s_ldkv = kb.sem("s_ldkv")
    s_nq = [kb.sem("s_nq0"), kb.sem("s_nq1")]
    s_st = kb.sem("s_st")
    C = make_consts(kb)
    ident = C["ident"]
    cmask = kb.sbuf("cmask", [128, 128], BF16)
    kb.op("gpsimd", lambda E: E.memset(cmask[:, :], 0.0))
    tok_cm = kb.op("gpsimd", lambda E: E.affine_select(
        out=cmask[:, :], in_=cmask[:, :], compare_op=ALU.is_ge, fill=NEG,
        base=0, pattern=[[-1, 128]], channel_multiplier=1))

    nkvT = kb.sbuf("nkvT", [128, 4, S], BF16)
    kropeT = kb.sbuf("kropeT", [64, S], BF16)
    nqs = kb.sbuf("nqs", [128, 2, 12, 512], BF16)
    wdqn_t = kb.sbuf("wdqn_t", [128, 12, 512], BF16)
    wdqr_t = kb.sbuf("wdqr_t", [128, 12, 256], BF16)
    wdkn_t = kb.sbuf("wdkn_t", [128, 4, 512], BF16)
    wdv_t = kb.sbuf("wdv_t", [128, 4, 512], BF16)
    rcs = kb.sbuf("rcs", [64, 3, 4, 64], F32)
    rt = kb.sbuf("rt", [64, 4, 64], BF16)
    qnT = kb.sbuf("qnT", [128, S], BF16)
    qrT = kb.sbuf("qrT", [64, S], BF16)
    qr_raw = kb.sbuf("qr_raw", [64, S], BF16)
    knT = kb.sbuf("knT", [128, S], BF16)
    krh = kb.sbuf("krh", [64, S], BF16)
    v_tm = kb.sbuf("v_tm", [128, 32, 128], BF16)
    sc = kb.sbuf("sc", [128, S], F32)
    probs = kb.sbuf("probs", [128, S], BF16)
    probs2 = kb.sbuf("probs2", [128, S], BF16)
    pT = kb.sbuf("pT", [128, 32, 128], BF16)
    attn_h = kb.sbuf("attn_h", [128, S], BF16)
    cmax = kb.sbuf("cmax", [128, 8], F32)
    gmax = kb.sbuf("gmax", [128, 1], F32)
    negmax = kb.sbuf("negmax", [128, 1], F32)
    sums = kb.sbuf("sums", [128, 1], F32)
    rsum = kb.sbuf("rsum", [128, 1], F32)
    ps_mm = kb.psum("ps_mm", [128, 2, 512], F32)
    ps_tp = kb.psum("ps_tp", [128, 2, 1024], BF16)
    ps_pv = kb.psum("ps_pv", [128, 2, 512], F32)
    ps_x1 = kb.psum("ps_x1", [128, 1, 512], F32)

    for c4 in range(4):
        kb.dma("gpsimd", nkvT[:, c4, :].rearrange("p (r t) -> p r t", r=4),
               nkv_in[c4 * 128:(c4 + 1) * 128], s_ldkv)
    kb.dma("gpsimd", kropeT[:, :].rearrange("p (r t) -> p r t", r=4),
           kr_in[0:64], s_ldkv)
    load_wT(kb, wdqn_t, wdqn, 1536, s_ldw)
    load_wT(kb, wdqr_t, wdqr, 1536, s_ldw)
    load_wT(kb, wdkn_t, wdkn, 512, s_ldw)
    load_wT(kb, wdv_t, wdv, 512, s_ldw)
    for i, t in enumerate((rc_in, rsp_in, rsn_in)):
        for h in range(4):
            kb.dma("gpsimd", rcs[:, i, h, :], t[h, :, :], s_ldw)
    ldw_tok = s_ldw.tok()
    ldkv_tok = s_ldkv.tok()

    kb.wait("vector", ldw_tok)
    for h in range(4):
        kb.op("vector", lambda E, h=h: E.tensor_add(
            rt[:, h, :], rcs[:, 0, h, :], rcs[:, 1, h, :]))
        kb.op("vector", lambda E, h=h: E.tensor_sub(
            rt[:, h, :], rt[:, h, :], rcs[:, 2, h, :]))
    rt_tok = kb.eauto("vector").tok()

    nq_span_idx = 0
    nq_used = {}

    def head_attention(h):
        nonlocal nq_span_idx
        s_t, s_v, s_s = kb.eauto("tensor"), kb.eauto("vector"), kb.eauto("scalar")

        kb.wait("tensor", [ldw_tok, ldkv_tok, rt_tok, tok_cm])
        for sp in range(8):
            slot = nq_span_idx % 2
            prev = nq_used.get(slot)
            if prev is not None:
                kb.wait("sync", prev)
            for j in range(12):
                kb.dma("sync", nqs[:, slot, j, :],
                       nq_in[j * 128:(j + 1) * 128, sp // 2,
                             (sp % 2) * 512:(sp % 2) * 512 + 512],
                       s_nq[slot])
            kb.wait("tensor", s_nq[slot].tok())
            pp = sp % 2
            kb.wait("tensor", kb._l0_bank.get(("mm", pp)))
            def mmq(E, h=h, sp=sp, slot=slot, pp=pp):
                for j in range(11):
                    E.matmul(ps_mm[:, pp, :], wdqn_t[:, j, h * 128:(h + 1) * 128],
                             nqs[:, slot, j, :], start=(j == 0), stop=False)
                return E.matmul(ps_mm[:, pp, :], wdqn_t[:, 11, h * 128:(h + 1) * 128],
                                nqs[:, slot, 11, :], start=False, stop=True)
            tok_q = kb.op("tensor", mmq)
            kb.wait("vector", tok_q)
            ev_q = kb.op("vector", lambda E, sp=sp, pp=pp: E.tensor_scalar_mul(
                qnT[:, sp * 512:(sp + 1) * 512], ps_mm[:, pp, :], ASCALE))
            kb._l0_bank[("mm", pp)] = [ev_q]
            pp2 = sp % 2
            kb.wait("tensor", kb._l0_bank.get(("pv", pp2)))
            def mmqr(E, h=h, sp=sp, slot=slot, pp2=pp2):
                for j in range(11):
                    E.matmul(ps_pv[0:64, pp2, :], wdqr_t[:, j, h * 64:(h + 1) * 64],
                             nqs[:, slot, j, :], start=(j == 0), stop=False)
                return E.matmul(ps_pv[0:64, pp2, :], wdqr_t[:, 11, h * 64:(h + 1) * 64],
                                nqs[:, slot, 11, :], start=False, stop=True)
            tok_qr = kb.op("tensor", mmqr)
            nq_used[slot] = tok_qr
            nq_span_idx += 1
            kb.wait("vector", tok_qr)
            ev_qr = kb.op("vector", lambda E, sp=sp, pp2=pp2: E.tensor_copy(
                qr_raw[:, sp * 512:(sp + 1) * 512], ps_pv[0:64, pp2, :]))
            kb._l0_bank[("pv", pp2)] = [ev_qr]
        for sp in range(8):
            pp2 = sp % 2
            kb.wait("tensor", kb._l0_bank.get(("pv", pp2)))
            kb.wait("tensor", kb.eauto("vector").tok())
            tok_r = kb.op("tensor", lambda E, h=h, sp=sp, pp2=pp2: E.matmul(
                ps_pv[0:64, pp2, :], rt[:, h, :],
                qr_raw[:, sp * 512:(sp + 1) * 512], start=True, stop=True))
            kb.wait("vector", tok_r)
            ev = kb.op("vector", lambda E, sp=sp, pp2=pp2: E.tensor_scalar_mul(
                qrT[:, sp * 512:(sp + 1) * 512], ps_pv[0:64, pp2, :], ASCALE))
            kb._l0_bank[("pv", pp2)] = [ev]
        for sp in range(8):
            pp = sp % 2
            kb.wait("tensor", kb._l0_bank.get(("mm", pp)))
            def mmk(E, h=h, sp=sp, pp=pp):
                for j in range(3):
                    E.matmul(ps_mm[:, pp, :], wdkn_t[:, j, h * 128:(h + 1) * 128],
                             nkvT[:, j, sp * 512:(sp + 1) * 512],
                             start=(j == 0), stop=False)
                return E.matmul(ps_mm[:, pp, :], wdkn_t[:, 3, h * 128:(h + 1) * 128],
                                nkvT[:, 3, sp * 512:(sp + 1) * 512],
                                start=False, stop=True)
            tok_k = kb.op("tensor", mmk)
            kb.wait("vector", tok_k)
            ev_k = kb.op("vector", lambda E, sp=sp, pp=pp: E.tensor_copy(
                knT[:, sp * 512:(sp + 1) * 512], ps_mm[:, pp, :]))
            kb._l0_bank[("mm", pp)] = [ev_k]
            pp2 = sp % 2
            kb.wait("tensor", kb._l0_bank.get(("pv", pp2)))
            tok_kr = kb.op("tensor", lambda E, h=h, sp=sp, pp2=pp2: E.matmul(
                ps_pv[0:64, pp2, :], rt[:, h, :],
                kropeT[:, sp * 512:(sp + 1) * 512], start=True, stop=True))
            kb.wait("vector", tok_kr)
            ev_kr = kb.op("vector", lambda E, sp=sp, pp2=pp2: E.tensor_copy(
                krh[:, sp * 512:(sp + 1) * 512], ps_pv[0:64, pp2, :]))
            kb._l0_bank[("pv", pp2)] = [ev_kr]
        for tt in range(32):
            kb.wait("tensor", kb._l0_bank.get(("x1", 0)))
            def mmv(E, h=h, tt=tt):
                for j in range(3):
                    E.matmul(ps_x1[:, 0, 0:128], nkvT[:, j, tt * 128:(tt + 1) * 128],
                             wdv_t[:, j, h * 128:(h + 1) * 128],
                             start=(j == 0), stop=False)
                return E.matmul(ps_x1[:, 0, 0:128], nkvT[:, 3, tt * 128:(tt + 1) * 128],
                                wdv_t[:, 3, h * 128:(h + 1) * 128],
                                start=False, stop=True)
            tok_v = kb.op("tensor", mmv)
            kb.wait("vector", tok_v)
            ev_v = kb.op("vector", lambda E, tt=tt: E.tensor_scalar_mul(
                v_tm[:, tt, :], ps_x1[:, 0, 0:128], VSCALE))
            kb._l0_bank[("x1", 0)] = [ev_v]

        tok_exp_prev = None
        tok_n_prev = None
        tok_tp_prev = None
        for qt in range(n_qt):
            klen = (qt + 1) * 128
            nch = (klen + 511) // 512
            if tok_exp_prev is not None:
                kb.wait("vector", tok_exp_prev)
            for c in range(nch):
                w = min(512, klen - c * 512)
                pp = c % 2
                kb.wait("tensor", kb._l0_bank.get(("mm", pp)))
                def mmsc(E, h=h, qt=qt, c=c, w=w, pp=pp):
                    E.matmul(ps_mm[:, pp, 0:w],
                             qnT[:, qt * 128:(qt + 1) * 128],
                             knT[:, c * 512:c * 512 + w], start=True, stop=False)
                    return E.matmul(ps_mm[:, pp, 0:w],
                                    qrT[0:64, qt * 128:(qt + 1) * 128],
                                    krh[0:64, c * 512:c * 512 + w],
                                    start=False, stop=True)
                tok_s = kb.op("tensor", mmsc)
                kb.wait("vector", tok_s)
                if c == nch - 1:
                    if w > 128:
                        kb.op("vector", lambda E, c=c, w=w, pp=pp: E.tensor_copy(
                            sc[:, c * 512:c * 512 + w - 128], ps_mm[:, pp, 0:w - 128]))
                    kb.op("vector", lambda E, c=c, w=w, pp=pp: E.tensor_add(
                        sc[:, c * 512 + w - 128:c * 512 + w],
                        ps_mm[:, pp, w - 128:w], cmask[:, :]))
                else:
                    kb.op("vector", lambda E, c=c, w=w, pp=pp: E.tensor_copy(
                        sc[:, c * 512:c * 512 + w], ps_mm[:, pp, 0:w]))
                ev_s = kb.op("vector", lambda E, c=c, w=w: E.reduce_max(
                    cmax[:, c:c + 1], sc[:, c * 512:c * 512 + w], AX.X))
                kb._l0_bank[("mm", pp)] = [ev_s]
            kb.op("vector", lambda E, nch=nch: E.reduce_max(
                gmax[:, 0:1], cmax[:, 0:nch], AX.X))
            tok_nm = kb.op("vector", lambda E: E.tensor_scalar_mul(
                negmax[:, 0:1], gmax[:, 0:1], -1.0))
            kb.wait("scalar", tok_nm)
            kb.wait("scalar", tok_n_prev)
            tok_exp = kb.op("scalar", lambda E, klen=klen: E.activation(
                probs[:, 0:klen], sc[:, 0:klen], AF.Exp,
                bias=negmax[:, 0:1], accum_out=sums[:, 0:1]))
            tok_exp_prev = tok_exp
            kb.wait("vector", tok_exp)
            kb.op("vector", lambda E: E.reciprocal(rsum[:, 0:1], sums[:, 0:1]))
            kb.wait("vector", tok_tp_prev)
            tok_n = kb.op("vector", lambda E, klen=klen: E.tensor_scalar_mul(
                probs2[:, 0:klen], probs[:, 0:klen], rsum[:, 0:1]))
            tok_n_prev = tok_n
            for kt in range(qt + 1):
                ppt = kt % 2
                kb.wait("tensor", kb._l0_bank.get(("tp", ppt)))
                kb.wait("tensor", tok_n)
                tok_t = kb.op("tensor", lambda E, kt=kt, ppt=ppt: E.transpose(
                    ps_tp[:, ppt, 0:128], probs2[:, kt * 128:(kt + 1) * 128],
                    ident[:, :]))
                tok_tp_prev = tok_t
                kb.wait("vector", tok_t)
                ev_t = kb.op("vector", lambda E, kt=kt, ppt=ppt: E.tensor_copy(
                    pT[:, kt, :], ps_tp[:, ppt, 0:128]))
                kb._l0_bank[("tp", ppt)] = [ev_t]
            ppv = qt % 2
            kb.wait("tensor", kb._l0_bank.get(("pv", ppv)))
            kb.wait("tensor", kb.eauto("vector").tok())
            def mmpv(E, qt=qt, ppv=ppv):
                r = None
                for kt in range(qt + 1):
                    r = E.matmul(ps_pv[:, ppv, 0:128], v_tm[:, kt, :],
                                 pT[:, kt, :], start=(kt == 0), stop=(kt == qt))
                return r
            tok_pv = kb.op("tensor", mmpv)
            kb.wait("vector", tok_pv)
            ev_pv = kb.op("vector", lambda E, qt=qt, ppv=ppv: E.tensor_copy(
                attn_h[:, qt * 128:(qt + 1) * 128], ps_pv[:, ppv, 0:128]))
            kb._l0_bank[("pv", ppv)] = [ev_pv]
        kb.wait("sync", kb.eauto("vector").tok())
        kb.dma("sync", attn_out[h * 128:(h + 1) * 128, 0:n_qt * 128], attn_h[:, 0:n_qt * 128], s_st)
        kb._l0_bank[("attn_h",)] = s_st.tok()

    kb._l0_bank = {}
    for h in range(n_heads):
        prev_st = kb._l0_bank.get(("attn_h",))
        if prev_st is not None:
            kb.wait("vector", prev_st)
        head_attention(h)
    kb.wait("sync", s_st.tok())
    return kb



def build_post(name, with_proj, x_token_major, final=False, gating=False,
               n_parts=4, kb=None, io=None):
    own = kb is None
    if own:
        kb = LB(name)
        io = {}
    if with_proj:
        in_sl_d = io.get("inT_sl") if "inT_sl" in io else \
            kb.din("inT_sl", [D, T_SL], BF16)
        w_in = io.get("w") if "w" in io else kb.din("w", [D, D])
    else:
        parts = io.get("parts") if "parts" in io else \
            kb.din("parts", [n_parts, D, T_SL])
    if gating:
        gw_in = io.get("gate_w") if "gate_w" in io else kb.din("gate_w", [8, D])
        a2a_comb = io.get("a2a_combine")
        if a2a_comb is None:
            comb_out = io.get("combine") if "combine" in io else \
                kb.dout("combine", [T_SL, 8])
    if x_token_major:
        x_in = io.get("x_sl") if "x_sl" in io else kb.din("x_sl", [T_SL, D])
    else:
        x_in = io.get("xT_prev") if "xT_prev" in io else \
            kb.din("xT_prev", [D, T_SL])
    if final:
        out_f = io.get("x_out") if "x_out" in io else kb.dout("x_out", [T_SL, D])
    else:
        out_f = io.get("xT_new") if "xT_new" in io else \
            kb.dout("xT_new", [D, T_SL])
        out_b = io.get("xT_new_bf") if "xT_new_bf" in io else \
            kb.dout("xT_new_bf", [D, T_SL], BF16)

    s_ldw = kb.sem("s_ldw")
    s_lin = kb.sem("s_lin")
    s_st = kb.sem("s_st")
    s_xs = [kb.sem("s_xs0"), kb.sem("s_xs1")]
    C = make_consts(kb)
    ident, ones128 = C["ident"], C["ones128"]

    if with_proj:
        w_t = kb.sbuf("w_t", [128, DC, D], BF16)
        in_sl = kb.sbuf("in_sl", [128, DC, T_SL], BF16)
    else:
        parts_sb = kb.sbuf("parts_sb", [128, 2, 4, 512],
                           BF16 if (io and "parts" in io) else F32)
    x_sb = kb.sbuf("x_sb", [128, 2, 512], F32)
    if x_token_major:
        xtm_s = kb.sbuf("xtm_s", [128, 2, 4, 128], F32)
        identf = kb.sbuf("identf", [128, 128], F32)
        t0 = kb.op("gpsimd", lambda E: E.memset(identf[:, :], 0.0))
        kb.wait("gpsimd", t0)
        kb.op("gpsimd", lambda E: E.affine_select(
            out=identf[:, :], in_=identf[:, :], compare_op=ALU.not_equal,
            fill=1.0, base=0, pattern=[[-1, 128]], channel_multiplier=1))
        ps_tpx = kb.psum("ps_tpx", [128, 2, 512], F32)
    if final:
        identf = kb.sbuf("identf", [128, 128], F32)
        t0 = kb.op("gpsimd", lambda E: E.memset(identf[:, :], 0.0))
        kb.wait("gpsimd", t0)
        kb.op("gpsimd", lambda E: E.affine_select(
            out=identf[:, :], in_=identf[:, :], compare_op=ALU.not_equal,
            fill=1.0, base=0, pattern=[[-1, 128]], channel_multiplier=1))
        ps_tpf = kb.psum("ps_tpf", [128, 2, 512], F32)
        otm = kb.sbuf("otm", [128, 4, 128], F32)
    consts_tok = kb.eauto("gpsimd").tok()
    r_f = kb.sbuf("r_f", [128, DC, 512], F32)
    r_bf = kb.sbuf("r_bf", [128, DC, 512], BF16)
    sq = kb.sbuf("sq2", [128, DC, 512], BF16)
    mu = kb.sbuf("mu", [128, 512], F32)
    ms = kb.sbuf("ms", [128, 512], F32)
    var = kb.sbuf("var", [128, 512], F32)
    rstd = kb.sbuf("rstd", [128, 512], F32)
    scr = kb.sbuf("scr", [128, 512], F32)
    obf = kb.sbuf("obf", [128, 512], BF16)
    ps_mm = kb.psum("ps_mm2", [128, 2, 512], F32)
    ps_st = kb.psum("ps_st2", [128, 2, 512], F32)

    if gating:
        gw_sb = kb.sbuf("gw_sb", [8, D], F32)
        gwT = kb.sbuf("gwT", [128, DC, 8], F32)
        identg = kb.sbuf("identg", [128, 128], F32)
        tg = kb.op("gpsimd", lambda E: E.memset(identg[:, :], 0.0))
        kb.wait("gpsimd", tg)
        kb.op("gpsimd", lambda E: E.affine_select(
            out=identg[:, :], in_=identg[:, :], compare_op=ALU.not_equal,
            fill=1.0, base=0, pattern=[[-1, 128]], channel_multiplier=1))
        gconsts = kb.eauto("gpsimd").tok()
        s_gw = kb.sem("s_gw")
        kb.dma("gpsimd", gw_sb[:, :], gw_in[:, :], s_gw)
        lg = kb.sbuf("lg", [128, 8], F32)
        m1 = kb.sbuf("m1", [128, 1], F32)
        m2 = kb.sbuf("m2", [128, 1], F32)
        eqm = kb.sbuf("eqm", [128, 8], F32)
        l2t = kb.sbuf("l2t", [128, 8], F32)
        selt = kb.sbuf("selt", [128, 8], F32)
        negm1 = kb.sbuf("negm1", [128, 1], F32)
        expt = kb.sbuf("expt", [128, 8], F32)
        sumt = kb.sbuf("sumt", [128, 1], F32)
        rst = kb.sbuf("rst", [128, 1], F32)
        comb_sb = kb.sbuf("comb_sb", [128, 8], F32)
        ps_g = kb.psum("ps_g", [128, 1, 512], F32)
        kb.wait("tensor", [s_gw.tok(), gconsts])
        for k in range(DC):
            tok_t = kb.op("tensor", lambda E, k=k: E.transpose(
                ps_g[:, 0, 0:8], gw_sb[0:8, k * 128:(k + 1) * 128],
                identg[0:8, 0:8]))
            kb.wait("vector", tok_t)
            ev = kb.op("vector", lambda E, k=k: E.tensor_copy(
                gwT[:, k, :], ps_g[:, 0, 0:8]))
            kb.wait("tensor", ev)
        gwT_tok = kb.eauto("vector").tok()

        def emit_gating(si, u, r_f, bank, r_f_done):
            kb.wait("vector", s_st.tok())
            kb.wait("tensor", r_f_done)
            kb.wait("tensor", [gwT_tok, bank.get(("gt",))])
            def mmg(E, si=si, u=u):
                for k in range(DC - 1):
                    E.matmul(ps_g[:, 0, 0:8], r_f[:, k, u * 128:(u + 1) * 128],
                             gwT[:, k, :], start=(k == 0), stop=False)
                return E.matmul(ps_g[:, 0, 0:8],
                                r_f[:, DC - 1, u * 128:(u + 1) * 128],
                                gwT[:, DC - 1, :], start=False, stop=True)
            tok_g = kb.op("tensor", mmg)
            kb.wait("vector", tok_g)
            kb.op("vector", lambda E: E.tensor_copy(lg[:, :], ps_g[:, 0, 0:8]))
            kb.op("vector", lambda E: E.reduce_max(m1[:, 0:1], lg[:, :], AX.X))
            kb.op("vector", lambda E: E.tensor_single_scalar(
                eqm[:, :], lg[:, :], m1[:, 0:1], ALU.is_equal))
            kb.op("vector", lambda E: E.scalar_tensor_tensor(
                l2t[:, :], eqm[:, :], -1e30, lg[:, :], ALU.mult, ALU.add))
            kb.op("vector", lambda E: E.reduce_max(m2[:, 0:1], l2t[:, :], AX.X))
            kb.op("vector", lambda E: E.tensor_single_scalar(
                selt[:, :], lg[:, :], m2[:, 0:1], ALU.is_ge))
            tok_v = kb.op("vector", lambda E: E.tensor_scalar_mul(
                negm1[:, 0:1], m1[:, 0:1], -1.0))
            kb.wait("scalar", tok_v)
            tok_e = kb.op("scalar", lambda E: E.activation(
                expt[:, :], lg[:, :], AF.Exp, bias=negm1[:, 0:1]))
            kb.wait("vector", tok_e)
            kb.op("vector", lambda E: E.tensor_mul(l2t[:, :], expt[:, :],
                                                   selt[:, :]))
            kb.op("vector", lambda E: E.reduce_sum(sumt[:, 0:1], l2t[:, :], AX.X))
            kb.op("vector", lambda E: E.reciprocal(rst[:, 0:1], sumt[:, 0:1]))
            tok_c = kb.op("vector", lambda E: E.tensor_scalar_mul(
                comb_sb[:, :], l2t[:, :], rst[:, 0:1]))
            bank[("gt",)] = [tok_g, tok_c]
            kb.wait("sync", tok_c)
            if a2a_comb is not None:
                for jj in range(4):
                    kb.dma("sync",
                           a2a_comb[1024 * jj + si * 512 + u * 128:
                                    1024 * jj + si * 512 + (u + 1) * 128, :],
                           comb_sb[:, 2 * jj:2 * jj + 2], s_st)
            else:
                kb.dma("sync",
                       comb_out[si * 512 + u * 128:si * 512 + (u + 1) * 128, :],
                       comb_sb[:, :], s_st)

    if with_proj:
        load_wT(kb, w_t, w_in, D, s_ldw)
        for k in range(DC):
            kb.dma("gpsimd", in_sl[:, k, :], in_sl_d[k * 128:(k + 1) * 128, :],
                   s_lin)
    ldw_tok = s_ldw.tok()
    lin_tok = s_lin.tok()

    bank = {}
    x_used = {}
    for si in range(2):
        for j in range(DC):
            slot = (si * DC + j) % 2
            prev = x_used.get(slot)
            if prev is not None:
                kb.wait("sync", prev)
            if x_token_major:
                for u in range(4):
                    kb.dma("sync", xtm_s[:, slot, u, :],
                           x_in[si * 512 + u * 128:si * 512 + (u + 1) * 128,
                                j * 128:(j + 1) * 128], s_xs[slot])
            else:
                kb.dma("sync", x_sb[:, slot, 0:512],
                       x_in[j * 128:(j + 1) * 128, si * 512:(si + 1) * 512],
                       s_xs[slot])
            if not with_proj:
                for p4 in range(n_parts):
                    src_ap = parts[p4, j * 128:(j + 1) * 128,
                                   si * 512:(si + 1) * 512] \
                        if len(parts.shape) == 3 else \
                        parts[j * 128:(j + 1) * 128, si * 512:(si + 1) * 512]
                    kb.dma("sync", parts_sb[:, slot, p4, :], src_ap, s_xs[slot])
            if x_token_major:
                kb.wait("tensor", s_xs[slot].tok())
                kb.wait("tensor", consts_tok)
                for u in range(4):
                    ppt = u % 2
                    kb.wait("tensor", bank.get(("tpx", ppt)))
                    tok_t = kb.op("tensor", lambda E, slot=slot, u=u, ppt=ppt:
                                  E.transpose(ps_tpx[:, ppt, 0:128],
                                              xtm_s[:, slot, u, :], identf[:, :]))
                    kb.wait("vector", tok_t)
                    ev = kb.op("vector", lambda E, slot=slot, u=u, ppt=ppt:
                               E.tensor_copy(x_sb[:, slot, u * 128:(u + 1) * 128],
                                             ps_tpx[:, ppt, 0:128]))
                    bank[("tpx", ppt)] = [ev]
                x_ready = kb.eauto("vector").tok()
            else:
                kb.wait("vector", s_xs[slot].tok())
                x_ready = None
            if with_proj:
                pp = (si * DC + j) % 2
                kb.wait("tensor", [lin_tok, ldw_tok])
                kb.wait("tensor", bank.get(("mm", pp)))
                def mmp(E, j=j, si=si, pp=pp):
                    for k in range(DC - 1):
                        E.matmul(ps_mm[:, pp, :], w_t[:, k, j * 128:(j + 1) * 128],
                                 in_sl[:, k, si * 512:(si + 1) * 512],
                                 start=(k == 0), stop=False)
                    return E.matmul(ps_mm[:, pp, :],
                                    w_t[:, DC - 1, j * 128:(j + 1) * 128],
                                    in_sl[:, DC - 1, si * 512:(si + 1) * 512],
                                    start=False, stop=True)
                tok_mm = kb.op("tensor", mmp)
                kb.wait("vector", [tok_mm, x_ready])
                tok_r = kb.op("vector", lambda E, j=j, slot=slot, pp=pp:
                              E.tensor_add(r_f[:, j, :], ps_mm[:, pp, :],
                                           x_sb[:, slot, 0:512]))
                bank[("mm", pp)] = [tok_r]
            else:
                kb.wait("vector", s_xs[slot].tok())
                if x_ready is not None:
                    kb.wait("vector", x_ready)
                if n_parts == 1:
                    tok_r = kb.op("vector", lambda E, j=j, slot=slot: E.tensor_add(
                        r_f[:, j, :], x_sb[:, slot, 0:512],
                        parts_sb[:, slot, 0, :]))
                else:
                    kb.op("vector", lambda E, slot=slot: E.tensor_add(
                        scr[:, :], x_sb[:, slot, 0:512], parts_sb[:, slot, 0, :]))
                    kb.op("vector", lambda E, j=j, slot=slot: E.tensor_add(
                        r_f[:, j, :], scr[:, :], parts_sb[:, slot, 1, :]))
                    kb.op("vector", lambda E, j=j, slot=slot: E.tensor_add(
                        scr[:, :], r_f[:, j, :], parts_sb[:, slot, 2, :]))
                    tok_r = kb.op("vector", lambda E, j=j, slot=slot: E.tensor_add(
                        r_f[:, j, :], scr[:, :], parts_sb[:, slot, 3, :]))
            x_used[slot] = tok_r
            kb.op("vector", lambda E, j=j: E.tensor_copy(r_bf[:, j, :],
                                                         r_f[:, j, :]))
            kb.wait("scalar", kb.eauto("vector").tok())
            kb.op("scalar", lambda E, j=j: E.activation(sq[:, j, :], r_f[:, j, :],
                                                        AF.Square))
        sq_tok = kb.eauto("scalar").tok()
        rbf_tok = kb.eauto("vector").tok()
        kb.wait("tensor", [sq_tok, rbf_tok])
        def statsA(E):
            for j in range(DC - 1):
                E.matmul(ps_st[:, 0, :], ones128[:, :], r_bf[:, j, :],
                         start=(j == 0), stop=False)
            return E.matmul(ps_st[:, 0, :], ones128[:, :], r_bf[:, DC - 1, :],
                            start=False, stop=True)
        tokA = kb.op("tensor", statsA)
        def statsB(E):
            for j in range(DC - 1):
                E.matmul(ps_st[:, 1, :], ones128[:, :], sq[:, j, :],
                         start=(j == 0), stop=False)
            return E.matmul(ps_st[:, 1, :], ones128[:, :], sq[:, DC - 1, :],
                            start=False, stop=True)
        tokB = kb.op("tensor", statsB)
        kb.wait("vector", tokA)
        kb.op("vector", lambda E: E.tensor_scalar_mul(mu[:, :], ps_st[:, 0, :],
                                                      1.0 / D))
        kb.wait("vector", tokB)
        tok_ms = kb.op("vector", lambda E: E.tensor_scalar_mul(
            ms[:, :], ps_st[:, 1, :], 1.0 / D))
        kb.op("vector", lambda E: E.tensor_mul(scr[:, :], mu[:, :], mu[:, :]))
        tok_var = kb.op("vector", lambda E: E.tensor_sub(var[:, :], ms[:, :],
                                                         scr[:, :]))
        bank[("st",)] = [tok_ms]
        kb.wait("scalar", tok_var)
        kb.op("scalar", lambda E: E.activation(ms[:, :], var[:, :], AF.Ln,
                                               bias=C["eps"][:, 0:1]))
        tok_rstd = kb.op("scalar", lambda E: E.activation(
            rstd[:, :], ms[:, :], AF.Exp, scale=-0.5))
        kb.wait("vector", tok_rstd)
        for j in range(DC):
            kb.op("vector", lambda E, j=j: E.tensor_sub(scr[:, :], r_f[:, j, :],
                                                        mu[:, :]))
            tok_o = kb.op("vector", lambda E, j=j: E.tensor_mul(
                r_f[:, j, :], scr[:, :], rstd[:, :]))
            if final:
                for u in range(4):
                    ppt = u % 2
                    kb.wait("tensor", bank.get(("tpo", ppt)))
                    kb.wait("tensor", [tok_o, consts_tok])
                    tok_t = kb.op("tensor", lambda E, j=j, u=u, ppt=ppt:
                                  E.transpose(ps_tpf[:, ppt, 0:128],
                                              r_f[:, j, u * 128:(u + 1) * 128],
                                              identf[:, :]))
                    kb.wait("vector", tok_t)
                    ev = kb.op("vector", lambda E, j=j, u=u, ppt=ppt:
                               E.tensor_copy(otm[:, u, :], ps_tpf[:, ppt, 0:128]))
                    bank[("tpo", ppt)] = [ev]
                kb.wait("sync", kb.eauto("vector").tok())
                for u in range(4):
                    kb.dma("sync",
                           out_f[si * 512 + u * 128:si * 512 + (u + 1) * 128,
                                 j * 128:(j + 1) * 128], otm[:, u, :], s_st)
                kb.wait("vector", s_st.tok())
            else:
                tok_ob = kb.op("vector", lambda E, j=j: E.tensor_copy(
                    obf[:, :], r_f[:, j, :]))
                kb.wait("sync", tok_ob)
                kb.dma("sync", out_f[j * 128:(j + 1) * 128,
                                     si * 512:(si + 1) * 512], r_f[:, j, :], s_st)
                kb.dma("sync", out_b[j * 128:(j + 1) * 128,
                                     si * 512:(si + 1) * 512], obf[:, :], s_st)
                kb.wait("vector", s_st.tok())
        if gating:
            r_f_done = kb.eauto("vector").tok()
            for u in range(4):
                emit_gating(si, u, r_f, bank, r_f_done)
            kb.wait("vector", bank.get(("gt",)))
        kb.wait("tensor", bank.get(("st",)))
        kb.wait("vector", s_st.tok())
    kb.wait("sync", s_st.tok())
    return kb



def build_moe(n_spans=8, kb=None, io=None):
    own = kb is None
    if own:
        kb = LB("moe")
        io = {}
    x_in = _v3(io["xT_bf_full"] if "xT_bf_full" in io else
               kb.din("xT_bf_full", [D, S], BF16))
    w1_in = io.get("w1_sl") if "w1_sl" in io else kb.din("w1_sl", [2, 2816, D])
    w2_in = io.get("w2_sl") if "w2_sl" in io else kb.din("w2_sl", [2, D, 1408])
    comb_in = io.get("combine_sl") if "combine_sl" in io else \
        kb.din("combine_sl", [S, 8])
    esel_in = io.get("esel") if "esel" in io else kb.din("esel", [8, 2])
    out_p = _v3(io["moeP"] if "moeP" in io else kb.dout("moeP", [D, S]))

    s_cast = kb.sem("s_cast")
    s_w1 = kb.sem("s_w1")
    s_w2 = kb.sem("s_w2")
    s_x = [kb.sem("s_x0"), kb.sem("s_x1")]
    s_cb = kb.sem("s_cb")
    s_st = kb.sem("s_st")
    C = make_consts(kb)
    ident, ones_row = C["ident"], C["ones_row"]

    identf = kb.sbuf("identf_m", [128, 128], F32)
    tg = kb.op("gpsimd", lambda E: E.memset(identf[:, :], 0.0))
    kb.wait("gpsimd", tg)
    kb.op("gpsimd", lambda E: E.affine_select(
        out=identf[:, :], in_=identf[:, :], compare_op=ALU.not_equal,
        fill=1.0, base=0, pattern=[[-1, 128]], channel_multiplier=1))
    identf_tok = kb.eauto("gpsimd").tok()

    w1_bf = kb.dram("w1_bf", [2, 2816, D], BF16)
    w2_bf = kb.dram("w2_bf", [2, D, 1408], BF16)

    xs_sb = kb.sbuf("xs_sb", [128, 2, DC, 512], BF16)
    w1T = kb.sbuf("w1T", [128, DC, 1408], BF16)
    w2T = kb.sbuf("w2T", [128, 2, 11, D], BF16)
    sg = kb.sbuf("sg", [128, 11, 512], BF16)
    act = kb.sbuf("act", [128, 2, 11, 512], BF16)
    cb_sb = kb.sbuf("cb_sb", [128, 4, 8], F32)
    cbT_bf = kb.sbuf("cbT_bf", [8, 128], BF16)
    esel_bf = kb.sbuf("esel_bf", [8, 2], BF16)
    ce_row = kb.sbuf("ce_row", [1, 2, 512], BF16)
    cbc = kb.sbuf("cbc", [128, 2, 512], BF16)
    u_tmp = kb.sbuf("u_tmp", [128, 512], BF16)
    stg = kb.sbuf("stg", [128, 2, 512],
                  BF16 if (io and "moeP" in io) else F32)
    ps_gu = kb.psum("ps_gu", [128, 2, 512], F32)
    ps_dn = kb.psum("ps_dn", [128, 2, 512], F32)
    ps_cb = kb.psum("ps_cb", [128, 1, 512], F32)

    s_es = kb.sem("s_es")
    kb.dma("gpsimd", esel_bf[:, :], esel_in[:, :], s_es)
    esel_tok = s_es.tok()
    for e in range(2):
        for o4 in range(4):
            kb.dma("gpsimd", w1_bf[e, o4 * 704:(o4 + 1) * 704, :],
                   w1_in[e, o4 * 704:(o4 + 1) * 704, :], s_cast)
        for o4 in range(4):
            kb.dma("gpsimd", w2_bf[e, o4 * 512:(o4 + 1) * 512, :],
                   w2_in[e, o4 * 512:(o4 + 1) * 512, :], s_cast)
    cast_tok = s_cast.tok()
    kb.wait("sync", cast_tok)
    for e in range(2):
        for i in range(11):
            kb.dma("sync", w2T[:, e, i, :],
                   w2_bf[e, :, i * 128:(i + 1) * 128], s_w2, transpose=True)
    w2_tok = s_w2.tok()

    bank = {}
    st_idx = 0
    for sp in range(n_spans):
        slot = sp % 2
        prev = bank.get(("xuse", slot))
        if prev is not None:
            kb.wait("sync", prev)
        for k in range(DC):
            kb.dma("sync", xs_sb[:, slot, k, :],
                   x_in[k * 128:(k + 1) * 128, sp // 2,
                        (sp % 2) * 512:(sp % 2) * 512 + 512], s_x[slot])
        x_tok = s_x[slot].tok()
        kb.wait("sync", bank.get(("cbuse",)))
        kb.dma("sync", cb_sb[:, :, :],
               comb_in[sp * 512:(sp + 1) * 512, :].rearrange(
                   "(u p) e -> p u e", p=128), s_cb)
        cb_tok = s_cb.tok()
        kb.wait("tensor", [cb_tok, C["tok"], identf_tok, esel_tok])
        for u in range(4):
            tok_t = kb.op("tensor", lambda E, u=u: E.transpose(
                ps_cb[0:8, 0, 0:128], cb_sb[:, u, :], identf[:, :]))
            kb.wait("vector", tok_t)
            ev = kb.op("vector", lambda E, u=u: E.tensor_copy(
                cbT_bf[:, :], ps_cb[0:8, 0, 0:128]))
            kb.wait("tensor", ev)
            for e in range(2):
                tok_s = kb.op("tensor", lambda E, e=e: E.matmul(
                    ps_cb[0:1, 0, 0:128], esel_bf[:, e:e + 1], cbT_bf[:, :],
                    start=True, stop=True))
                kb.wait("vector", tok_s)
                ev2 = kb.op("vector", lambda E, e=e, u=u: E.tensor_copy(
                    ce_row[0:1, e, u * 128:(u + 1) * 128], ps_cb[0:1, 0, 0:128]))
                kb.wait("tensor", ev2)
        row_tok = kb.eauto("vector").tok()
        kb.wait("tensor", row_tok)
        for e in range(2):
            tok_b = kb.op("tensor", lambda E, e=e: E.matmul(
                ps_cb[:, 0, :], ones_row[:, :], ce_row[0:1, e, :],
                start=True, stop=True))
            kb.wait("vector", tok_b)
            evb = kb.op("vector", lambda E, e=e: E.tensor_copy(
                cbc[:, e, :], ps_cb[:, 0, :]))
            kb.wait("tensor", evb)
        cbc_tok = kb.eauto("vector").tok()
        bank[("cbuse",)] = cbc_tok

        for e in range(2):
            for half in range(2):
                kb.wait("sync", bank.get(("w1use",)))
                for k in range(DC):
                    kb.dma("sync", w1T[:, k, :],
                           w1_bf[e, half * 1408:(half + 1) * 1408,
                                 k * 128:(k + 1) * 128], s_w1, transpose=True)
                w1_tok = s_w1.tok()
                if half == 1:
                    kb.wait("vector", kb.eauto("scalar").tok())
                kb.wait("tensor", [w1_tok, x_tok, cbc_tok])
                for j in range(11):
                    pp = j % 2
                    kb.wait("tensor", bank.get(("gu", pp)))
                    def mmgu(E, j=j, pp=pp, slot=slot):
                        for k in range(DC - 1):
                            E.matmul(ps_gu[:, pp, :],
                                     w1T[:, k, j * 128:(j + 1) * 128],
                                     xs_sb[:, slot, k, :],
                                     start=(k == 0), stop=False)
                        return E.matmul(ps_gu[:, pp, :],
                                        w1T[:, DC - 1, j * 128:(j + 1) * 128],
                                        xs_sb[:, slot, DC - 1, :],
                                        start=False, stop=True)
                    tok_mm = kb.op("tensor", mmgu)
                    if half == 0:
                        kb.wait("scalar", tok_mm)
                        ev = kb.op("scalar", lambda E, j=j, pp=pp: E.activation(
                            sg[:, j, :], ps_gu[:, pp, :], AF.Silu))
                        bank[("gu", pp)] = [ev]
                    else:
                        kb.wait("vector", tok_mm)
                        kb.op("vector", lambda E, j=j, pp=pp:
                              E.tensor_mul(u_tmp[:, :], sg[:, j, :],
                                           ps_gu[:, pp, :]))
                        ev = kb.op("vector", lambda E, j=j, e=e: E.tensor_mul(
                            act[:, e, j, :], u_tmp[:, :], cbc[:, e, :]))
                        bank[("gu", pp)] = [ev]
                w1use = kb.eauto("scalar" if half == 0 else "vector").tok()
                bank[("w1use",)] = [w1use]
        act_tok = kb.eauto("vector").tok()
        sg_tok = kb.eauto("scalar").tok()

        kb.wait("tensor", [act_tok, w2_tok])
        for hc in range(DC):
            pp = hc % 2
            kb.wait("tensor", bank.get(("dn", pp)))
            def mmdn(E, hc=hc, pp=pp):
                r = None
                for e in range(2):
                    for i in range(11):
                        r = E.matmul(ps_dn[:, pp, :],
                                     w2T[:, e, i, hc * 128:(hc + 1) * 128],
                                     act[:, e, i, :],
                                     start=(e == 0 and i == 0),
                                     stop=(e == 1 and i == 10))
                return r
            tok_dn = kb.op("tensor", mmdn)
            kb.wait("vector", tok_dn)
            kb.wait("vector", s_st.tok())
            ev = kb.op("vector", lambda E, hc=hc, pp=pp: E.tensor_copy(
                stg[:, pp, :], ps_dn[:, pp, :]))
            bank[("dn", pp)] = [ev]
            kb.wait("sync", ev)
            kb.dma("sync", out_p[hc * 128:(hc + 1) * 128, sp // 2,
                                 (sp % 2) * 512:(sp % 2) * 512 + 512],
                   stg[:, pp, :], s_st)
        bank[("xuse", slot)] = kb.eauto("tensor").tok()
    kb.wait("sync", s_st.tok())
    return kb



def build_l1ffn(n_spans=8, kb=None, io=None):
    own = kb is None
    if own:
        kb = LB("l1ffn")
        io = {}
    x_in = _v3(io["xT_bf_full"] if "xT_bf_full" in io else
               kb.din("xT_bf_full", [D, S], BF16))
    wg_in = io.get("wg_sl") if "wg_sl" in io else kb.din("wg_sl", [D, D])
    wu_in = io.get("wu_sl") if "wu_sl" in io else kb.din("wu_sl", [D, D])
    wd_in = io.get("wd_sl") if "wd_sl" in io else kb.din("wd_sl", [D, D])
    out_p = _v3(io["ffnP"] if "ffnP" in io else kb.dout("ffnP", [D, S]))

    s_cast = kb.sem("s_cast")
    s_w = kb.sem("s_w")
    s_x = [kb.sem("s_x0"), kb.sem("s_x1")]
    s_st = kb.sem("s_st")
    C = make_consts(kb)

    wg_bf = kb.dram("wg_bf", [D, D], BF16)
    wu_bf = kb.dram("wu_bf", [D, D], BF16)
    wd_bf = kb.dram("wd_bf", [D, D], BF16)

    xs_sb = kb.sbuf("xs_sb", [128, 2, DC, 512], BF16)
    wT = kb.sbuf("wT", [128, 2, DC, D], BF16)
    sg = kb.sbuf("sg", [128, DC, 512], BF16)
    act = kb.sbuf("act", [128, DC, 512], BF16)
    u_tmp = kb.sbuf("u_tmp", [128, 512], BF16)
    stg = kb.sbuf("stg", [128, 2, 512],
                  BF16 if (io and "ffnP" in io) else F32)
    ps_mm = kb.psum("ps_mm", [128, 2, 512], F32)

    for w_in, w_bf in ((wg_in, wg_bf), (wu_in, wu_bf), (wd_in, wd_bf)):
        for o4 in range(4):
            kb.dma("gpsimd", w_bf[o4 * 512:(o4 + 1) * 512, :],
                   w_in[o4 * 512:(o4 + 1) * 512, :], s_cast)
    cast_tok = s_cast.tok()
    kb.wait("sync", cast_tok)

    bank = {}
    s_ws = [s_w, kb.sem("s_wb")]
    steps = [(sp, ph) for sp in range(n_spans) for ph in range(3)]
    wuse = {}

    def issue_wload(i):
        if i >= len(steps):
            return
        sp, ph = steps[i]
        ws = i % 2
        kb.wait("sync", wuse.get(ws))
        w_bf = (wg_bf, wu_bf, wd_bf)[ph]
        for k in range(DC):
            kb.dma("sync", wT[:, ws, k, :], w_bf[:, k * 128:(k + 1) * 128],
                   s_ws[ws], transpose=True)

    issue_wload(0)
    issue_wload(1)
    for i, (sp, phase) in enumerate(steps):
        slot = sp % 2
        ws = i % 2
        if phase == 0:
            prev = bank.get(("xuse", slot))
            if prev is not None:
                kb.wait("sync", prev)
            for k in range(DC):
                kb.dma("sync", xs_sb[:, slot, k, :],
                       x_in[k * 128:(k + 1) * 128, sp // 2,
                            (sp % 2) * 512:(sp % 2) * 512 + 512], s_x[slot])
        x_tok = s_x[slot].tok()
        kb.wait("tensor", [s_ws[ws].tok(), x_tok])
        if phase == 0:
            kb.wait("scalar", kb.eauto("vector").tok())
        if phase == 2:
            kb.wait("tensor", kb.eauto("vector").tok())
        for j in range(DC):
            pp = j % 2
            kb.wait("tensor", bank.get(("mm", pp)))
            if phase < 2:
                def mmgu(E, j=j, pp=pp, slot=slot, ws=ws):
                    for k in range(DC - 1):
                        E.matmul(ps_mm[:, pp, :],
                                 wT[:, ws, k, j * 128:(j + 1) * 128],
                                 xs_sb[:, slot, k, :],
                                 start=(k == 0), stop=False)
                    return E.matmul(ps_mm[:, pp, :],
                                    wT[:, ws, DC - 1, j * 128:(j + 1) * 128],
                                    xs_sb[:, slot, DC - 1, :],
                                    start=False, stop=True)
                tok_mm = kb.op("tensor", mmgu)
                if phase == 0:
                    kb.wait("scalar", tok_mm)
                    ev = kb.op("scalar", lambda E, j=j, pp=pp: E.activation(
                        sg[:, j, :], ps_mm[:, pp, :], AF.Silu))
                else:
                    kb.wait("vector", tok_mm)
                    ev = kb.op("vector", lambda E, j=j, pp=pp: E.tensor_mul(
                        act[:, j, :], sg[:, j, :], ps_mm[:, pp, :]))
                bank[("mm", pp)] = [ev]
            else:
                def mmdn(E, j=j, pp=pp, ws=ws):
                    for k in range(DC - 1):
                        E.matmul(ps_mm[:, pp, :],
                                 wT[:, ws, k, j * 128:(j + 1) * 128],
                                 act[:, k, :], start=(k == 0), stop=False)
                    return E.matmul(ps_mm[:, pp, :],
                                    wT[:, ws, DC - 1, j * 128:(j + 1) * 128],
                                    act[:, DC - 1, :],
                                    start=False, stop=True)
                tok_mm = kb.op("tensor", mmdn)
                kb.wait("vector", tok_mm)
                kb.wait("vector", s_st.tok())
                ev = kb.op("vector", lambda E, j=j, pp=pp: E.tensor_copy(
                    stg[:, pp, :], ps_mm[:, pp, :]))
                bank[("mm", pp)] = [ev]
                kb.wait("sync", ev)
                kb.dma("sync", out_p[j * 128:(j + 1) * 128, sp // 2,
                                     (sp % 2) * 512:(sp % 2) * 512 + 512],
                       stg[:, pp, :], s_st)
        if phase == 0:
            kb.wait("vector", kb.eauto("scalar").tok())
        wuse[ws] = kb.eauto("tensor").tok()
        issue_wload(i + 2)
        if phase == 2:
            bank[("xuse", slot)] = kb.eauto("tensor").tok()
    kb.wait("sync", s_st.tok())
    return kb



def build_l1attn(n_heads=4, n_chunks=32, n_spans=8, kb=None, io=None):
    own = kb is None
    if own:
        kb = LB("l1attn")
        io = {}
    x_in = _v3(io["xT_bf_full"] if "xT_bf_full" in io else
               kb.din("xT_bf_full", [D, S], BF16))
    wq_in = io.get("wq_sl") if "wq_sl" in io else kb.din("wq_sl", [D, 512])
    wk_in = io.get("wk_sl") if "wk_sl" in io else kb.din("wk_sl", [D, 512])
    wv_in = io.get("wv_sl") if "wv_sl" in io else kb.din("wv_sl", [D, 512])
    wg_in = io.get("wg_sl") if "wg_sl" in io else kb.din("wg_sl", [D, 512])
    wa_in = io.get("wa_sl") if "wa_sl" in io else kb.din("wa_sl", [D, 4])
    wb_in = io.get("wb_sl") if "wb_sl" in io else kb.din("wb_sl", [D, 4])
    ba_in = io.get("ba_sl") if "ba_sl" in io else kb.din("ba_sl", [4])
    bb_in = io.get("bb_sl") if "bb_sl" in io else kb.din("bb_sl", [4])
    qcw_in = io.get("qcw_sl") if "qcw_sl" in io else kb.din("qcw_sl", [512, 4])
    qcb_in = io.get("qcb_sl") if "qcb_sl" in io else kb.din("qcb_sl", [512])
    kcw_in = io.get("kcw_sl") if "kcw_sl" in io else kb.din("kcw_sl", [512, 4])
    kcb_in = io.get("kcb_sl") if "kcb_sl" in io else kb.din("kcb_sl", [512])
    vcw_in = io.get("vcw_sl") if "vcw_sl" in io else kb.din("vcw_sl", [512, 4])
    vcb_in = io.get("vcb_sl") if "vcb_sl" in io else kb.din("vcb_sl", [512])
    lng_in = io.get("ln_g") if "ln_g" in io else kb.din("ln_g", [128])
    lnb_in = io.get("ln_b") if "ln_b" in io else kb.din("ln_b", [128])
    og_out = io.get("ogT") if "ogT" in io else kb.dout("ogT", [512, S], BF16)

    s_ldw = kb.sem("s_ldw")
    s_wh = kb.sem("s_wh")
    s_x = [kb.sem("s_x0"), kb.sem("s_x1")]
    s_st = kb.sem("s_st")
    C = make_consts(kb)
    ident, ones128 = C["ident"], C["ones128"]

    utonesf = kb.sbuf("utonesf", [128, 128], F32)
    maskA = kb.sbuf("maskA", [128, 128], F32)
    maskB = kb.sbuf("maskB", [128, 128], F32)
    zerof = kb.sbuf("zerof", [128, 128], F32)
    onef = kb.sbuf("onef", [128, 1], F32)
    identf = kb.sbuf("identfL", [128, 128], F32)
    gcst = kb.sem("s_gcst")
    for tile, fill, op, val in (
        (utonesf, 0.0, ALU.is_gt, 1.0),
        (maskA, 0.0, ALU.is_gt, NEG),
        (maskB, NEG, ALU.is_gt, 0.0),
    ):
        t0 = kb.op("gpsimd", lambda E, t=tile, f=fill: E.memset(t[:, :], f), gcst)
        kb.wait("gpsimd", t0)
        kb.op("gpsimd", lambda E, t=tile, o=op, v=val: E.affine_select(
            out=t[:, :], in_=t[:, :], compare_op=o, fill=v,
            base=0, pattern=[[-1, 128]], channel_multiplier=1), gcst)
    kb.op("gpsimd", lambda E: E.memset(zerof[:, :], 0.0), gcst)
    kb.op("gpsimd", lambda E: E.memset(onef[:, :], 1.0), gcst)
    t0i = kb.op("gpsimd", lambda E: E.memset(identf[:, :], 0.0), gcst)
    kb.wait("gpsimd", t0i)
    kb.op("gpsimd", lambda E: E.affine_select(
        out=identf[:, :], in_=identf[:, :], compare_op=ALU.not_equal,
        fill=1.0, base=0, pattern=[[-1, 128]], channel_multiplier=1), gcst)
    gconst_tok = kb.eauto("gpsimd").tok()

    xs = kb.sbuf("xs", [128, 2, DC, 512], BF16)
    wq_t = kb.sbuf("wq_t", [128, DC, 128], BF16)
    wk_t = kb.sbuf("wk_t", [128, DC, 128], BF16)
    wv_t = kb.sbuf("wv_t", [128, DC, 128], BF16)
    wg_t = kb.sbuf("wg_t", [128, DC, 128], BF16)
    wa_t = kb.sbuf("wa_t", [128, DC, 4], BF16)
    wb_t = kb.sbuf("wb_t", [128, DC, 4], BF16)
    bacol = kb.sbuf("bacol", [4, 1], F32)
    bbcol = kb.sbuf("bbcol", [4, 1], F32)
    qcw = kb.sbuf("qcw", [128, 4, 4], F32)
    qcb = kb.sbuf("qcb", [128, 4], F32)
    kcw = kb.sbuf("kcw", [128, 4, 4], F32)
    kcb = kb.sbuf("kcb", [128, 4], F32)
    vcw = kb.sbuf("vcw", [128, 4, 4], F32)
    vcb = kb.sbuf("vcb", [128, 4], F32)
    lngc = kb.sbuf("lngc", [128, 1], F32)
    lnbc = kb.sbuf("lnbc", [128, 1], F32)
    zq = kb.sbuf("zq", [4, 512], F32)
    zb = kb.sbuf("zb", [4, 512], F32)
    zs1 = kb.sbuf("zs1", [4, 512], F32)
    zs2 = kb.sbuf("zs2", [4, 512], F32)
    la_sp = kb.sbuf("la_sp", [4, 512], F32)
    b_sp = kb.sbuf("b_sp", [4, 512], F32)
    labtm = kb.sbuf("labtm", [128, 32, 8], F32)
    gtm = kb.sbuf("gtm", [128, 32, 4], F32)
    rawq = kb.sbuf("rawq", [128, S + 3], BF16)
    rawk = kb.sbuf("rawk", [128, S + 3], BF16)
    rawv = kb.sbuf("rawv", [128, S + 3], BF16)
    cq = kb.sbuf("cq", [128, S], BF16)
    ck = kb.sbuf("ck", [128, S], BF16)
    cv_ = kb.sbuf("cv_", [128, S], BF16)
    gateh = kb.sbuf("gateh", [128, S], BF16)
    ogsb = kb.sbuf("ogsb", [128, 2, S], BF16)
    cvs = kb.sbuf("cvs", [128, 2, 512], F32)
    kss = kb.sbuf("kss", [128, 512], BF16)
    gbc_s = kb.sbuf("gbc_s", [128, 128], F32)
    escr = kb.sbuf("escr", [128, 128], F32)
    dn_bf = kb.sbuf("dn_bf", [128, 128], BF16)
    nt_bf = kb.sbuf("nt_bf", [128, 128], BF16)
    t1n = kb.sbuf("t1n", [128, 2, 128], BF16)
    t2n = kb.sbuf("t2n", [128, 2, 128], BF16)
    xsol = kb.sbuf("xsol", [128, 2, 128], BF16)
    dft_bf = kb.sbuf("dft_bf", [128, 128], BF16)
    sht_bf = kb.sbuf("sht_bf", [128, 128], BF16)
    vtm_bf = kb.sbuf("vtm_bf", [128, 128], BF16)
    ktm_bf = kb.sbuf("ktm_bf", [128, 128], BF16)
    qg_bf = kb.sbuf("qg_bf", [128, 128], BF16)
    gambc = kb.sbuf("gambc", [128, 128], BF16)
    uyl_bf = kb.sbuf("uyl_bf", [128, 128], BF16)
    r_bf = kb.sbuf("r_bf", [128, 128], BF16)
    scrR = kb.sbuf("scrR", [128, 128], F32)
    colv = kb.sbuf("colv", [128, 8], F32)
    labc = kb.sbuf("labc", [128, 128], F32)
    st_f = kb.sbuf("st_f", [128, 4, 2, 128], F32)
    st_bf = kb.sbuf("st_bf", [128, 128], BF16)
    o_s = kb.sbuf("o_s", [128, 128], F32)
    sqo = kb.sbuf("sqo", [128, 128], BF16)
    o_bf = kb.sbuf("o_bf", [128, 128], BF16)
    muo = kb.sbuf("muo", [128, 128], F32)
    mso = kb.sbuf("mso", [128, 128], F32)
    varo = kb.sbuf("varo", [128, 128], F32)
    rstdo = kb.sbuf("rstdo", [128, 128], F32)
    oscr = kb.sbuf("oscr", [128, 128], F32)
    ps_pj = kb.psum("ps_pj", [128, 2, 512], F32)
    ps_c = kb.psum("ps_c", [128, 1, 512], F32)
    ps_d = kb.psum("ps_d", [128, 1, 512], F32)
    ps_e = kb.psum("ps_e", [128, 1, 512], F32)
    ps_tp = kb.psum("ps_tpL", [128, 2, 1024], BF16)
    ps_x = kb.psum("ps_x", [128, 1, 512], F32)

    load_wT(kb, wa_t, wa_in, D, s_ldw)
    load_wT(kb, wb_t, wb_in, D, s_ldw)
    kb.dma("gpsimd", bacol[:, :], ba_in[:][:, None], s_ldw)
    kb.dma("gpsimd", bbcol[:, :], bb_in[:][:, None], s_ldw)
    for cw, cb, cwi, cbi in ((qcw, qcb, qcw_in, qcb_in),
                             (kcw, kcb, kcw_in, kcb_in),
                             (vcw, vcb, vcw_in, vcb_in)):
        for h in range(4):
            kb.dma("gpsimd", cw[:, h, :], cwi[h * 128:(h + 1) * 128, :], s_ldw)
            kb.dma("gpsimd", cb[:, h:h + 1], cbi[h * 128:(h + 1) * 128][:, None],
                   s_ldw)
    kb.dma("gpsimd", lngc[:, :], lng_in[:][:, None], s_ldw)
    kb.dma("gpsimd", lnbc[:, :], lnb_in[:][:, None], s_ldw)
    ldw_tok = s_ldw.tok()

    bank = {}

    for sp in range(n_spans):
        slot = sp % 2
        prev = bank.get(("xuse", slot))
        if prev is not None:
            kb.wait("sync", prev)
        for k in range(DC):
            kb.dma("sync", xs[:, slot, k, :],
                   x_in[k * 128:(k + 1) * 128, sp // 2,
                        (sp % 2) * 512:(sp % 2) * 512 + 512], s_x[slot])
        x_tok = s_x[slot].tok()
        kb.wait("tensor", [x_tok, ldw_tok, gconst_tok, C["tok"]])
        for which, wt, zt in ((0, wa_t, zq), (1, wb_t, zb)):
            pp = which
            kb.wait("tensor", bank.get(("pj", pp)))
            def mmab(E, wt=wt, pp=pp, slot=slot):
                for k in range(DC - 1):
                    E.matmul(ps_pj[0:4, pp, :], wt[:, k, :], xs[:, slot, k, :],
                             start=(k == 0), stop=False)
                return E.matmul(ps_pj[0:4, pp, :], wt[:, DC - 1, :],
                                xs[:, slot, DC - 1, :], start=False, stop=True)
            tok_mm = kb.op("tensor", mmab)
            kb.wait("vector", tok_mm)
            ev = kb.op("vector", lambda E, zt=zt, pp=pp, bc=(bacol if which == 0
                                                            else bbcol):
                       E.tensor_single_scalar(zt[:, :], ps_pj[0:4, pp, :],
                                              bc[:, 0:1], ALU.add))
            bank[("pj", pp)] = [ev]
        kb.wait("scalar", kb.eauto("vector").tok())
        kb.op("scalar", lambda E: E.activation(zs1[:, :], zq[:, :], AF.Abs))
        kb.op("scalar", lambda E: E.activation(zs2[:, :], zs1[:, :], AF.Exp,
                                               scale=-1.0))
        tok_ln = kb.op("scalar", lambda E: E.activation(
            zs1[:, :], zs2[:, :], AF.Ln, bias=onef[0:4, 0:1]))
        kb.wait("vector", tok_ln)
        kb.op("vector", lambda E: E.tensor_scalar_min(zs2[:, :], zq[:, :], 0.0))
        tok_la = kb.op("vector", lambda E: E.tensor_sub(la_sp[:, :], zs2[:, :],
                                                        zs1[:, :]))
        tok_b = kb.op("scalar", lambda E: E.activation(
            b_sp[:, :], zb[:, :], AF.Sigmoid))
        kb.wait("tensor", [tok_la, tok_b])
        for u in range(4):
            ch = sp * 4 + u
            tok_t = kb.op("tensor", lambda E, u=u: E.transpose(
                ps_c[0:128, 0, 0:4], la_sp[0:4, u * 128:(u + 1) * 128],
                identf[0:4, 0:4]))
            kb.wait("vector", tok_t)
            ev = kb.op("vector", lambda E, ch=ch: E.tensor_copy(
                labtm[:, ch, 0:4], ps_c[0:128, 0, 0:4]))
            kb.wait("tensor", ev)
            tok_t = kb.op("tensor", lambda E, u=u: E.transpose(
                ps_c[0:128, 0, 0:4], b_sp[0:4, u * 128:(u + 1) * 128],
                identf[0:4, 0:4]))
            kb.wait("vector", tok_t)
            ev = kb.op("vector", lambda E, ch=ch: E.tensor_copy(
                labtm[:, ch, 4:8], ps_c[0:128, 0, 0:4]))
            kb.wait("tensor", ev)
            tok_g = kb.op("tensor", lambda E, ch=ch: E.matmul(
                ps_c[:, 0, 0:4], utonesf[:, :], labtm[:, ch, 0:4],
                start=True, stop=True))
            kb.wait("vector", tok_g)
            ev = kb.op("vector", lambda E, ch=ch: E.tensor_copy(
                gtm[:, ch, :], ps_c[:, 0, 0:4]))
            kb.wait("tensor", ev)
        bank[("xuse", slot)] = kb.eauto("tensor").tok()
    labtm_tok = kb.eauto("vector").tok()

    for h in range(n_heads):
        kb.wait("scalar", kb.eauto("vector").tok())
        kb.wait("gpsimd", bank.get(("whuse",)))
        for wt, wi in ((wq_t, wq_in), (wk_t, wk_in), (wv_t, wv_in), (wg_t, wg_in)):
            load_wT(kb, wt, wi, D, s_wh, cols=(h * 128, (h + 1) * 128))
        wh_tok = s_wh.tok()
        if h == 0:
            for raw in (rawq, rawk, rawv):
                kb.op("gpsimd", lambda E, raw=raw: E.memset(raw[:, 0:3], 0.0))
            pad_tok = kb.eauto("gpsimd").tok()
        for sp in range(n_spans):
            slot = sp % 2
            prev = bank.get(("xuse", slot))
            if prev is not None:
                kb.wait("sync", prev)
            for k in range(DC):
                kb.dma("sync", xs[:, slot, k, :],
                       x_in[k * 128:(k + 1) * 128, sp // 2,
                            (sp % 2) * 512:(sp % 2) * 512 + 512],
                       s_x[slot])
            x_tok = s_x[slot].tok()
            kb.wait("tensor", [x_tok, wh_tok, pad_tok])
            for wi, (wt, dst) in enumerate(((wq_t, rawq), (wk_t, rawk),
                                            (wv_t, rawv), (wg_t, None))):
                pp = wi % 2
                kb.wait("tensor", bank.get(("pj", pp)))
                def mmp(E, wt=wt, pp=pp, slot=slot):
                    for k in range(DC - 1):
                        E.matmul(ps_pj[:, pp, :], wt[:, k, :], xs[:, slot, k, :],
                                 start=(k == 0), stop=False)
                    return E.matmul(ps_pj[:, pp, :], wt[:, DC - 1, :],
                                    xs[:, slot, DC - 1, :],
                                    start=False, stop=True)
                tok_mm = kb.op("tensor", mmp)
                if dst is None:
                    kb.wait("scalar", tok_mm)
                    ev = kb.op("scalar", lambda E, sp=sp, pp=pp: E.activation(
                        gateh[:, sp * 512:(sp + 1) * 512], ps_pj[:, pp, :],
                        AF.Sigmoid))
                else:
                    kb.wait("vector", tok_mm)
                    ev = kb.op("vector", lambda E, dst=dst, sp=sp, pp=pp:
                               E.tensor_copy(dst[:, 3 + sp * 512:3 + (sp + 1) * 512],
                                             ps_pj[:, pp, :]))
                bank[("pj", pp)] = [ev]
            raw_tok = kb.eauto("vector").tok()
            for raw, cw, cb, cout in ((rawq, qcw, qcb, cq), (rawk, kcw, kcb, ck),
                                      (rawv, vcw, vcb, cv_)):
                kb.wait("vector", bank.get(("cvsilu",)))
                kb.op("vector", lambda E, raw=raw, cw=cw, sp=sp, h=h:
                      E.tensor_single_scalar(
                          cvs[:, 0, :], raw[:, sp * 512:sp * 512 + 512],
                          cw[:, h, 0:1], ALU.mult))
                for tau in range(1, 4):
                    src_pp, dst_pp = (tau - 1) % 2, tau % 2
                    kb.op("vector", lambda E, raw=raw, cw=cw, sp=sp, h=h,
                          tau=tau, a=src_pp, b2=dst_pp: E.scalar_tensor_tensor(
                              cvs[:, b2, :],
                              raw[:, sp * 512 + tau:sp * 512 + tau + 512],
                              cw[:, h, tau:tau + 1], cvs[:, a, :],
                              ALU.mult, ALU.add))
                tok_cv = kb.eauto("vector").tok()
                kb.wait("scalar", tok_cv)
                if cout is ck:
                    tok_sl = kb.op("scalar", lambda E, cb=cb, h=h: E.activation(
                        kss[:, :], cvs[:, 1, :], AF.Silu, bias=cb[:, h:h + 1]))
                    kb.wait("vector", tok_sl)
                    kb.op("vector", lambda E, sp=sp: E.tensor_scalar_mul(
                        ck[:, sp * 512:(sp + 1) * 512], kss[:, :],
                        1.0 / math.sqrt(128.0)))
                    bank[("cvsilu",)] = [tok_sl]
                else:
                    tok_sl = kb.op("scalar", lambda E, cout=cout, cb=cb, sp=sp, h=h:
                          E.activation(cout[:, sp * 512:(sp + 1) * 512],
                                       cvs[:, 1, :], AF.Silu,
                                       bias=cb[:, h:h + 1]))
                    bank[("cvsilu",)] = [tok_sl]
            bank[("xuse", slot)] = kb.eauto("tensor").tok()
        conv_tok = [kb.eauto("vector").tok(), kb.eauto("scalar").tok()]

        kb.wait("tensor", conv_tok + [labtm_tok])
        stpp = 0
        kb.op("vector", lambda E, h=h: E.memset(st_f[:, h, 0, :], 0.0))
        tok_st0 = kb.op("vector", lambda E: E.memset(st_bf[:, :], 0.0))
        st_tok = tok_st0
        for c in range(n_chunks):
            t0 = c * 128
            kb.wait("vector", bank.get(("colv",)))
            kb.op("vector", lambda E, c=c, h=h: E.tensor_copy(
                colv[:, 0:1], labtm[:, c, h:h + 1]))
            kb.op("vector", lambda E, c=c, h=h: E.tensor_copy(
                colv[:, 1:2], gtm[:, c, h:h + 1]))
            kb.op("vector", lambda E: E.tensor_sub(
                colv[:, 2:3], colv[:, 1:2], colv[:, 0:1]))
            tok_col = kb.eauto("vector").tok()
            kb.wait("scalar", tok_col)
            tok_gp = kb.op("scalar", lambda E: E.activation(
                colv[:, 3:4], colv[:, 2:3], AF.Exp))
            kb.wait("vector", tok_gp)
            tok_bgp = kb.op("vector", lambda E, c=c, h=h: E.tensor_mul(
                colv[:, 4:5], labtm[:, c, 4 + h:5 + h], colv[:, 3:4]))
            kb.wait("vector", bank.get(("labc",)))
            tok_labc = kb.op("vector", lambda E: E.tensor_single_scalar(
                labc[:, :], zerof[:, :], colv[:, 0:1], ALU.add))
            kb.wait("tensor", [tok_labc, bank.get(("ps_c",))])
            tok_gbc = kb.op("tensor", lambda E: E.matmul(
                ps_c[:, 0, 0:128], labc[:, :], utonesf[:, :],
                start=True, stop=True))
            bank[("labc",)] = [tok_gbc]
            kb.wait("vector", tok_gbc)
            tok_gbcs = kb.op("vector", lambda E: E.tensor_copy(
                gbc_s[:, :], ps_c[:, 0, 0:128]))
            bank[("ps_c",)] = [tok_gbcs]
            kb.op("vector", lambda E: E.tensor_sub(
                colv[:, 7:8], gbc_s[:, 127:128], colv[:, 1:2]))
            tok_l1 = kb.eauto("vector").tok()
            kb.wait("scalar", tok_l1)
            kb.op("scalar", lambda E: E.activation(colv[:, 5:6], colv[:, 7:8],
                                                   AF.Exp))
            tok_gamC = kb.op("scalar", lambda E: E.activation(
                colv[:, 6:7], gbc_s[:, 127:128], AF.Exp))
            bank[("colv",)] = [tok_gamC]
            kb.wait("scalar", bank.get(("gambc",)))
            tok_gam = kb.op("scalar", lambda E: E.activation(
                gambc[:, :], gbc_s[:, :], AF.Exp))
            kb.wait("tensor", bank.get(("pj", 0)))
            tok_gram = kb.op("tensor", lambda E, t0=t0: E.matmul(
                ps_pj[:, 0, 0:128], ck[:, t0:t0 + 128], ck[:, t0:t0 + 128],
                start=True, stop=True))
            kb.wait("tensor", bank.get(("pj", 1)))
            tok_kq = kb.op("tensor", lambda E, t0=t0: E.matmul(
                ps_pj[:, 1, 0:128], ck[:, t0:t0 + 128], cq[:, t0:t0 + 128],
                start=True, stop=True))
            kb.wait("vector", [tok_gbcs])
            kb.op("vector", lambda E: E.tensor_scalar(
                escr[:, :], gbc_s[:, :], -1.0, colv[:, 2:3],
                ALU.mult, ALU.add))
            tok_escr = kb.op("vector", lambda E: E.tensor_add(
                escr[:, :], escr[:, :], maskA[:, :]))
            kb.wait("scalar", tok_escr)
            tok_dn = kb.op("scalar", lambda E: E.activation(
                dn_bf[:, :], escr[:, :], AF.Exp))
            kb.wait("vector", [tok_dn, tok_gram, tok_bgp])
            tok_nt = kb.op("vector", lambda E: E.scalar_tensor_tensor(
                nt_bf[:, :], dn_bf[:, :], labtm[:, c, 4 + h:5 + h],
                ps_pj[:, 0, 0:128], ALU.mult, ALU.mult))
            bank[("pj", 0)] = [tok_nt]
            kb.wait("tensor", [tok_nt, bank.get(("tp", 0))])
            tok_tN = kb.op("tensor", lambda E: E.transpose(
                ps_tp[:, 0, 0:128], nt_bf[:, :], ident[:, :]))
            kb.wait("scalar", tok_tN)
            tok_t2 = kb.op("scalar", lambda E: E.activation(
                t2n[:, 0, :], ps_tp[:, 0, 0:128], AF.Identity, scale=-1.0))
            bank[("tp", 0)] = [tok_t2]
            kb.wait("vector", tok_nt)
            tok_t1 = kb.op("vector", lambda E: E.tensor_scalar_mul(
                t1n[:, 0, :], nt_bf[:, :], -1.0))
            kb.wait("vector", tok_dn)
            kb.op("vector", lambda E: E.tensor_scalar(
                escr[:, :], gbc_s[:, :], 1.0, colv[:, 1:2],
                ALU.mult, ALU.subtract))
            tok_es2 = kb.op("vector", lambda E: E.tensor_add(
                escr[:, :], escr[:, :], maskB[:, :]))
            kb.wait("scalar", tok_es2)
            tok_dft = kb.op("scalar", lambda E: E.activation(
                dft_bf[:, :], escr[:, :], AF.Exp))
            kb.wait("vector", [tok_dft, tok_kq])
            tok_sht = kb.op("vector", lambda E: E.tensor_mul(
                sht_bf[:, :], dft_bf[:, :], ps_pj[:, 1, 0:128]))
            bank[("pj", 1)] = [tok_sht]
            kb.wait("tensor", bank.get(("tp", 1)))
            tok_tV = kb.op("tensor", lambda E, t0=t0: E.transpose(
                ps_tp[:, 1, 0:128], cv_[:, t0:t0 + 128], ident[:, :]))
            kb.wait("vector", tok_tV)
            tok_vtm = kb.op("vector", lambda E: E.tensor_copy(
                vtm_bf[:, :], ps_tp[:, 1, 0:128]))
            bank[("tp", 1)] = [tok_vtm]
            kb.wait("tensor", bank.get(("tp", 0)))
            tok_tK = kb.op("tensor", lambda E, t0=t0: E.transpose(
                ps_tp[:, 0, 0:128], ck[:, t0:t0 + 128], ident[:, :]))
            kb.wait("vector", tok_tK)
            tok_ktm = kb.op("vector", lambda E: E.tensor_copy(
                ktm_bf[:, :], ps_tp[:, 0, 0:128]))
            bank[("tp", 0)] = [tok_ktm]
            kb.wait("tensor", [st_tok, bank.get(("ps_d",))])
            tok_kst = kb.op("tensor", lambda E, t0=t0: E.matmul(
                ps_d[:, 0, 0:128], ck[:, t0:t0 + 128], st_bf[:, :],
                start=True, stop=True))
            kb.wait("vector", tok_kst)
            kb.op("vector", lambda E: E.tensor_scalar_mul(
                scrR[:, :], ps_d[:, 0, 0:128], colv[:, 4:5]))
            bank[("ps_d",)] = [kb.eauto("vector").tok()]
            tok_rhs = kb.op("vector", lambda E, c=c, h=h: E.scalar_tensor_tensor(
                r_bf[:, :], vtm_bf[:, :], labtm[:, c, 4 + h:5 + h],
                scrR[:, :], ALU.mult, ALU.subtract))
            kb.op("vector", lambda E: E.tensor_copy(xsol[:, 0, :], r_bf[:, :]))
            x_cur, x_nxt = 0, 1
            p_cur, p_nxt = 0, 1
            kb.wait("tensor", kb.eauto("vector").tok())
            kb.wait("tensor", [tok_t2])
            tok_xa = None
            tok_e1 = None
            for lvl in range(7):
                kb.wait("tensor", tok_xa)
                kb.wait("tensor", tok_e1)
                tok_xu = kb.op("tensor", lambda E, p_cur=p_cur, x_cur=x_cur:
                               E.matmul(ps_x[:, 0, 0:128], t2n[:, p_cur, :],
                                        xsol[:, x_cur, :],
                                        start=True, stop=True))
                kb.wait("vector", tok_xu)
                tok_xa = kb.op("vector", lambda E, x_cur=x_cur, x_nxt=x_nxt:
                               E.tensor_add(xsol[:, x_nxt, :],
                                            xsol[:, x_cur, :],
                                            ps_x[:, 0, 0:128]))
                if lvl < 6:
                    kb.wait("tensor", bank.get(("ps_d",)))
                    tok_s1 = kb.op("tensor", lambda E, p_cur=p_cur: E.matmul(
                        ps_d[:, 0, 0:128], t1n[:, p_cur, :], t2n[:, p_cur, :],
                        start=True, stop=True))
                    kb.wait("scalar", tok_s1)
                    tok_e1 = kb.op("scalar", lambda E, p_nxt=p_nxt: E.copy(
                        t2n[:, p_nxt, :], ps_d[:, 0, 0:128]))
                    kb.wait("tensor", tok_e1)
                    tok_s2 = kb.op("tensor", lambda E, p_cur=p_cur: E.matmul(
                        ps_d[:, 0, 0:128], t2n[:, p_cur, :], t1n[:, p_cur, :],
                        start=True, stop=True))
                    kb.wait("scalar", tok_s2)
                    tok_e2 = kb.op("scalar", lambda E, p_nxt=p_nxt: E.copy(
                        t1n[:, p_nxt, :], ps_d[:, 0, 0:128]))
                    bank[("ps_d",)] = [tok_e2]
                    p_cur, p_nxt = p_nxt, p_cur
                x_cur, x_nxt = x_nxt, x_cur
            uy_tok = kb.eauto("vector").tok()
            kb.wait("vector", [uy_tok, tok_gamC])
            tok_uyl = kb.op("vector", lambda E, x_cur=x_cur: E.tensor_scalar_mul(
                uyl_bf[:, :], xsol[:, x_cur, :], colv[:, 5:6]))
            kb.wait("vector", [tok_gam])
            tok_qg = kb.op("vector", lambda E, t0=t0: E.tensor_mul(
                qg_bf[:, :], cq[:, t0:t0 + 128], gambc[:, :]))
            bank[("gambc",)] = [tok_qg]
            kb.wait("tensor", [tok_qg, tok_sht, bank.get(("ps_c",)),
                               kb.eauto("vector").tok()])
            def mmo(E, x_cur=x_cur):
                E.matmul(ps_c[:, 0, 0:128], st_bf[:, :], qg_bf[:, :],
                         start=True, stop=False)
                return E.matmul(ps_c[:, 0, 0:128], xsol[:, x_cur, :],
                                sht_bf[:, :], start=False, stop=True)
            tok_o = kb.op("tensor", mmo)
            kb.wait("scalar", tok_o)
            tok_oev = kb.op("scalar", lambda E: E.copy(o_s[:, :],
                                                       ps_c[:, 0, 0:128]))
            bank[("ps_c",)] = [tok_oev]
            kb.wait("tensor", [tok_uyl, tok_ktm, bank.get(("ps_e",))])
            tok_smm = kb.op("tensor", lambda E: E.matmul(
                ps_e[:, 0, 0:128], ktm_bf[:, :], uyl_bf[:, :],
                start=True, stop=True))
            kb.wait("vector", [tok_smm, tok_gamC])
            tok_stn = kb.op("vector", lambda E, h=h, stpp=stpp: E.scalar_tensor_tensor(
                st_f[:, h, 1 - stpp, :], st_f[:, h, stpp, :], colv[:, 6:7],
                ps_e[:, 0, 0:128], ALU.mult, ALU.add))
            bank[("ps_e",)] = [tok_stn]
            kb.wait("vector", tok_o)
            tok_stbf = kb.op("vector", lambda E, h=h, stpp=stpp: E.tensor_copy(
                st_bf[:, :], st_f[:, h, 1 - stpp, :]))
            stpp = 1 - stpp
            st_tok = tok_stbf
            kb.wait("scalar", tok_oev)
            tok_sqo = kb.op("scalar", lambda E: E.activation(
                sqo[:, :], o_s[:, :], AF.Square))
            kb.wait("vector", tok_oev)
            tok_obf = kb.op("vector", lambda E: E.tensor_copy(o_bf[:, :],
                                                              o_s[:, :]))
            kb.wait("tensor", [tok_sqo, tok_obf, bank.get(("ps_d",)),
                               bank.get(("ps_e",))])
            tok_mA = kb.op("tensor", lambda E: E.matmul(
                ps_d[:, 0, 0:128], ones128[:, :], o_bf[:, :],
                start=True, stop=True))
            tok_mB = kb.op("tensor", lambda E: E.matmul(
                ps_e[:, 0, 128:256], ones128[:, :], sqo[:, :],
                start=True, stop=True))
            kb.wait("vector", tok_mA)
            kb.op("vector", lambda E: E.tensor_scalar_mul(
                muo[:, :], ps_d[:, 0, 0:128], 1.0 / 128.0))
            bank[("ps_d",)] = [kb.eauto("vector").tok()]
            kb.wait("vector", tok_mB)
            kb.op("vector", lambda E: E.tensor_scalar_mul(
                mso[:, :], ps_e[:, 0, 128:256], 1.0 / 128.0))
            bank[("ps_e",)] = [kb.eauto("vector").tok()]
            kb.op("vector", lambda E: E.tensor_mul(oscr[:, :], muo[:, :],
                                                   muo[:, :]))
            tok_var = kb.op("vector", lambda E: E.tensor_sub(
                varo[:, :], mso[:, :], oscr[:, :]))
            kb.wait("scalar", tok_var)
            kb.op("scalar", lambda E: E.activation(mso[:, :], varo[:, :],
                                                   AF.Ln, bias=C["eps"][:, 0:1]))
            tok_rstd = kb.op("scalar", lambda E: E.activation(
                rstdo[:, :], mso[:, :], AF.Exp, scale=-0.5))
            kb.wait("vector", tok_rstd)
            kb.op("vector", lambda E: E.tensor_sub(oscr[:, :], o_s[:, :],
                                                   muo[:, :]))
            kb.op("vector", lambda E: E.tensor_mul(varo[:, :], oscr[:, :],
                                                   rstdo[:, :]))
            kb.op("vector", lambda E: E.tensor_scalar(
                oscr[:, :], varo[:, :], lngc[:, 0:1], lnbc[:, 0:1],
                ALU.mult, ALU.add))
            tok_og = kb.op("vector", lambda E, t0=t0, h=h: E.tensor_mul(
                ogsb[:, h % 2, t0:t0 + 128], oscr[:, :],
                gateh[:, t0:t0 + 128]))
        kb.wait("sync", kb.eauto("vector").tok())
        kb.dma("sync", og_out[h * 128:(h + 1) * 128, 0:n_chunks * 128],
               ogsb[:, h % 2, 0:n_chunks * 128], s_st)
        bank[("whuse",)] = kb.eauto("tensor").tok()
        kb.wait("vector", s_st.tok())
    kb.wait("sync", s_st.tok())
    return kb




def _run(kb, in_maps, **kw):
    nc = kb.build()
    return run_bass_kernel_spmd(nc, in_maps, core_ids=list(range(8)), **kw)



def emit_projpart(kb, io, name):
    w_in = io["w_sl"]
    in_loc = io["inT_loc"]
    out_p = _v3(io["partial"])
    s_w = kb.sem(f"s_w_{name}")
    s_i = [kb.sem(f"s_i0_{name}"), kb.sem(f"s_i1_{name}")]
    s_st = kb.sem(f"s_st_{name}")
    wT = kb.sbuf("ppw", [128, 4, D], BF16)
    isb = kb.sbuf("ppi", [128, 2, 4, 512], BF16)
    stg = kb.sbuf("ppstg", [128, 2, 512], BF16)
    ps = kb.psum("pp_ps", [128, 2, 512], F32)
    load_wT(kb, wT, w_in, 512, s_w)
    w_tok = s_w.tok()
    bank = {}
    for sp in range(8):
        slot = sp % 2
        prev = bank.get(("iuse", slot))
        if prev is not None:
            kb.wait("sync", prev)
        for k in range(4):
            kb.dma("sync", isb[:, slot, k, :],
                   in_loc[k * 128:(k + 1) * 128, sp * 512:(sp + 1) * 512],
                   s_i[slot])
        i_tok = s_i[slot].tok()
        kb.wait("tensor", [w_tok, i_tok])
        for j in range(DC):
            pp = j % 2
            kb.wait("tensor", bank.get(("mm", pp)))
            def mmp(E, j=j, pp=pp, slot=slot):
                for k in range(3):
                    E.matmul(ps[:, pp, :], wT[:, k, j * 128:(j + 1) * 128],
                             isb[:, slot, k, :], start=(k == 0), stop=False)
                return E.matmul(ps[:, pp, :], wT[:, 3, j * 128:(j + 1) * 128],
                                isb[:, slot, 3, :], start=False, stop=True)
            tok_mm = kb.op("tensor", mmp)
            kb.wait("vector", [tok_mm, s_st.tok()])
            ev = kb.op("vector", lambda E, pp=pp: E.tensor_copy(
                stg[:, pp, :], ps[:, pp, :]))
            bank[("mm", pp)] = [ev]
            kb.wait("sync", ev)
            kb.dma("sync", out_p[j * 128:(j + 1) * 128, sp // 2,
                                 (sp % 2) * 512:(sp % 2) * 512 + 512],
                   stg[:, pp, :], s_st)
        bank[("iuse", slot)] = kb.eauto("tensor").tok()
    kb.wait("sync", s_st.tok())


RG4 = [[0, 1, 2, 3], [4, 5, 6, 7]]


def build_fused():
    kb = LB("fused")
    d = kb.din
    ins = {
        "x_sl": d("x_sl", [T_SL, D]),
        "w_cq": d("w_cq", [D, 1536]), "w_ckv": d("w_ckv", [D, 512]),
        "w_kr": d("w_kr", [D, 64]),
        "q_norm_w": d("q_norm_w", [1536]), "kv_norm_w": d("kv_norm_w", [512]),
        "wdqn_sl": d("wdqn_sl", [1536, 512]), "wdqr_sl": d("wdqr_sl", [1536, 256]),
        "wdkn_sl": d("wdkn_sl", [512, 512]), "wdv_sl": d("wdv_sl", [512, 512]),
        "rc": d("rc", [4, 64, 64]), "rsp": d("rsp", [4, 64, 64]),
        "rsn": d("rsn", [4, 64, 64]),
        "wproj_sl": d("wproj_sl", [512, D]),
        "gate_w": d("gate_w", [8, D]),
        "w1_sl": d("w1_sl", [2, 2816, D]), "w2_sl": d("w2_sl", [2, D, 1408]),
        "esel": d("esel", [8, 2]),
        "wq_sl": d("wq_sl", [D, 512]), "wk_sl": d("wk_sl", [D, 512]),
        "wv_sl": d("wv_sl", [D, 512]), "wg_sl": d("wg_sl", [D, 512]),
        "wa_sl": d("wa_sl", [D, 4]), "wb_sl": d("wb_sl", [D, 4]),
        "ba_sl": d("ba_sl", [4]), "bb_sl": d("bb_sl", [4]),
        "qcw_sl": d("qcw_sl", [512, 4]), "qcb_sl": d("qcb_sl", [512]),
        "kcw_sl": d("kcw_sl", [512, 4]), "kcb_sl": d("kcb_sl", [512]),
        "vcw_sl": d("vcw_sl", [512, 4]), "vcb_sl": d("vcb_sl", [512]),
        "ln_g": d("ln_g", [128]), "ln_b": d("ln_b", [128]),
        "wo_sl": d("wo_sl", [512, D]),
        "wgate_sl": d("wgate_sl", [D, D]), "wup_sl": d("wup_sl", [D, D]),
        "wdown_sl": d("wdown_sl", [D, D]),
    }
    x_out = kb.dout("x_out", [T_SL, D])
    di = kb.dram
    ag1_in = di("ag1_in", [2112, T_SL], BF16)
    ag1_out = di("ag1_out", [4 * 2112, T_SL], BF16)
    attn_loc = di("attn_loc", [512, S], BF16)
    rs1_in = di("rs1_in", [4 * D, T_SL], BF16)
    rs1_out = di("rs1_out", [D, T_SL], BF16)
    x1 = di("x1", [D, T_SL], F32)
    ag2_in = di("ag2_in", [D, T_SL], BF16)
    ag2_out = di("ag2_out", [4 * D, T_SL], BF16)
    cb_ag_in = di("cb_ag_in", [T_SL, 8], F32)
    cb_ag_out = di("cb_ag_out", [S, 8], F32)
    rs2_in = di("rs2_in", [4 * D, T_SL], BF16)
    rs2_out = di("rs2_out", [D, T_SL], BF16)
    x2 = di("x2", [D, T_SL], F32)
    ag3_in = di("ag3_in", [D, T_SL], BF16)
    ag3_out = di("ag3_out", [4 * D, T_SL], BF16)
    og_loc = di("og_loc", [512, S], BF16)
    rs3_in = di("rs3_in", [4 * D, T_SL], BF16)
    rs3_out = di("rs3_out", [D, T_SL], BF16)
    x3 = di("x3", [D, T_SL], F32)
    ag4_in = di("ag4_in", [D, T_SL], BF16)
    ag4_out = di("ag4_out", [4 * D, T_SL], BF16)
    rs4_in = di("rs4_in", [4 * D, T_SL], BF16)
    rs4_out = di("rs4_out", [D, T_SL], BF16)

    def coll(kind, op, i, o):
        kb.barrier()
        kb.op("gpsimd", lambda E, i=i, o=o, kind=kind, op=op: E.collective_compute(
            kind, op, replica_groups=RG4, ins=[i[:, :].opt()],
            outs=[o[:, :].opt()]))
        kb.barrier()

    with kb.scope():
        build_pre0(kb=kb, io={
            "x_sl": ins["x_sl"], "w_cq": ins["w_cq"], "w_ckv": ins["w_ckv"],
            "w_kr": ins["w_kr"], "q_norm_w": ins["q_norm_w"],
            "kv_norm_w": ins["kv_norm_w"],
            "nqT": ag1_in[0:1536, :], "nkvT": ag1_in[1536:2048, :],
            "kropeT": ag1_in[2048:2112, :]})
    coll("AllGather", ALU.bypass, ag1_in, ag1_out)
    with kb.scope():
        build_l0attn(kb=kb, io={
            "nqT_full": _v3(ag1_out)[0:1536], "nkvT_full": _v3(ag1_out)[1536:2048],
            "kropeT_full": _v3(ag1_out)[2048:2112],
            "wdqn_sl": ins["wdqn_sl"], "wdqr_sl": ins["wdqr_sl"],
            "wdkn_sl": ins["wdkn_sl"], "wdv_sl": ins["wdv_sl"],
            "rc": ins["rc"], "rsp": ins["rsp"], "rsn": ins["rsn"],
            "attnT": attn_loc})
    kb.barrier()
    with kb.scope():
        emit_projpart(kb, {"w_sl": ins["wproj_sl"], "inT_loc": attn_loc,
                           "partial": rs1_in}, "p1")
    coll("ReduceScatter", ALU.add, rs1_in, rs1_out)
    with kb.scope():
        build_post("l0post", False, True, gating=True, n_parts=1, kb=kb, io={
            "parts": rs1_out, "x_sl": ins["x_sl"], "gate_w": ins["gate_w"],
            "combine": cb_ag_in, "xT_new": x1, "xT_new_bf": ag2_in})
    coll("AllGather", ALU.bypass, ag2_in, ag2_out)
    coll("AllGather", ALU.bypass, cb_ag_in, cb_ag_out)
    with kb.scope():
        build_moe(kb=kb, io={
            "xT_bf_full": ag2_out, "w1_sl": ins["w1_sl"], "w2_sl": ins["w2_sl"],
            "combine_sl": cb_ag_out, "esel": ins["esel"], "moeP": rs2_in})
    coll("ReduceScatter", ALU.add, rs2_in, rs2_out)
    with kb.scope():
        build_post("moepost", False, False, n_parts=1, kb=kb, io={
            "parts": rs2_out, "xT_prev": x1, "xT_new": x2, "xT_new_bf": ag3_in})
    coll("AllGather", ALU.bypass, ag3_in, ag3_out)
    with kb.scope():
        build_l1attn(kb=kb, io={
            "xT_bf_full": ag3_out,
            "wq_sl": ins["wq_sl"], "wk_sl": ins["wk_sl"], "wv_sl": ins["wv_sl"],
            "wg_sl": ins["wg_sl"], "wa_sl": ins["wa_sl"], "wb_sl": ins["wb_sl"],
            "ba_sl": ins["ba_sl"], "bb_sl": ins["bb_sl"],
            "qcw_sl": ins["qcw_sl"], "qcb_sl": ins["qcb_sl"],
            "kcw_sl": ins["kcw_sl"], "kcb_sl": ins["kcb_sl"],
            "vcw_sl": ins["vcw_sl"], "vcb_sl": ins["vcb_sl"],
            "ln_g": ins["ln_g"], "ln_b": ins["ln_b"], "ogT": og_loc})
    kb.barrier()
    with kb.scope():
        emit_projpart(kb, {"w_sl": ins["wo_sl"], "inT_loc": og_loc,
                           "partial": rs3_in}, "p3")
    coll("ReduceScatter", ALU.add, rs3_in, rs3_out)
    with kb.scope():
        build_post("l1post", False, False, n_parts=1, kb=kb, io={
            "parts": rs3_out, "xT_prev": x2, "xT_new": x3, "xT_new_bf": ag4_in})
    coll("AllGather", ALU.bypass, ag4_in, ag4_out)
    with kb.scope():
        build_l1ffn(kb=kb, io={
            "xT_bf_full": ag4_out, "wg_sl": ins["wgate_sl"],
            "wu_sl": ins["wup_sl"], "wd_sl": ins["wdown_sl"], "ffnP": rs4_in})
    coll("ReduceScatter", ALU.add, rs4_in, rs4_out)
    with kb.scope():
        build_post("finpost", False, False, final=True, n_parts=1, kb=kb, io={
            "parts": rs4_out, "xT_prev": x3, "x_out": x_out})
    kb.barrier()
    return kb


def _esel(r):
    m = np.zeros((8, 2), np.float32)
    m[2 * r, 0] = 1.0
    m[2 * r + 1, 1] = 1.0
    return m


def kernel_fused(**inputs):
    inp = inputs
    ca = np.ascontiguousarray
    x = inp["L0_attn_x"]
    fc, fs = inp["L0_attn_freqs_cos"], inp["L0_attn_freqs_sin"]
    nc = _prog("fused", build_fused)
    maps = []
    for c in range(8):
        g, r = c // 4, c % 4
        rcm, rsp, rsn = _rope_mats_host(fc, fs, [4 * r + i for i in range(4)])
        sl5 = slice(512 * r, 512 * (r + 1))
        sl4 = slice(4 * r, 4 * (r + 1))
        maps.append({
            "x_sl": ca(x[g, r * 1024:(r + 1) * 1024]),
            "w_cq": inp["L0_attn_w_cq"], "w_ckv": inp["L0_attn_w_ckv"],
            "w_kr": inp["L0_attn_w_k_rope"],
            "q_norm_w": inp["L0_attn_q_norm_w"],
            "kv_norm_w": inp["L0_attn_kv_norm_w"],
            "wdqn_sl": ca(inp["L0_attn_w_dq_nope"][:, sl5]),
            "wdqr_sl": ca(inp["L0_attn_w_dq_rope"][:, 256 * r:256 * (r + 1)]),
            "wdkn_sl": ca(inp["L0_attn_w_dk_nope"][:, sl5]),
            "wdv_sl": ca(inp["L0_attn_w_dv"][:, sl5]),
            "rc": rcm, "rsp": rsp, "rsn": rsn,
            "wproj_sl": ca(inp["L0_attn_w_proj"][sl5, :]),
            "gate_w": inp["L0_ffn_gate_w"],
            "w1_sl": ca(inp["L0_ffn_w1"][2 * r:2 * r + 2]),
            "w2_sl": ca(inp["L0_ffn_w2"][2 * r:2 * r + 2]),
            "esel": _esel(r),
            "wq_sl": ca(inp["L1_attn_Wq"][:, sl5]),
            "wk_sl": ca(inp["L1_attn_Wk"][:, sl5]),
            "wv_sl": ca(inp["L1_attn_Wv"][:, sl5]),
            "wg_sl": ca(inp["L1_attn_Wg"][:, sl5]),
            "wa_sl": ca(inp["L1_attn_Wa"][:, sl4]),
            "wb_sl": ca(inp["L1_attn_Wb"][:, sl4]),
            "ba_sl": ca(inp["L1_attn_ba"][sl4]),
            "bb_sl": ca(inp["L1_attn_bb"][sl4]),
            "qcw_sl": ca(inp["L1_attn_qconv_w"][sl5, 0, :]),
            "qcb_sl": ca(inp["L1_attn_qconv_b"][sl5]),
            "kcw_sl": ca(inp["L1_attn_kconv_w"][sl5, 0, :]),
            "kcb_sl": ca(inp["L1_attn_kconv_b"][sl5]),
            "vcw_sl": ca(inp["L1_attn_vconv_w"][sl5, 0, :]),
            "vcb_sl": ca(inp["L1_attn_vconv_b"][sl5]),
            "ln_g": inp["L1_attn_ln_g"], "ln_b": inp["L1_attn_ln_b"],
            "wo_sl": ca(inp["L1_attn_Wo"][sl5, :]),
            "wgate_sl": ca(inp["L1_ffn_w_gate"][2048 * r:2048 * (r + 1), :]),
            "wup_sl": ca(inp["L1_ffn_w_up"][2048 * r:2048 * (r + 1), :]),
            "wdown_sl": ca(inp["L1_ffn_w_down"][:, 2048 * r:2048 * (r + 1)]),
        })
    res = _spmd(nc, maps)
    out = np.empty((2, 4096, 2048), np.float32)
    for c in range(8):
        g, r = c // 4, c % 4
        out[g, r * 1024:(r + 1) * 1024] = res[c]["x_out"]
    return out


_PROG_CACHE = {}


def _prog(name, builder, *a, **kw):
    if name not in _PROG_CACHE:
        _PROG_CACHE[name] = builder(*a, **kw).build()
    return _PROG_CACHE[name]


def _rope_mats_host(fc, fs, heads):
    rc = np.zeros((4, 64, 64), np.float32)
    rsp = np.zeros((4, 64, 64), np.float32)
    rsn = np.zeros((4, 64, 64), np.float32)
    for i, gh in enumerate(heads):
        for j in range(32):
            rc[i, 2 * j, 2 * j] = fc[gh, j]
            rc[i, 2 * j + 1, 2 * j + 1] = fc[gh, j]
            rsp[i, 2 * j, 2 * j + 1] = fs[gh, j]
            rsn[i, 2 * j + 1, 2 * j] = fs[gh, j]
    return rc, rsp, rsn


def _spmd(nc, in_maps):
    return run_bass_kernel_spmd(nc, in_maps, core_ids=list(range(8))).results


def kernel(**inputs):
    _ = inputs.get("L0_attn_mask")
    return kernel_fused(**inputs)


def kernel_unfused(**inputs):
    inp = inputs
    ca = np.ascontiguousarray
    x = inp["L0_attn_x"]
    fc, fs = inp["L0_attn_freqs_cos"], inp["L0_attn_freqs_sin"]

    nc = _prog("pre0", build_pre0)
    maps = []
    for c in range(8):
        g, r = c // 4, c % 4
        maps.append({
            "x_sl": ca(x[g, r * 1024:(r + 1) * 1024]),
            "w_cq": inp["L0_attn_w_cq"], "w_ckv": inp["L0_attn_w_ckv"],
            "w_kr": inp["L0_attn_w_k_rope"],
            "q_norm_w": inp["L0_attn_q_norm_w"],
            "kv_norm_w": inp["L0_attn_kv_norm_w"],
        })
    r1 = _spmd(nc, maps)

    nq_full = [np.concatenate([r1[4 * g + i]["nqT"] for i in range(4)], axis=1)
               for g in range(2)]
    nkv_full = [np.concatenate([r1[4 * g + i]["nkvT"] for i in range(4)], axis=1)
                for g in range(2)]
    kr_full = [np.concatenate([r1[4 * g + i]["kropeT"] for i in range(4)], axis=1)
               for g in range(2)]
    nc = _prog("l0attn", build_l0attn)
    maps = []
    for c in range(8):
        g, r = c // 4, c % 4
        rc, rsp, rsn = _rope_mats_host(fc, fs, [4 * r + i for i in range(4)])
        maps.append({
            "nqT_full": ca(nq_full[g]), "nkvT_full": ca(nkv_full[g]),
            "kropeT_full": ca(kr_full[g]),
            "wdqn_sl": ca(inp["L0_attn_w_dq_nope"][:, 512 * r:512 * (r + 1)]),
            "wdqr_sl": ca(inp["L0_attn_w_dq_rope"][:, 256 * r:256 * (r + 1)]),
            "wdkn_sl": ca(inp["L0_attn_w_dk_nope"][:, 512 * r:512 * (r + 1)]),
            "wdv_sl": ca(inp["L0_attn_w_dv"][:, 512 * r:512 * (r + 1)]),
            "rc": rc, "rsp": rsp, "rsn": rsn,
        })
    r2 = _spmd(nc, maps)

    attn_full = [np.concatenate([r2[4 * g + i]["attnT"] for i in range(4)], axis=0)
                 for g in range(2)]
    nc = _prog("l0post", build_post, "l0post", True, True, gating=True)
    maps = []
    for c in range(8):
        g, r = c // 4, c % 4
        maps.append({
            "inT_sl": ca(attn_full[g][:, r * 1024:(r + 1) * 1024]),
            "w": inp["L0_attn_w_proj"],
            "x_sl": ca(x[g, r * 1024:(r + 1) * 1024]),
            "gate_w": inp["L0_ffn_gate_w"],
        })
    r3 = _spmd(nc, maps)

    x1bf_full = [np.concatenate([r3[4 * g + i]["xT_new_bf"] for i in range(4)],
                                axis=1) for g in range(2)]
    comb_full = [np.concatenate([r3[4 * g + i]["combine"] for i in range(4)],
                                axis=0) for g in range(2)]
    nc = _prog("moe", build_moe)
    maps = []
    for c in range(8):
        g, r = c // 4, c % 4
        maps.append({
            "xT_bf_full": ca(x1bf_full[g]),
            "w1_sl": ca(inp["L0_ffn_w1"][2 * r:2 * r + 2]),
            "w2_sl": ca(inp["L0_ffn_w2"][2 * r:2 * r + 2]),
            "combine_sl": ca(comb_full[g][:, 2 * r:2 * r + 2]),
        })
    r4 = _spmd(nc, maps)

    nc = _prog("sumpost", build_post, "sumpost", False, False)
    maps = []
    for c in range(8):
        g, r = c // 4, c % 4
        parts = np.stack([r4[4 * g + i]["moeP"][:, r * 1024:(r + 1) * 1024]
                          for i in range(4)], axis=0)
        maps.append({"parts": ca(parts), "xT_prev": ca(r3[c]["xT_new"])})
    r5 = _spmd(nc, maps)

    x2bf_full = [np.concatenate([r5[4 * g + i]["xT_new_bf"] for i in range(4)],
                                axis=1) for g in range(2)]
    nc = _prog("l1attn", build_l1attn)
    maps = []
    for c in range(8):
        g, r = c // 4, c % 4
        sl5 = slice(512 * r, 512 * (r + 1))
        sl4 = slice(4 * r, 4 * (r + 1))
        maps.append({
            "xT_bf_full": ca(x2bf_full[g]),
            "wq_sl": ca(inp["L1_attn_Wq"][:, sl5]),
            "wk_sl": ca(inp["L1_attn_Wk"][:, sl5]),
            "wv_sl": ca(inp["L1_attn_Wv"][:, sl5]),
            "wg_sl": ca(inp["L1_attn_Wg"][:, sl5]),
            "wa_sl": ca(inp["L1_attn_Wa"][:, sl4]),
            "wb_sl": ca(inp["L1_attn_Wb"][:, sl4]),
            "ba_sl": ca(inp["L1_attn_ba"][sl4]),
            "bb_sl": ca(inp["L1_attn_bb"][sl4]),
            "qcw_sl": ca(inp["L1_attn_qconv_w"][sl5, 0, :]),
            "qcb_sl": ca(inp["L1_attn_qconv_b"][sl5]),
            "kcw_sl": ca(inp["L1_attn_kconv_w"][sl5, 0, :]),
            "kcb_sl": ca(inp["L1_attn_kconv_b"][sl5]),
            "vcw_sl": ca(inp["L1_attn_vconv_w"][sl5, 0, :]),
            "vcb_sl": ca(inp["L1_attn_vconv_b"][sl5]),
            "ln_g": inp["L1_attn_ln_g"], "ln_b": inp["L1_attn_ln_b"],
        })
    r6 = _spmd(nc, maps)

    og_full = [np.concatenate([r6[4 * g + i]["ogT"] for i in range(4)], axis=0)
               for g in range(2)]
    nc = _prog("l1post", build_post, "l1post", True, False)
    maps = []
    for c in range(8):
        g, r = c // 4, c % 4
        maps.append({
            "inT_sl": ca(og_full[g][:, r * 1024:(r + 1) * 1024]),
            "w": inp["L1_attn_Wo"],
            "xT_prev": ca(r5[c]["xT_new"]),
        })
    r7 = _spmd(nc, maps)

    x3bf_full = [np.concatenate([r7[4 * g + i]["xT_new_bf"] for i in range(4)],
                                axis=1) for g in range(2)]
    nc = _prog("l1ffn", build_l1ffn)
    maps = []
    for c in range(8):
        g, r = c // 4, c % 4
        maps.append({
            "xT_bf_full": ca(x3bf_full[g]),
            "wg_sl": ca(inp["L1_ffn_w_gate"][2048 * r:2048 * (r + 1), :]),
            "wu_sl": ca(inp["L1_ffn_w_up"][2048 * r:2048 * (r + 1), :]),
            "wd_sl": ca(inp["L1_ffn_w_down"][:, 2048 * r:2048 * (r + 1)]),
        })
    r8 = _spmd(nc, maps)

    nc = _prog("finpost", build_post, "finpost", False, False, final=True)
    maps = []
    for c in range(8):
        g, r = c // 4, c % 4
        parts = np.stack([r8[4 * g + i]["ffnP"][:, r * 1024:(r + 1) * 1024]
                          for i in range(4)], axis=0)
        maps.append({"parts": ca(parts), "xT_prev": ca(r7[c]["xT_new"])})
    r9 = _spmd(nc, maps)

    out = np.empty((2, 4096, 2048), np.float32)
    for c in range(8):
        g, r = c // 4, c % 4
        out[g, r * 1024:(r + 1) * 1024] = r9[c]["x_out"]
    return out


# version 05
# speedup vs baseline: 1.1184x; 1.0011x over previous
import math
import numpy as np
import concourse.bass as bass
import concourse.mybir as mybir
from concourse.bass_utils import run_bass_kernel_spmd

F32 = mybir.dt.float32
BF16 = mybir.dt.bfloat16
AF = mybir.ActivationFunctionType
ALU = mybir.AluOpType
AX = mybir.AxisListType

B = 2
S = 4096
T_SL = 1024
D = 2048
DC = D // 128
NEG = -1e30
EPS = 1e-5



class Sem:
    def __init__(self, handle):
        self.h = handle
        self.n = 0
        self.barrier_exempt = False

    def tok(self):
        return (self, self.n)


class LB:

    ENGINES = ("tensor", "vector", "scalar", "gpsimd", "sync")
    AUTOCHAIN = {"vector": True, "scalar": True, "gpsimd": True,
                 "tensor": False, "sync": False}

    def __init__(self, name):
        self.nc = bass.Bass("TRN2", target_bir_lowering=False, debug=False)
        self.name = name
        self.q = {e: [] for e in self.ENGINES}
        self._ctx = []
        self._mem = []
        self._waited = {e: {} for e in self.ENGINES}
        self.auto = {}
        self._all_sems = []
        self._uniq = 0

    def scope(self):
        from contextlib import contextmanager

        @contextmanager
        def _s():
            mark = len(self._mem)
            yield
            for g in reversed(self._mem[mark:]):
                g.__exit__(None, None, None)
            del self._mem[mark:]
        return _s()

    def barrier(self):
        toks = [s.tok() for s in self._all_sems if not s.barrier_exempt]
        for e in self.ENGINES:
            self.wait(e, toks)

    def eauto(self, eng):
        if eng not in self.auto:
            self.auto[eng] = self.sem(f"s_auto_{eng}")
        return self.auto[eng]

    def dram(self, name, shape, dtype, kind="Internal"):
        return self.nc.dram_tensor(name, list(shape), dtype, kind=kind).ap()

    def din(self, name, shape, dtype=F32):
        return self.dram(name, shape, dtype, kind="ExternalInput")

    def dout(self, name, shape, dtype=F32):
        return self.dram(name, shape, dtype, kind="ExternalOutput")

    def sbuf(self, name, shape, dtype):
        self._uniq += 1
        g = self.nc.sbuf_tensor(f"{name}_u{self._uniq}", list(shape), dtype)
        t = g.__enter__()
        self._mem.append(g)
        return t

    def psum(self, name, shape, dtype=F32):
        self._uniq += 1
        g = self.nc.psum_tensor(f"{name}_u{self._uniq}", list(shape), dtype)
        t = g.__enter__()
        self._mem.append(g)
        return t

    def sem(self, name):
        self._uniq += 1
        g = self.nc.semaphore(f"{name}_u{self._uniq}")
        h = g.__enter__()
        self._ctx.append(g)
        s = Sem(h)
        self._all_sems.append(s)
        return s

    def add(self, eng, fn):
        self.q[eng].append(fn)

    def wait(self, eng, tok):
        if tok is None:
            return
        if isinstance(tok, list):
            for t in tok:
                self.wait(eng, t)
            return
        sem, val = tok
        if val <= 0:
            return
        prev = self._waited[eng].get(sem, 0)
        if val <= prev:
            return
        self._waited[eng][sem] = val
        self.add(eng, lambda E, s=sem.h, v=val: E.wait_ge(s, v))

    def op(self, eng, fn, sem=None, n=1, chain=None):
        a = self.eauto(eng)
        if chain is None:
            chain = self.AUTOCHAIN[eng]
        if chain:
            self.wait(eng, a.tok())
        a.n += 1
        self.add(eng, lambda E, f=fn, s=a.h, n=a.n: f(E).then_inc(s, 1))
        return a.tok()

    def dma(self, eng, out, in_, sem, **kw):
        sem.n += 16
        self.add(eng, lambda E, o=out, i=in_, s=sem.h, kw=kw:
                 E.dma_start(out=o, in_=i, **kw).then_inc(s, 16))
        return sem.tok()

    def build(self):
        with self.nc.Block() as block:
            for e in self.ENGINES:
                ops = self.q[e]
                if not ops:
                    continue
                getattr(block, e)(lambda E, ops=ops: [f(E) for f in ops])
        for g in reversed(self._mem):
            g.__exit__(None, None, None)
        self._mem = []
        for g in reversed(self._ctx):
            g.__exit__(None, None, None)
        return self.nc


def make_consts(kb):
    g = kb.sem("s_consts")
    out = {}
    ident = kb.sbuf("c_ident", [128, 128], BF16)
    t0 = kb.op("gpsimd", lambda E, t=ident: E.memset(t[:, :], 0.0), g)
    kb.wait("gpsimd", t0)
    kb.op("gpsimd", lambda E, t=ident: E.affine_select(
        out=t[:, :], in_=t[:, :], compare_op=ALU.not_equal, fill=1.0,
        base=0, pattern=[[-1, 128]], channel_multiplier=1), g)
    out["ident"] = ident
    ones_col = kb.sbuf("c_ones_col", [128, 1], BF16)
    kb.op("gpsimd", lambda E, t=ones_col: E.memset(t[:, :], 1.0), g)
    out["ones_col"] = ones_col
    ones_row = kb.sbuf("c_ones_row", [1, 128], BF16)
    out["ones_row"] = ones_row
    kb.op("gpsimd", lambda E, t=ones_row: E.memset(t[:, :], 1.0), g)
    ones128 = kb.sbuf("c_ones128", [128, 128], BF16)
    out["ones128"] = ones128
    kb.op("gpsimd", lambda E, t=ones128: E.memset(t[:, :], 1.0), g)
    epsc = kb.sbuf("c_eps", [128, 1], F32)
    out["eps"] = epsc
    kb.op("gpsimd", lambda E, t=epsc: E.memset(t[:, :], EPS), g)
    out["tok"] = g.tok()
    return out


def load_wT(kb, sb, hbm, in_dim, sem, eng="gpsimd", cols=None):
    kc = in_dim // 128
    for k in range(kc):
        src = hbm[k * 128:(k + 1) * 128, :] if cols is None else \
            hbm[k * 128:(k + 1) * 128, cols[0]:cols[1]]
        tok = kb.dma(eng, sb[:, k, :], src, sem)
    return tok


def emit_transpose_fm(kb, dst, dst_cols, src_ap, ident, ps_tp, pp,
                      s_pe, s_ve, pre_waits_t=None, eng_ev="vector", s_ev=None):
    if pre_waits_t:
        kb.wait("tensor", pre_waits_t)
    tok_t = kb.op("tensor", lambda E: E.transpose(
        ps_tp[:, pp, 0:src_ap.shape[0]], src_ap, ident[:, :]), s_pe)
    kb.wait(eng_ev, tok_t)
    tok_v = kb.op(eng_ev, lambda E: E.tensor_copy(
        dst[:, dst_cols[0]:dst_cols[1]], ps_tp[:, pp, 0:src_ap.shape[0]]),
        s_ev or s_ve)
    return tok_t, tok_v



def build_pre0(stage="all", kb=None, io=None):
    own = kb is None
    if own:
        kb = LB("pre0")
        io = {}
    x_sl = io.get("x_sl") if "x_sl" in io else kb.din("x_sl", [T_SL, D])
    w_cq = io.get("w_cq") if "w_cq" in io else kb.din("w_cq", [D, 1536])
    w_ckv = io.get("w_ckv") if "w_ckv" in io else kb.din("w_ckv", [D, 512])
    w_kr = io.get("w_kr") if "w_kr" in io else kb.din("w_kr", [D, 64])
    qnw_in = io.get("q_norm_w") if "q_norm_w" in io else kb.din("q_norm_w", [1536])
    kvnw_in = io.get("kv_norm_w") if "kv_norm_w" in io else kb.din("kv_norm_w", [512])
    nq_out = io.get("nqT") if "nqT" in io else kb.dout("nqT", [1536, T_SL], BF16)
    nkv_out = io.get("nkvT") if "nkvT" in io else kb.dout("nkvT", [512, T_SL], BF16)
    kr_out = io.get("kropeT") if "kropeT" in io else kb.dout("kropeT", [64, T_SL], BF16)

    s_ld = kb.sem("s_ld")
    s_ldx = kb.sem("s_ldx")
    s_pe = kb.eauto("tensor")
    s_ve = kb.eauto("vector")
    s_se = kb.eauto("scalar")
    s_st = kb.sem("s_st")

    C = make_consts(kb)
    ident, ones128 = C["ident"], C["ones128"]

    xtm = kb.sbuf("xtm", [128, 8, D], BF16)
    xT = kb.sbuf("xT", [128, DC, T_SL], BF16)
    wcq_t = kb.sbuf("wcq_t", [128, DC, 1536], BF16)
    wckv_t = kb.sbuf("wckv_t", [128, DC, 512], BF16)
    wkr_t = kb.sbuf("wkr_t", [128, DC, 64], BF16)
    qnw = kb.sbuf("qnw", [128, 12], F32)
    kvnw = kb.sbuf("kvnw", [128, 4], F32)
    pre = kb.sbuf("pre", [128, 12, T_SL], BF16)
    tmp_ln = kb.sbuf("tmp_ln", [128, 512], F32)
    sq = kb.sbuf("sq", [128, 12, 512], BF16)
    rstd_bc = kb.sbuf("rstd_bc", [128, 2, 512], BF16)
    outbuf = kb.sbuf("outbuf", [128, 12, T_SL], BF16)
    ps_tp = kb.psum("ps_tp", [128, 2, 1024], BF16)
    ps_mm = kb.psum("ps_mm", [128, 2, 512], F32)
    ps_st = kb.psum("ps_st", [128, 2, 512], F32)

    t_x = None
    for i in range(8):
        t_x = kb.dma("gpsimd", xtm[:, i, :], x_sl[i * 128:(i + 1) * 128, :], s_ldx)
    load_wT(kb, wcq_t, w_cq, D, s_ld)
    load_wT(kb, wckv_t, w_ckv, D, s_ld)
    load_wT(kb, wkr_t, w_kr, D, s_ld)
    for j in range(12):
        kb.dma("gpsimd", qnw[:, j:j + 1], qnw_in[j * 128:(j + 1) * 128][:, None], s_ld)
    for j in range(4):
        kb.dma("gpsimd", kvnw[:, j:j + 1], kvnw_in[j * 128:(j + 1) * 128][:, None], s_ld)
    all_loads = s_ld.tok()

    kb.wait("tensor", t_x)
    kb.wait("tensor", C["tok"])
    ev_toks = [None, None]
    for idx in range(0 if stage == "lv6" else 8 * DC):
        i, k = divmod(idx, DC)
        pp = idx % 2
        kb.wait("tensor", ev_toks[pp])
        tok_t = kb.op("tensor", lambda E, i=i, k=k, pp=pp: E.transpose(
            ps_tp[:, pp, 0:128], xtm[:, i, k * 128:(k + 1) * 128], ident[:, :]), s_pe)
        kb.wait("vector", tok_t)
        ev_toks[pp] = kb.op("vector", lambda E, i=i, k=k, pp=pp: E.tensor_copy(
            xT[:, k, i * 128:(i + 1) * 128], ps_tp[:, pp, 0:128]), s_ve)
    xT_ready = s_ve.tok()

    def proj(wt, OC, nw_tile, out_ext, do_norm, mw=128, level=0):
        snap = [s_pe.tok(), s_ve.tok(), s_se.tok(), s_st.tok()]
        for e in ("tensor", "vector", "scalar"):
            kb.wait(e, snap)
        kb.wait("tensor", [xT_ready, all_loads])
        DIMN = OC * mw
        bank_free = [None, None]
        for si in range(2):
            tok_sq = None
            for j in range(OC):
                pp = j % 2
                kb.wait("tensor", bank_free[pp])
                def mm(E, j=j, si=si, pp=pp, wt=wt, mw=mw):
                    for k in range(DC - 1):
                        E.matmul(ps_mm[0:mw, pp, :],
                                 wt[:, k, j * mw:(j + 1) * mw],
                                 xT[:, k, si * 512:(si + 1) * 512],
                                 start=(k == 0), stop=False)
                    return E.matmul(ps_mm[0:mw, pp, :],
                                    wt[:, DC - 1, j * mw:(j + 1) * mw],
                                    xT[:, DC - 1, si * 512:(si + 1) * 512],
                                    start=False, stop=True)
                tok_mm = kb.op("tensor", mm)
                if do_norm:
                    kb.wait("scalar", tok_mm)
                    if level == 4:
                        tok_sq = kb.op("scalar", lambda E, j=j, pp=pp: E.copy(
                            sq[:, j, :], ps_mm[:, pp, :]))
                    elif level == 5:
                        tok_sq = tok_mm
                    else:
                        tok_sq = kb.op("scalar", lambda E, j=j, pp=pp: E.activation(
                            sq[:, j, :], ps_mm[:, pp, :], AF.Square))
                    kb.wait("vector", tok_sq)
                    tok_ev = kb.op("vector", lambda E, j=j, si=si, pp=pp:
                                   E.tensor_copy(pre[:, j, si * 512:(si + 1) * 512],
                                                 ps_mm[:, pp, :]))
                    bank_free[pp] = [tok_sq, tok_ev]
                else:
                    kb.wait("vector", tok_mm)
                    tok_ev = kb.op("vector", lambda E, si=si, pp=pp, mw=mw:
                                   E.tensor_copy(
                        outbuf[0:mw, 0, si * 512:(si + 1) * 512],
                        ps_mm[0:mw, pp, :]))
                    bank_free[pp] = [tok_ev]
            if not do_norm or level >= 3:
                continue
            kb.wait("tensor", tok_sq)
            def stats(E, si=si, OC=OC):
                for j in range(OC - 1):
                    E.matmul(ps_st[:, si, :], ones128[:, :], sq[:, j, :],
                             start=(j == 0), stop=False)
                return E.matmul(ps_st[:, si, :], ones128[:, :], sq[:, OC - 1, :],
                                start=(OC == 1), stop=True)
            tok_stats = kb.op("tensor", stats)
            kb.wait("scalar", tok_stats)
            kb.op("scalar", lambda E, si=si, DIMN=DIMN: E.activation(
                tmp_ln[:, :], ps_st[:, si, :], AF.Ln,
                bias=C["eps"][:, 0:1], scale=1.0 / DIMN))
            tok_r = kb.op("scalar", lambda E, si=si: E.activation(
                rstd_bc[:, si, :], tmp_ln[:, :], AF.Exp, scale=-0.5))
            kb.wait("vector", tok_r)
            for j in range(OC):
                if level >= 2:
                    kb.op("vector", lambda E, j=j, si=si: E.tensor_copy(
                        outbuf[:, j, si * 512:(si + 1) * 512],
                        pre[:, j, si * 512:(si + 1) * 512]))
                elif level == 1:
                    kb.op("vector", lambda E, j=j, si=si: E.tensor_mul(
                        outbuf[:, j, si * 512:(si + 1) * 512],
                        pre[:, j, si * 512:(si + 1) * 512],
                        rstd_bc[:, si, :]))
                else:
                    kb.op("vector", lambda E, j=j, si=si: E.scalar_tensor_tensor(
                        outbuf[:, j, si * 512:(si + 1) * 512],
                        pre[:, j, si * 512:(si + 1) * 512],
                        nw_tile[:, j:j + 1],
                        rstd_bc[:, si, :], ALU.mult, ALU.mult))
        tok_fin = kb.eauto("vector").tok()
        kb.wait("sync", tok_fin)
        t = None
        if do_norm:
            for j in range(OC):
                t = kb.dma("sync", out_ext[j * 128:(j + 1) * 128, :],
                           outbuf[:, j, 0:T_SL], s_st)
        else:
            t = kb.dma("sync", out_ext[:, :], outbuf[0:mw, 0, 0:T_SL], s_st)
        return t

    if stage == "tp":
        kb.wait("sync", xT_ready)
        kb.dma("sync", nq_out[0:128, :], xT[:, 0, 0:T_SL], s_st)
    elif stage == "krope":
        proj(wkr_t, 1, None, kr_out, False, mw=64)
    elif stage == "proj1":
        proj(wcq_t, 12, qnw, nq_out, True)
    elif stage == "lv6":
        proj(wcq_t, 12, qnw, nq_out, True, level=4)
    elif stage.startswith("lv"):
        proj(wcq_t, 12, qnw, nq_out, True, level=int(stage[2:]))
    else:
        proj(wcq_t, 12, qnw, nq_out, True)
        proj(wckv_t, 4, kvnw, nkv_out, True)
        proj(wkr_t, 1, None, kr_out, False, mw=64)
    kb.wait("sync", s_st.tok())
    return kb


def _v3(ap, rows=None):
    if len(ap.shape) == 3:
        return ap
    if ap.shape[1] == S:
        return ap.rearrange("a (r t) -> a r t", r=4)
    v = ap.rearrange("(r a) t -> a r t", r=4)
    return v if rows is None else v[rows[0]:rows[1]]



ASCALE = 1.0 / math.sqrt(192.0)
VSCALE = 1.0 / math.sqrt(2048.0)


def build_l0attn(n_heads=4, n_qt=32, kb=None, io=None):
    own = kb is None
    if own:
        kb = LB("l0attn")
        io = {}
    nq_in = _v3(io["nqT_full"] if "nqT_full" in io else
                kb.din("nqT_full", [1536, S], BF16))
    nkv_in = _v3(io["nkvT_full"] if "nkvT_full" in io else
                 kb.din("nkvT_full", [512, S], BF16))
    kr_in = _v3(io["kropeT_full"] if "kropeT_full" in io else
                kb.din("kropeT_full", [64, S], BF16))
    wdqn = io.get("wdqn_sl") if "wdqn_sl" in io else kb.din("wdqn_sl", [1536, 512])
    wdqr = io.get("wdqr_sl") if "wdqr_sl" in io else kb.din("wdqr_sl", [1536, 256])
    wdkn = io.get("wdkn_sl") if "wdkn_sl" in io else kb.din("wdkn_sl", [512, 512])
    wdv = io.get("wdv_sl") if "wdv_sl" in io else kb.din("wdv_sl", [512, 512])
    rc_in = io.get("rc") if "rc" in io else kb.din("rc", [4, 64, 64])
    rsp_in = io.get("rsp") if "rsp" in io else kb.din("rsp", [4, 64, 64])
    rsn_in = io.get("rsn") if "rsn" in io else kb.din("rsn", [4, 64, 64])
    attn_out = io.get("attnT") if "attnT" in io else kb.dout("attnT", [512, S], BF16)

    s_ldw = kb.sem("s_ldw")
    s_ldkv = kb.sem("s_ldkv")
    s_nq = [kb.sem("s_nq0"), kb.sem("s_nq1")]
    s_st = kb.sem("s_st")
    C = make_consts(kb)
    ident = C["ident"]
    cmask = kb.sbuf("cmask", [128, 128], BF16)
    kb.op("gpsimd", lambda E: E.memset(cmask[:, :], 0.0))
    tok_cm = kb.op("gpsimd", lambda E: E.affine_select(
        out=cmask[:, :], in_=cmask[:, :], compare_op=ALU.is_ge, fill=NEG,
        base=0, pattern=[[-1, 128]], channel_multiplier=1))

    nkvT = kb.sbuf("nkvT", [128, 4, S], BF16)
    kropeT = kb.sbuf("kropeT", [64, S], BF16)
    nqs = kb.sbuf("nqs", [128, 2, 12, 512], BF16)
    wdqn_t = kb.sbuf("wdqn_t", [128, 12, 512], BF16)
    wdqr_t = kb.sbuf("wdqr_t", [128, 12, 256], BF16)
    wdkn_t = kb.sbuf("wdkn_t", [128, 4, 512], BF16)
    wdv_t = kb.sbuf("wdv_t", [128, 4, 512], BF16)
    rcs = kb.sbuf("rcs", [64, 3, 4, 64], F32)
    rt = kb.sbuf("rt", [64, 4, 64], BF16)
    qnT = kb.sbuf("qnT", [128, S], BF16)
    qrT = kb.sbuf("qrT", [64, S], BF16)
    qr_raw = kb.sbuf("qr_raw", [64, S], BF16)
    knT = kb.sbuf("knT", [128, S], BF16)
    krh = kb.sbuf("krh", [64, S], BF16)
    v_tm = kb.sbuf("v_tm", [128, 32, 128], BF16)
    sc = kb.sbuf("sc", [128, S], F32)
    probs = kb.sbuf("probs", [128, S], BF16)
    probs2 = kb.sbuf("probs2", [128, S], BF16)
    pT = kb.sbuf("pT", [128, 32, 128], BF16)
    attn_h = kb.sbuf("attn_h", [128, S], BF16)
    cmax = kb.sbuf("cmax", [128, 8], F32)
    gmax = kb.sbuf("gmax", [128, 1], F32)
    negmax = kb.sbuf("negmax", [128, 1], F32)
    sums = kb.sbuf("sums", [128, 1], F32)
    rsum = kb.sbuf("rsum", [128, 1], F32)
    ps_mm = kb.psum("ps_mm", [128, 2, 512], F32)
    ps_tp = kb.psum("ps_tp", [128, 2, 1024], BF16)
    ps_pv = kb.psum("ps_pv", [128, 2, 512], F32)
    ps_x1 = kb.psum("ps_x1", [128, 1, 512], F32)

    for c4 in range(4):
        kb.dma("gpsimd", nkvT[:, c4, :].rearrange("p (r t) -> p r t", r=4),
               nkv_in[c4 * 128:(c4 + 1) * 128], s_ldkv)
    kb.dma("gpsimd", kropeT[:, :].rearrange("p (r t) -> p r t", r=4),
           kr_in[0:64], s_ldkv)
    load_wT(kb, wdqn_t, wdqn, 1536, s_ldw)
    load_wT(kb, wdqr_t, wdqr, 1536, s_ldw)
    load_wT(kb, wdkn_t, wdkn, 512, s_ldw)
    load_wT(kb, wdv_t, wdv, 512, s_ldw)
    for i, t in enumerate((rc_in, rsp_in, rsn_in)):
        for h in range(4):
            kb.dma("gpsimd", rcs[:, i, h, :], t[h, :, :], s_ldw)
    ldw_tok = s_ldw.tok()
    ldkv_tok = s_ldkv.tok()

    kb.wait("vector", ldw_tok)
    for h in range(4):
        kb.op("vector", lambda E, h=h: E.tensor_add(
            rt[:, h, :], rcs[:, 0, h, :], rcs[:, 1, h, :]))
        kb.op("vector", lambda E, h=h: E.tensor_sub(
            rt[:, h, :], rt[:, h, :], rcs[:, 2, h, :]))
    rt_tok = kb.eauto("vector").tok()

    nq_span_idx = 0
    nq_used = {}

    def head_attention(h):
        nonlocal nq_span_idx
        s_t, s_v, s_s = kb.eauto("tensor"), kb.eauto("vector"), kb.eauto("scalar")

        kb.wait("tensor", [ldw_tok, ldkv_tok, rt_tok, tok_cm])
        for sp in range(8):
            slot = nq_span_idx % 2
            prev = nq_used.get(slot)
            if prev is not None:
                kb.wait("sync", prev)
            for j in range(12):
                kb.dma("sync", nqs[:, slot, j, :],
                       nq_in[j * 128:(j + 1) * 128, sp // 2,
                             (sp % 2) * 512:(sp % 2) * 512 + 512],
                       s_nq[slot])
            kb.wait("tensor", s_nq[slot].tok())
            pp = sp % 2
            kb.wait("tensor", kb._l0_bank.get(("mm", pp)))
            def mmq(E, h=h, sp=sp, slot=slot, pp=pp):
                for j in range(11):
                    E.matmul(ps_mm[:, pp, :], wdqn_t[:, j, h * 128:(h + 1) * 128],
                             nqs[:, slot, j, :], start=(j == 0), stop=False)
                return E.matmul(ps_mm[:, pp, :], wdqn_t[:, 11, h * 128:(h + 1) * 128],
                                nqs[:, slot, 11, :], start=False, stop=True)
            tok_q = kb.op("tensor", mmq)
            kb.wait("vector", tok_q)
            ev_q = kb.op("vector", lambda E, sp=sp, pp=pp: E.tensor_scalar_mul(
                qnT[:, sp * 512:(sp + 1) * 512], ps_mm[:, pp, :], ASCALE))
            kb._l0_bank[("mm", pp)] = [ev_q]
            pp2 = sp % 2
            kb.wait("tensor", kb._l0_bank.get(("pv", pp2)))
            def mmqr(E, h=h, sp=sp, slot=slot, pp2=pp2):
                for j in range(11):
                    E.matmul(ps_pv[0:64, pp2, :], wdqr_t[:, j, h * 64:(h + 1) * 64],
                             nqs[:, slot, j, :], start=(j == 0), stop=False)
                return E.matmul(ps_pv[0:64, pp2, :], wdqr_t[:, 11, h * 64:(h + 1) * 64],
                                nqs[:, slot, 11, :], start=False, stop=True)
            tok_qr = kb.op("tensor", mmqr)
            nq_used[slot] = tok_qr
            nq_span_idx += 1
            kb.wait("vector", tok_qr)
            ev_qr = kb.op("vector", lambda E, sp=sp, pp2=pp2: E.tensor_copy(
                qr_raw[:, sp * 512:(sp + 1) * 512], ps_pv[0:64, pp2, :]))
            kb._l0_bank[("pv", pp2)] = [ev_qr]
        for sp in range(8):
            pp2 = sp % 2
            kb.wait("tensor", kb._l0_bank.get(("pv", pp2)))
            kb.wait("tensor", kb.eauto("vector").tok())
            tok_r = kb.op("tensor", lambda E, h=h, sp=sp, pp2=pp2: E.matmul(
                ps_pv[0:64, pp2, :], rt[:, h, :],
                qr_raw[:, sp * 512:(sp + 1) * 512], start=True, stop=True))
            kb.wait("vector", tok_r)
            ev = kb.op("vector", lambda E, sp=sp, pp2=pp2: E.tensor_scalar_mul(
                qrT[:, sp * 512:(sp + 1) * 512], ps_pv[0:64, pp2, :], ASCALE))
            kb._l0_bank[("pv", pp2)] = [ev]
        for sp in range(8):
            pp = sp % 2
            kb.wait("tensor", kb._l0_bank.get(("mm", pp)))
            def mmk(E, h=h, sp=sp, pp=pp):
                for j in range(3):
                    E.matmul(ps_mm[:, pp, :], wdkn_t[:, j, h * 128:(h + 1) * 128],
                             nkvT[:, j, sp * 512:(sp + 1) * 512],
                             start=(j == 0), stop=False)
                return E.matmul(ps_mm[:, pp, :], wdkn_t[:, 3, h * 128:(h + 1) * 128],
                                nkvT[:, 3, sp * 512:(sp + 1) * 512],
                                start=False, stop=True)
            tok_k = kb.op("tensor", mmk)
            kb.wait("vector", tok_k)
            ev_k = kb.op("vector", lambda E, sp=sp, pp=pp: E.tensor_copy(
                knT[:, sp * 512:(sp + 1) * 512], ps_mm[:, pp, :]))
            kb._l0_bank[("mm", pp)] = [ev_k]
            pp2 = sp % 2
            kb.wait("tensor", kb._l0_bank.get(("pv", pp2)))
            tok_kr = kb.op("tensor", lambda E, h=h, sp=sp, pp2=pp2: E.matmul(
                ps_pv[0:64, pp2, :], rt[:, h, :],
                kropeT[:, sp * 512:(sp + 1) * 512], start=True, stop=True))
            kb.wait("vector", tok_kr)
            ev_kr = kb.op("vector", lambda E, sp=sp, pp2=pp2: E.tensor_copy(
                krh[:, sp * 512:(sp + 1) * 512], ps_pv[0:64, pp2, :]))
            kb._l0_bank[("pv", pp2)] = [ev_kr]
        for tt in range(32):
            kb.wait("tensor", kb._l0_bank.get(("x1", 0)))
            def mmv(E, h=h, tt=tt):
                for j in range(3):
                    E.matmul(ps_x1[:, 0, 0:128], nkvT[:, j, tt * 128:(tt + 1) * 128],
                             wdv_t[:, j, h * 128:(h + 1) * 128],
                             start=(j == 0), stop=False)
                return E.matmul(ps_x1[:, 0, 0:128], nkvT[:, 3, tt * 128:(tt + 1) * 128],
                                wdv_t[:, 3, h * 128:(h + 1) * 128],
                                start=False, stop=True)
            tok_v = kb.op("tensor", mmv)
            kb.wait("vector", tok_v)
            ev_v = kb.op("vector", lambda E, tt=tt: E.tensor_scalar_mul(
                v_tm[:, tt, :], ps_x1[:, 0, 0:128], VSCALE))
            kb._l0_bank[("x1", 0)] = [ev_v]

        tok_exp_prev = None
        tok_n_prev = None
        tok_tp_prev = None
        for qt in range(n_qt):
            klen = (qt + 1) * 128
            nch = (klen + 511) // 512
            if tok_exp_prev is not None:
                kb.wait("vector", tok_exp_prev)
            for c in range(nch):
                w = min(512, klen - c * 512)
                pp = c % 2
                kb.wait("tensor", kb._l0_bank.get(("mm", pp)))
                def mmsc(E, h=h, qt=qt, c=c, w=w, pp=pp):
                    E.matmul(ps_mm[:, pp, 0:w],
                             qnT[:, qt * 128:(qt + 1) * 128],
                             knT[:, c * 512:c * 512 + w], start=True, stop=False)
                    return E.matmul(ps_mm[:, pp, 0:w],
                                    qrT[0:64, qt * 128:(qt + 1) * 128],
                                    krh[0:64, c * 512:c * 512 + w],
                                    start=False, stop=True)
                tok_s = kb.op("tensor", mmsc)
                kb.wait("vector", tok_s)
                if c == nch - 1:
                    if w > 128:
                        kb.op("vector", lambda E, c=c, w=w, pp=pp: E.tensor_copy(
                            sc[:, c * 512:c * 512 + w - 128], ps_mm[:, pp, 0:w - 128]))
                    kb.op("vector", lambda E, c=c, w=w, pp=pp: E.tensor_add(
                        sc[:, c * 512 + w - 128:c * 512 + w],
                        ps_mm[:, pp, w - 128:w], cmask[:, :]))
                else:
                    kb.op("vector", lambda E, c=c, w=w, pp=pp: E.tensor_copy(
                        sc[:, c * 512:c * 512 + w], ps_mm[:, pp, 0:w]))
                ev_s = kb.op("vector", lambda E, c=c, w=w: E.reduce_max(
                    cmax[:, c:c + 1], sc[:, c * 512:c * 512 + w], AX.X))
                kb._l0_bank[("mm", pp)] = [ev_s]
            kb.op("vector", lambda E, nch=nch: E.reduce_max(
                gmax[:, 0:1], cmax[:, 0:nch], AX.X))
            tok_nm = kb.op("vector", lambda E: E.tensor_scalar_mul(
                negmax[:, 0:1], gmax[:, 0:1], -1.0))
            kb.wait("scalar", tok_nm)
            kb.wait("scalar", tok_n_prev)
            tok_exp = kb.op("scalar", lambda E, klen=klen: E.activation(
                probs[:, 0:klen], sc[:, 0:klen], AF.Exp,
                bias=negmax[:, 0:1], accum_out=sums[:, 0:1]))
            tok_exp_prev = tok_exp
            kb.wait("vector", tok_exp)
            kb.op("vector", lambda E: E.reciprocal(rsum[:, 0:1], sums[:, 0:1]))
            kb.wait("vector", tok_tp_prev)
            tok_n = kb.op("vector", lambda E, klen=klen: E.tensor_scalar_mul(
                probs2[:, 0:klen], probs[:, 0:klen], rsum[:, 0:1]))
            tok_n_prev = tok_n
            for kt in range(qt + 1):
                ppt = kt % 2
                kb.wait("tensor", kb._l0_bank.get(("tp", ppt)))
                kb.wait("tensor", tok_n)
                tok_t = kb.op("tensor", lambda E, kt=kt, ppt=ppt: E.transpose(
                    ps_tp[:, ppt, 0:128], probs2[:, kt * 128:(kt + 1) * 128],
                    ident[:, :]))
                tok_tp_prev = tok_t
                kb.wait("vector", tok_t)
                ev_t = kb.op("vector", lambda E, kt=kt, ppt=ppt: E.tensor_copy(
                    pT[:, kt, :], ps_tp[:, ppt, 0:128]))
                kb._l0_bank[("tp", ppt)] = [ev_t]
            ppv = qt % 2
            kb.wait("tensor", kb._l0_bank.get(("pv", ppv)))
            kb.wait("tensor", kb.eauto("vector").tok())
            def mmpv(E, qt=qt, ppv=ppv):
                r = None
                for kt in range(qt + 1):
                    r = E.matmul(ps_pv[:, ppv, 0:128], v_tm[:, kt, :],
                                 pT[:, kt, :], start=(kt == 0), stop=(kt == qt))
                return r
            tok_pv = kb.op("tensor", mmpv)
            kb.wait("vector", tok_pv)
            ev_pv = kb.op("vector", lambda E, qt=qt, ppv=ppv: E.tensor_copy(
                attn_h[:, qt * 128:(qt + 1) * 128], ps_pv[:, ppv, 0:128]))
            kb._l0_bank[("pv", ppv)] = [ev_pv]
        kb.wait("sync", kb.eauto("vector").tok())
        kb.dma("sync", attn_out[h * 128:(h + 1) * 128, 0:n_qt * 128], attn_h[:, 0:n_qt * 128], s_st)
        kb._l0_bank[("attn_h",)] = s_st.tok()

    kb._l0_bank = {}
    for h in range(n_heads):
        prev_st = kb._l0_bank.get(("attn_h",))
        if prev_st is not None:
            kb.wait("vector", prev_st)
        head_attention(h)
    kb.wait("sync", s_st.tok())
    return kb



def build_post(name, with_proj, x_token_major, final=False, gating=False,
               n_parts=4, kb=None, io=None):
    own = kb is None
    if own:
        kb = LB(name)
        io = {}
    if with_proj:
        in_sl_d = io.get("inT_sl") if "inT_sl" in io else \
            kb.din("inT_sl", [D, T_SL], BF16)
        w_in = io.get("w") if "w" in io else kb.din("w", [D, D])
    else:
        parts = io.get("parts") if "parts" in io else \
            kb.din("parts", [n_parts, D, T_SL])
    if gating:
        gw_in = io.get("gate_w") if "gate_w" in io else kb.din("gate_w", [8, D])
        a2a_comb = io.get("a2a_combine")
        if a2a_comb is None:
            comb_out = io.get("combine") if "combine" in io else \
                kb.dout("combine", [T_SL, 8])
    if x_token_major:
        x_in = io.get("x_sl") if "x_sl" in io else kb.din("x_sl", [T_SL, D])
    else:
        x_in = io.get("xT_prev") if "xT_prev" in io else \
            kb.din("xT_prev", [D, T_SL])
    if final:
        out_f = io.get("x_out") if "x_out" in io else kb.dout("x_out", [T_SL, D])
    else:
        out_f = io.get("xT_new") if "xT_new" in io else \
            kb.dout("xT_new", [D, T_SL])
        out_b = io.get("xT_new_bf") if "xT_new_bf" in io else \
            kb.dout("xT_new_bf", [D, T_SL], BF16)

    s_ldw = kb.sem("s_ldw")
    s_lin = kb.sem("s_lin")
    s_st = kb.sem("s_st")
    s_xs = [kb.sem("s_xs0"), kb.sem("s_xs1")]
    C = make_consts(kb)
    ident, ones128 = C["ident"], C["ones128"]

    if with_proj:
        w_t = kb.sbuf("w_t", [128, DC, D], BF16)
        in_sl = kb.sbuf("in_sl", [128, DC, T_SL], BF16)
    else:
        parts_sb = kb.sbuf("parts_sb", [128, 2, 4, 512],
                           BF16 if (io and "parts" in io) else F32)
    x_sb = kb.sbuf("x_sb", [128, 2, 512], F32)
    if x_token_major:
        xtm_s = kb.sbuf("xtm_s", [128, 2, 4, 128], F32)
        identf = kb.sbuf("identf", [128, 128], F32)
        t0 = kb.op("gpsimd", lambda E: E.memset(identf[:, :], 0.0))
        kb.wait("gpsimd", t0)
        kb.op("gpsimd", lambda E: E.affine_select(
            out=identf[:, :], in_=identf[:, :], compare_op=ALU.not_equal,
            fill=1.0, base=0, pattern=[[-1, 128]], channel_multiplier=1))
        ps_tpx = kb.psum("ps_tpx", [128, 2, 512], F32)
    if final:
        identf = kb.sbuf("identf", [128, 128], F32)
        t0 = kb.op("gpsimd", lambda E: E.memset(identf[:, :], 0.0))
        kb.wait("gpsimd", t0)
        kb.op("gpsimd", lambda E: E.affine_select(
            out=identf[:, :], in_=identf[:, :], compare_op=ALU.not_equal,
            fill=1.0, base=0, pattern=[[-1, 128]], channel_multiplier=1))
        ps_tpf = kb.psum("ps_tpf", [128, 2, 512], F32)
        otm = kb.sbuf("otm", [128, 4, 128], F32)
    consts_tok = kb.eauto("gpsimd").tok()
    r_f = kb.sbuf("r_f", [128, DC, 512], F32)
    r_bf = kb.sbuf("r_bf", [128, DC, 512], BF16)
    sq = kb.sbuf("sq2", [128, DC, 512], BF16)
    mu = kb.sbuf("mu", [128, 512], F32)
    ms = kb.sbuf("ms", [128, 512], F32)
    var = kb.sbuf("var", [128, 512], F32)
    rstd = kb.sbuf("rstd", [128, 512], F32)
    scr = kb.sbuf("scr", [128, 512], F32)
    obf = kb.sbuf("obf", [128, 512], BF16)
    ps_mm = kb.psum("ps_mm2", [128, 2, 512], F32)
    ps_st = kb.psum("ps_st2", [128, 2, 512], F32)

    if gating:
        gw_sb = kb.sbuf("gw_sb", [8, D], F32)
        gwT = kb.sbuf("gwT", [128, DC, 8], F32)
        identg = kb.sbuf("identg", [128, 128], F32)
        tg = kb.op("gpsimd", lambda E: E.memset(identg[:, :], 0.0))
        kb.wait("gpsimd", tg)
        kb.op("gpsimd", lambda E: E.affine_select(
            out=identg[:, :], in_=identg[:, :], compare_op=ALU.not_equal,
            fill=1.0, base=0, pattern=[[-1, 128]], channel_multiplier=1))
        gconsts = kb.eauto("gpsimd").tok()
        s_gw = kb.sem("s_gw")
        kb.dma("gpsimd", gw_sb[:, :], gw_in[:, :], s_gw)
        lg = kb.sbuf("lg", [128, 8], F32)
        m1 = kb.sbuf("m1", [128, 1], F32)
        m2 = kb.sbuf("m2", [128, 1], F32)
        eqm = kb.sbuf("eqm", [128, 8], F32)
        l2t = kb.sbuf("l2t", [128, 8], F32)
        selt = kb.sbuf("selt", [128, 8], F32)
        negm1 = kb.sbuf("negm1", [128, 1], F32)
        expt = kb.sbuf("expt", [128, 8], F32)
        sumt = kb.sbuf("sumt", [128, 1], F32)
        rst = kb.sbuf("rst", [128, 1], F32)
        comb_sb = kb.sbuf("comb_sb", [128, 8], F32)
        ps_g = kb.psum("ps_g", [128, 1, 512], F32)
        kb.wait("tensor", [s_gw.tok(), gconsts])
        for k in range(DC):
            tok_t = kb.op("tensor", lambda E, k=k: E.transpose(
                ps_g[:, 0, 0:8], gw_sb[0:8, k * 128:(k + 1) * 128],
                identg[0:8, 0:8]))
            kb.wait("vector", tok_t)
            ev = kb.op("vector", lambda E, k=k: E.tensor_copy(
                gwT[:, k, :], ps_g[:, 0, 0:8]))
            kb.wait("tensor", ev)
        gwT_tok = kb.eauto("vector").tok()

        def emit_gating(si, u, r_f, bank, r_f_done):
            kb.wait("vector", s_st.tok())
            kb.wait("tensor", r_f_done)
            kb.wait("tensor", [gwT_tok, bank.get(("gt",))])
            def mmg(E, si=si, u=u):
                for k in range(DC - 1):
                    E.matmul(ps_g[:, 0, 0:8], r_f[:, k, u * 128:(u + 1) * 128],
                             gwT[:, k, :], start=(k == 0), stop=False)
                return E.matmul(ps_g[:, 0, 0:8],
                                r_f[:, DC - 1, u * 128:(u + 1) * 128],
                                gwT[:, DC - 1, :], start=False, stop=True)
            tok_g = kb.op("tensor", mmg)
            kb.wait("vector", tok_g)
            kb.op("vector", lambda E: E.tensor_copy(lg[:, :], ps_g[:, 0, 0:8]))
            kb.op("vector", lambda E: E.reduce_max(m1[:, 0:1], lg[:, :], AX.X))
            kb.op("vector", lambda E: E.tensor_single_scalar(
                eqm[:, :], lg[:, :], m1[:, 0:1], ALU.is_equal))
            kb.op("vector", lambda E: E.scalar_tensor_tensor(
                l2t[:, :], eqm[:, :], -1e30, lg[:, :], ALU.mult, ALU.add))
            kb.op("vector", lambda E: E.reduce_max(m2[:, 0:1], l2t[:, :], AX.X))
            kb.op("vector", lambda E: E.tensor_single_scalar(
                selt[:, :], lg[:, :], m2[:, 0:1], ALU.is_ge))
            tok_v = kb.op("vector", lambda E: E.tensor_scalar_mul(
                negm1[:, 0:1], m1[:, 0:1], -1.0))
            kb.wait("scalar", tok_v)
            tok_e = kb.op("scalar", lambda E: E.activation(
                expt[:, :], lg[:, :], AF.Exp, bias=negm1[:, 0:1]))
            kb.wait("vector", tok_e)
            kb.op("vector", lambda E: E.tensor_mul(l2t[:, :], expt[:, :],
                                                   selt[:, :]))
            kb.op("vector", lambda E: E.reduce_sum(sumt[:, 0:1], l2t[:, :], AX.X))
            kb.op("vector", lambda E: E.reciprocal(rst[:, 0:1], sumt[:, 0:1]))
            tok_c = kb.op("vector", lambda E: E.tensor_scalar_mul(
                comb_sb[:, :], l2t[:, :], rst[:, 0:1]))
            bank[("gt",)] = [tok_g, tok_c]
            kb.wait("sync", tok_c)
            if a2a_comb is not None:
                for jj in range(4):
                    kb.dma("sync",
                           a2a_comb[1024 * jj + si * 512 + u * 128:
                                    1024 * jj + si * 512 + (u + 1) * 128, :],
                           comb_sb[:, 2 * jj:2 * jj + 2], s_st)
            else:
                kb.dma("sync",
                       comb_out[si * 512 + u * 128:si * 512 + (u + 1) * 128, :],
                       comb_sb[:, :], s_st)

    if with_proj:
        load_wT(kb, w_t, w_in, D, s_ldw)
        for k in range(DC):
            kb.dma("gpsimd", in_sl[:, k, :], in_sl_d[k * 128:(k + 1) * 128, :],
                   s_lin)
    ldw_tok = s_ldw.tok()
    lin_tok = s_lin.tok()

    bank = {}
    x_used = {}
    for si in range(2):
        for j in range(DC):
            slot = (si * DC + j) % 2
            prev = x_used.get(slot)
            if prev is not None:
                kb.wait("sync", prev)
            if x_token_major:
                for u in range(4):
                    kb.dma("sync", xtm_s[:, slot, u, :],
                           x_in[si * 512 + u * 128:si * 512 + (u + 1) * 128,
                                j * 128:(j + 1) * 128], s_xs[slot])
            else:
                kb.dma("sync", x_sb[:, slot, 0:512],
                       x_in[j * 128:(j + 1) * 128, si * 512:(si + 1) * 512],
                       s_xs[slot])
            if not with_proj:
                for p4 in range(n_parts):
                    src_ap = parts[p4, j * 128:(j + 1) * 128,
                                   si * 512:(si + 1) * 512] \
                        if len(parts.shape) == 3 else \
                        parts[j * 128:(j + 1) * 128, si * 512:(si + 1) * 512]
                    kb.dma("sync", parts_sb[:, slot, p4, :], src_ap, s_xs[slot])
            if x_token_major:
                kb.wait("tensor", s_xs[slot].tok())
                kb.wait("tensor", consts_tok)
                for u in range(4):
                    ppt = u % 2
                    kb.wait("tensor", bank.get(("tpx", ppt)))
                    tok_t = kb.op("tensor", lambda E, slot=slot, u=u, ppt=ppt:
                                  E.transpose(ps_tpx[:, ppt, 0:128],
                                              xtm_s[:, slot, u, :], identf[:, :]))
                    kb.wait("vector", tok_t)
                    ev = kb.op("vector", lambda E, slot=slot, u=u, ppt=ppt:
                               E.tensor_copy(x_sb[:, slot, u * 128:(u + 1) * 128],
                                             ps_tpx[:, ppt, 0:128]))
                    bank[("tpx", ppt)] = [ev]
                x_ready = kb.eauto("vector").tok()
            else:
                kb.wait("vector", s_xs[slot].tok())
                x_ready = None
            if with_proj:
                pp = (si * DC + j) % 2
                kb.wait("tensor", [lin_tok, ldw_tok])
                kb.wait("tensor", bank.get(("mm", pp)))
                def mmp(E, j=j, si=si, pp=pp):
                    for k in range(DC - 1):
                        E.matmul(ps_mm[:, pp, :], w_t[:, k, j * 128:(j + 1) * 128],
                                 in_sl[:, k, si * 512:(si + 1) * 512],
                                 start=(k == 0), stop=False)
                    return E.matmul(ps_mm[:, pp, :],
                                    w_t[:, DC - 1, j * 128:(j + 1) * 128],
                                    in_sl[:, DC - 1, si * 512:(si + 1) * 512],
                                    start=False, stop=True)
                tok_mm = kb.op("tensor", mmp)
                kb.wait("vector", [tok_mm, x_ready])
                tok_r = kb.op("vector", lambda E, j=j, slot=slot, pp=pp:
                              E.tensor_add(r_f[:, j, :], ps_mm[:, pp, :],
                                           x_sb[:, slot, 0:512]))
                bank[("mm", pp)] = [tok_r]
            else:
                kb.wait("vector", s_xs[slot].tok())
                if x_ready is not None:
                    kb.wait("vector", x_ready)
                if n_parts == 1:
                    tok_r = kb.op("vector", lambda E, j=j, slot=slot: E.tensor_add(
                        r_f[:, j, :], x_sb[:, slot, 0:512],
                        parts_sb[:, slot, 0, :]))
                else:
                    kb.op("vector", lambda E, slot=slot: E.tensor_add(
                        scr[:, :], x_sb[:, slot, 0:512], parts_sb[:, slot, 0, :]))
                    kb.op("vector", lambda E, j=j, slot=slot: E.tensor_add(
                        r_f[:, j, :], scr[:, :], parts_sb[:, slot, 1, :]))
                    kb.op("vector", lambda E, j=j, slot=slot: E.tensor_add(
                        scr[:, :], r_f[:, j, :], parts_sb[:, slot, 2, :]))
                    tok_r = kb.op("vector", lambda E, j=j, slot=slot: E.tensor_add(
                        r_f[:, j, :], scr[:, :], parts_sb[:, slot, 3, :]))
            x_used[slot] = tok_r
            kb.op("vector", lambda E, j=j: E.tensor_copy(r_bf[:, j, :],
                                                         r_f[:, j, :]))
            kb.wait("scalar", kb.eauto("vector").tok())
            kb.op("scalar", lambda E, j=j: E.activation(sq[:, j, :], r_f[:, j, :],
                                                        AF.Square))
        sq_tok = kb.eauto("scalar").tok()
        rbf_tok = kb.eauto("vector").tok()
        kb.wait("tensor", [sq_tok, rbf_tok])
        def statsA(E):
            for j in range(DC - 1):
                E.matmul(ps_st[:, 0, :], ones128[:, :], r_bf[:, j, :],
                         start=(j == 0), stop=False)
            return E.matmul(ps_st[:, 0, :], ones128[:, :], r_bf[:, DC - 1, :],
                            start=False, stop=True)
        tokA = kb.op("tensor", statsA)
        def statsB(E):
            for j in range(DC - 1):
                E.matmul(ps_st[:, 1, :], ones128[:, :], sq[:, j, :],
                         start=(j == 0), stop=False)
            return E.matmul(ps_st[:, 1, :], ones128[:, :], sq[:, DC - 1, :],
                            start=False, stop=True)
        tokB = kb.op("tensor", statsB)
        kb.wait("vector", tokA)
        kb.op("vector", lambda E: E.tensor_scalar_mul(mu[:, :], ps_st[:, 0, :],
                                                      1.0 / D))
        kb.wait("vector", tokB)
        tok_ms = kb.op("vector", lambda E: E.tensor_scalar_mul(
            ms[:, :], ps_st[:, 1, :], 1.0 / D))
        kb.op("vector", lambda E: E.tensor_mul(scr[:, :], mu[:, :], mu[:, :]))
        tok_var = kb.op("vector", lambda E: E.tensor_sub(var[:, :], ms[:, :],
                                                         scr[:, :]))
        bank[("st",)] = [tok_ms]
        kb.wait("scalar", tok_var)
        kb.op("scalar", lambda E: E.activation(ms[:, :], var[:, :], AF.Ln,
                                               bias=C["eps"][:, 0:1]))
        tok_rstd = kb.op("scalar", lambda E: E.activation(
            rstd[:, :], ms[:, :], AF.Exp, scale=-0.5))
        kb.wait("vector", tok_rstd)
        for j in range(DC):
            kb.op("vector", lambda E, j=j: E.tensor_sub(scr[:, :], r_f[:, j, :],
                                                        mu[:, :]))
            tok_o = kb.op("vector", lambda E, j=j: E.tensor_mul(
                r_f[:, j, :], scr[:, :], rstd[:, :]))
            if final:
                for u in range(4):
                    ppt = u % 2
                    kb.wait("tensor", bank.get(("tpo", ppt)))
                    kb.wait("tensor", [tok_o, consts_tok])
                    tok_t = kb.op("tensor", lambda E, j=j, u=u, ppt=ppt:
                                  E.transpose(ps_tpf[:, ppt, 0:128],
                                              r_f[:, j, u * 128:(u + 1) * 128],
                                              identf[:, :]))
                    kb.wait("vector", tok_t)
                    ev = kb.op("vector", lambda E, j=j, u=u, ppt=ppt:
                               E.tensor_copy(otm[:, u, :], ps_tpf[:, ppt, 0:128]))
                    bank[("tpo", ppt)] = [ev]
                kb.wait("sync", kb.eauto("vector").tok())
                for u in range(4):
                    kb.dma("sync",
                           out_f[si * 512 + u * 128:si * 512 + (u + 1) * 128,
                                 j * 128:(j + 1) * 128], otm[:, u, :], s_st)
                kb.wait("vector", s_st.tok())
            else:
                tok_ob = kb.op("vector", lambda E, j=j: E.tensor_copy(
                    obf[:, :], r_f[:, j, :]))
                kb.wait("sync", tok_ob)
                kb.dma("sync", out_f[j * 128:(j + 1) * 128,
                                     si * 512:(si + 1) * 512], r_f[:, j, :], s_st)
                kb.dma("sync", out_b[j * 128:(j + 1) * 128,
                                     si * 512:(si + 1) * 512], obf[:, :], s_st)
                kb.wait("vector", s_st.tok())
        if gating:
            r_f_done = kb.eauto("vector").tok()
            for u in range(4):
                emit_gating(si, u, r_f, bank, r_f_done)
            kb.wait("vector", bank.get(("gt",)))
        kb.wait("tensor", bank.get(("st",)))
        kb.wait("vector", s_st.tok())
    kb.wait("sync", s_st.tok())
    return kb



def build_moe(n_spans=8, kb=None, io=None):
    own = kb is None
    if own:
        kb = LB("moe")
        io = {}
    x_in = _v3(io["xT_bf_full"] if "xT_bf_full" in io else
               kb.din("xT_bf_full", [D, S], BF16))
    w1_in = io.get("w1_sl") if "w1_sl" in io else kb.din("w1_sl", [2, 2816, D])
    w2_in = io.get("w2_sl") if "w2_sl" in io else kb.din("w2_sl", [2, D, 1408])
    comb_in = io.get("combine_sl") if "combine_sl" in io else \
        kb.din("combine_sl", [S, 8])
    esel_in = io.get("esel") if "esel" in io else kb.din("esel", [8, 2])
    out_p = _v3(io["moeP"] if "moeP" in io else kb.dout("moeP", [D, S]))

    s_cast = kb.sem("s_cast")
    s_w1 = kb.sem("s_w1")
    s_w2 = kb.sem("s_w2")
    s_x = [kb.sem("s_x0"), kb.sem("s_x1")]
    s_cb = kb.sem("s_cb")
    s_st = kb.sem("s_st")
    C = make_consts(kb)
    ident, ones_row = C["ident"], C["ones_row"]

    identf = kb.sbuf("identf_m", [128, 128], F32)
    tg = kb.op("gpsimd", lambda E: E.memset(identf[:, :], 0.0))
    kb.wait("gpsimd", tg)
    kb.op("gpsimd", lambda E: E.affine_select(
        out=identf[:, :], in_=identf[:, :], compare_op=ALU.not_equal,
        fill=1.0, base=0, pattern=[[-1, 128]], channel_multiplier=1))
    identf_tok = kb.eauto("gpsimd").tok()

    w1_bf = io.get("w1_bf") if "w1_bf" in io else kb.dram("w1_bf", [2, 2816, D], BF16)
    w2_bf = io.get("w2_bf") if "w2_bf" in io else kb.dram("w2_bf", [2, D, 1408], BF16)

    xs_sb = kb.sbuf("xs_sb", [128, 2, DC, 512], BF16)
    w1T = kb.sbuf("w1T", [128, DC, 1408], BF16)
    w2T = kb.sbuf("w2T", [128, 2, 11, D], BF16)
    sg = kb.sbuf("sg", [128, 11, 512], BF16)
    act = kb.sbuf("act", [128, 2, 11, 512], BF16)
    cb_sb = kb.sbuf("cb_sb", [128, 4, 8], F32)
    cbT_bf = kb.sbuf("cbT_bf", [8, 128], BF16)
    esel_bf = kb.sbuf("esel_bf", [8, 2], BF16)
    ce_row = kb.sbuf("ce_row", [1, 2, 512], BF16)
    cbc = kb.sbuf("cbc", [128, 2, 512], BF16)
    u_tmp = kb.sbuf("u_tmp", [128, 512], BF16)
    stg = kb.sbuf("stg", [128, 2, 512],
                  BF16 if (io and "moeP" in io) else F32)
    ps_gu = kb.psum("ps_gu", [128, 2, 512], F32)
    ps_dn = kb.psum("ps_dn", [128, 2, 512], F32)
    ps_cb = kb.psum("ps_cb", [128, 1, 512], F32)

    s_es = kb.sem("s_es")
    kb.dma("gpsimd", esel_bf[:, :], esel_in[:, :], s_es)
    esel_tok = s_es.tok()
    if "cast_tok" in io:
        cast_tok = io["cast_tok"]
    else:
        for e in range(2):
            for o4 in range(4):
                kb.dma("gpsimd", w1_bf[e, o4 * 704:(o4 + 1) * 704, :],
                       w1_in[e, o4 * 704:(o4 + 1) * 704, :], s_cast)
            for o4 in range(4):
                kb.dma("gpsimd", w2_bf[e, o4 * 512:(o4 + 1) * 512, :],
                       w2_in[e, o4 * 512:(o4 + 1) * 512, :], s_cast)
        cast_tok = s_cast.tok()
    kb.wait("sync", cast_tok)
    for e in range(2):
        for i in range(11):
            kb.dma("sync", w2T[:, e, i, :],
                   w2_bf[e, :, i * 128:(i + 1) * 128], s_w2, transpose=True)
    w2_tok = s_w2.tok()

    bank = {}
    st_idx = 0
    for sp in range(n_spans):
        slot = sp % 2
        prev = bank.get(("xuse", slot))
        if prev is not None:
            kb.wait("sync", prev)
        for k in range(DC):
            kb.dma("sync", xs_sb[:, slot, k, :],
                   x_in[k * 128:(k + 1) * 128, sp // 2,
                        (sp % 2) * 512:(sp % 2) * 512 + 512], s_x[slot])
        x_tok = s_x[slot].tok()
        kb.wait("sync", bank.get(("cbuse",)))
        kb.dma("sync", cb_sb[:, :, :],
               comb_in[sp * 512:(sp + 1) * 512, :].rearrange(
                   "(u p) e -> p u e", p=128), s_cb)
        cb_tok = s_cb.tok()
        kb.wait("tensor", [cb_tok, C["tok"], identf_tok, esel_tok])
        for u in range(4):
            tok_t = kb.op("tensor", lambda E, u=u: E.transpose(
                ps_cb[0:8, 0, 0:128], cb_sb[:, u, :], identf[:, :]))
            kb.wait("vector", tok_t)
            ev = kb.op("vector", lambda E, u=u: E.tensor_copy(
                cbT_bf[:, :], ps_cb[0:8, 0, 0:128]))
            kb.wait("tensor", ev)
            for e in range(2):
                tok_s = kb.op("tensor", lambda E, e=e: E.matmul(
                    ps_cb[0:1, 0, 0:128], esel_bf[:, e:e + 1], cbT_bf[:, :],
                    start=True, stop=True))
                kb.wait("vector", tok_s)
                ev2 = kb.op("vector", lambda E, e=e, u=u: E.tensor_copy(
                    ce_row[0:1, e, u * 128:(u + 1) * 128], ps_cb[0:1, 0, 0:128]))
                kb.wait("tensor", ev2)
        row_tok = kb.eauto("vector").tok()
        kb.wait("tensor", row_tok)
        for e in range(2):
            tok_b = kb.op("tensor", lambda E, e=e: E.matmul(
                ps_cb[:, 0, :], ones_row[:, :], ce_row[0:1, e, :],
                start=True, stop=True))
            kb.wait("vector", tok_b)
            evb = kb.op("vector", lambda E, e=e: E.tensor_copy(
                cbc[:, e, :], ps_cb[:, 0, :]))
            kb.wait("tensor", evb)
        cbc_tok = kb.eauto("vector").tok()
        bank[("cbuse",)] = cbc_tok

        for e in range(2):
            for half in range(2):
                kb.wait("sync", bank.get(("w1use",)))
                for k in range(DC):
                    kb.dma("sync", w1T[:, k, :],
                           w1_bf[e, half * 1408:(half + 1) * 1408,
                                 k * 128:(k + 1) * 128], s_w1, transpose=True)
                w1_tok = s_w1.tok()
                if half == 1:
                    kb.wait("vector", kb.eauto("scalar").tok())
                kb.wait("tensor", [w1_tok, x_tok, cbc_tok])
                for j in range(11):
                    pp = j % 2
                    kb.wait("tensor", bank.get(("gu", pp)))
                    def mmgu(E, j=j, pp=pp, slot=slot):
                        for k in range(DC - 1):
                            E.matmul(ps_gu[:, pp, :],
                                     w1T[:, k, j * 128:(j + 1) * 128],
                                     xs_sb[:, slot, k, :],
                                     start=(k == 0), stop=False)
                        return E.matmul(ps_gu[:, pp, :],
                                        w1T[:, DC - 1, j * 128:(j + 1) * 128],
                                        xs_sb[:, slot, DC - 1, :],
                                        start=False, stop=True)
                    tok_mm = kb.op("tensor", mmgu)
                    if half == 0:
                        kb.wait("scalar", tok_mm)
                        ev = kb.op("scalar", lambda E, j=j, pp=pp: E.activation(
                            sg[:, j, :], ps_gu[:, pp, :], AF.Silu))
                        bank[("gu", pp)] = [ev]
                    else:
                        kb.wait("vector", tok_mm)
                        kb.op("vector", lambda E, j=j, pp=pp:
                              E.tensor_mul(u_tmp[:, :], sg[:, j, :],
                                           ps_gu[:, pp, :]))
                        ev = kb.op("vector", lambda E, j=j, e=e: E.tensor_mul(
                            act[:, e, j, :], u_tmp[:, :], cbc[:, e, :]))
                        bank[("gu", pp)] = [ev]
                w1use = kb.eauto("scalar" if half == 0 else "vector").tok()
                bank[("w1use",)] = [w1use]
        act_tok = kb.eauto("vector").tok()
        sg_tok = kb.eauto("scalar").tok()

        kb.wait("tensor", [act_tok, w2_tok])
        for hc in range(DC):
            pp = hc % 2
            kb.wait("tensor", bank.get(("dn", pp)))
            def mmdn(E, hc=hc, pp=pp):
                r = None
                for e in range(2):
                    for i in range(11):
                        r = E.matmul(ps_dn[:, pp, :],
                                     w2T[:, e, i, hc * 128:(hc + 1) * 128],
                                     act[:, e, i, :],
                                     start=(e == 0 and i == 0),
                                     stop=(e == 1 and i == 10))
                return r
            tok_dn = kb.op("tensor", mmdn)
            kb.wait("vector", tok_dn)
            kb.wait("vector", s_st.tok())
            ev = kb.op("vector", lambda E, hc=hc, pp=pp: E.tensor_copy(
                stg[:, pp, :], ps_dn[:, pp, :]))
            bank[("dn", pp)] = [ev]
            kb.wait("sync", ev)
            kb.dma("sync", out_p[hc * 128:(hc + 1) * 128, sp // 2,
                                 (sp % 2) * 512:(sp % 2) * 512 + 512],
                   stg[:, pp, :], s_st)
        bank[("xuse", slot)] = kb.eauto("tensor").tok()
    kb.wait("sync", s_st.tok())
    return kb



def build_l1ffn(n_spans=8, kb=None, io=None):
    own = kb is None
    if own:
        kb = LB("l1ffn")
        io = {}
    x_in = _v3(io["xT_bf_full"] if "xT_bf_full" in io else
               kb.din("xT_bf_full", [D, S], BF16))
    wg_in = io.get("wg_sl") if "wg_sl" in io else kb.din("wg_sl", [D, D])
    wu_in = io.get("wu_sl") if "wu_sl" in io else kb.din("wu_sl", [D, D])
    wd_in = io.get("wd_sl") if "wd_sl" in io else kb.din("wd_sl", [D, D])
    out_p = _v3(io["ffnP"] if "ffnP" in io else kb.dout("ffnP", [D, S]))

    s_cast = kb.sem("s_cast")
    s_w = kb.sem("s_w")
    s_x = [kb.sem("s_x0"), kb.sem("s_x1")]
    s_st = kb.sem("s_st")
    C = make_consts(kb)

    wg_bf = io.get("wg_bf") if "wg_bf" in io else kb.dram("wg_bf", [D, D], BF16)
    wu_bf = io.get("wu_bf") if "wu_bf" in io else kb.dram("wu_bf", [D, D], BF16)
    wd_bf = io.get("wd_bf") if "wd_bf" in io else kb.dram("wd_bf", [D, D], BF16)

    xs_sb = kb.sbuf("xs_sb", [128, 2, DC, 512], BF16)
    wT = kb.sbuf("wT", [128, 2, DC, D], BF16)
    sg = kb.sbuf("sg", [128, DC, 512], BF16)
    act = kb.sbuf("act", [128, DC, 512], BF16)
    u_tmp = kb.sbuf("u_tmp", [128, 512], BF16)
    stg = kb.sbuf("stg", [128, 2, 512],
                  BF16 if (io and "ffnP" in io) else F32)
    ps_mm = kb.psum("ps_mm", [128, 2, 512], F32)

    if "cast_tok" in io:
        cast_tok = io["cast_tok"]
    else:
        for w_in, w_bf in ((wg_in, wg_bf), (wu_in, wu_bf), (wd_in, wd_bf)):
            for o4 in range(4):
                kb.dma("gpsimd", w_bf[o4 * 512:(o4 + 1) * 512, :],
                       w_in[o4 * 512:(o4 + 1) * 512, :], s_cast)
        cast_tok = s_cast.tok()
    kb.wait("sync", cast_tok)

    bank = {}
    s_ws = [s_w, kb.sem("s_wb")]
    steps = [(sp, ph) for sp in range(n_spans) for ph in range(3)]
    wuse = {}

    def issue_wload(i):
        if i >= len(steps):
            return
        sp, ph = steps[i]
        ws = i % 2
        kb.wait("sync", wuse.get(ws))
        w_bf = (wg_bf, wu_bf, wd_bf)[ph]
        for k in range(DC):
            kb.dma("sync", wT[:, ws, k, :], w_bf[:, k * 128:(k + 1) * 128],
                   s_ws[ws], transpose=True)

    issue_wload(0)
    issue_wload(1)
    for i, (sp, phase) in enumerate(steps):
        slot = sp % 2
        ws = i % 2
        if phase == 0:
            prev = bank.get(("xuse", slot))
            if prev is not None:
                kb.wait("sync", prev)
            for k in range(DC):
                kb.dma("sync", xs_sb[:, slot, k, :],
                       x_in[k * 128:(k + 1) * 128, sp // 2,
                            (sp % 2) * 512:(sp % 2) * 512 + 512], s_x[slot])
        x_tok = s_x[slot].tok()
        kb.wait("tensor", [s_ws[ws].tok(), x_tok])
        if phase == 0:
            kb.wait("scalar", kb.eauto("vector").tok())
        if phase == 2:
            kb.wait("tensor", kb.eauto("vector").tok())
        for j in range(DC):
            pp = j % 2
            kb.wait("tensor", bank.get(("mm", pp)))
            if phase < 2:
                def mmgu(E, j=j, pp=pp, slot=slot, ws=ws):
                    for k in range(DC - 1):
                        E.matmul(ps_mm[:, pp, :],
                                 wT[:, ws, k, j * 128:(j + 1) * 128],
                                 xs_sb[:, slot, k, :],
                                 start=(k == 0), stop=False)
                    return E.matmul(ps_mm[:, pp, :],
                                    wT[:, ws, DC - 1, j * 128:(j + 1) * 128],
                                    xs_sb[:, slot, DC - 1, :],
                                    start=False, stop=True)
                tok_mm = kb.op("tensor", mmgu)
                if phase == 0:
                    kb.wait("scalar", tok_mm)
                    ev = kb.op("scalar", lambda E, j=j, pp=pp: E.activation(
                        sg[:, j, :], ps_mm[:, pp, :], AF.Silu))
                else:
                    kb.wait("vector", tok_mm)
                    ev = kb.op("vector", lambda E, j=j, pp=pp: E.tensor_mul(
                        act[:, j, :], sg[:, j, :], ps_mm[:, pp, :]))
                bank[("mm", pp)] = [ev]
            else:
                def mmdn(E, j=j, pp=pp, ws=ws):
                    for k in range(DC - 1):
                        E.matmul(ps_mm[:, pp, :],
                                 wT[:, ws, k, j * 128:(j + 1) * 128],
                                 act[:, k, :], start=(k == 0), stop=False)
                    return E.matmul(ps_mm[:, pp, :],
                                    wT[:, ws, DC - 1, j * 128:(j + 1) * 128],
                                    act[:, DC - 1, :],
                                    start=False, stop=True)
                tok_mm = kb.op("tensor", mmdn)
                kb.wait("vector", tok_mm)
                kb.wait("vector", s_st.tok())
                ev = kb.op("vector", lambda E, j=j, pp=pp: E.tensor_copy(
                    stg[:, pp, :], ps_mm[:, pp, :]))
                bank[("mm", pp)] = [ev]
                kb.wait("sync", ev)
                kb.dma("sync", out_p[j * 128:(j + 1) * 128, sp // 2,
                                     (sp % 2) * 512:(sp % 2) * 512 + 512],
                       stg[:, pp, :], s_st)
        if phase == 0:
            kb.wait("vector", kb.eauto("scalar").tok())
        wuse[ws] = kb.eauto("tensor").tok()
        issue_wload(i + 2)
        if phase == 2:
            bank[("xuse", slot)] = kb.eauto("tensor").tok()
    kb.wait("sync", s_st.tok())
    return kb



def build_l1attn(n_heads=4, n_chunks=32, n_spans=8, kb=None, io=None):
    own = kb is None
    if own:
        kb = LB("l1attn")
        io = {}
    x_in = _v3(io["xT_bf_full"] if "xT_bf_full" in io else
               kb.din("xT_bf_full", [D, S], BF16))
    wq_in = io.get("wq_sl") if "wq_sl" in io else kb.din("wq_sl", [D, 512])
    wk_in = io.get("wk_sl") if "wk_sl" in io else kb.din("wk_sl", [D, 512])
    wv_in = io.get("wv_sl") if "wv_sl" in io else kb.din("wv_sl", [D, 512])
    wg_in = io.get("wg_sl") if "wg_sl" in io else kb.din("wg_sl", [D, 512])
    wa_in = io.get("wa_sl") if "wa_sl" in io else kb.din("wa_sl", [D, 4])
    wb_in = io.get("wb_sl") if "wb_sl" in io else kb.din("wb_sl", [D, 4])
    ba_in = io.get("ba_sl") if "ba_sl" in io else kb.din("ba_sl", [4])
    bb_in = io.get("bb_sl") if "bb_sl" in io else kb.din("bb_sl", [4])
    qcw_in = io.get("qcw_sl") if "qcw_sl" in io else kb.din("qcw_sl", [512, 4])
    qcb_in = io.get("qcb_sl") if "qcb_sl" in io else kb.din("qcb_sl", [512])
    kcw_in = io.get("kcw_sl") if "kcw_sl" in io else kb.din("kcw_sl", [512, 4])
    kcb_in = io.get("kcb_sl") if "kcb_sl" in io else kb.din("kcb_sl", [512])
    vcw_in = io.get("vcw_sl") if "vcw_sl" in io else kb.din("vcw_sl", [512, 4])
    vcb_in = io.get("vcb_sl") if "vcb_sl" in io else kb.din("vcb_sl", [512])
    lng_in = io.get("ln_g") if "ln_g" in io else kb.din("ln_g", [128])
    lnb_in = io.get("ln_b") if "ln_b" in io else kb.din("ln_b", [128])
    og_out = io.get("ogT") if "ogT" in io else kb.dout("ogT", [512, S], BF16)

    s_ldw = kb.sem("s_ldw")
    s_wh = kb.sem("s_wh")
    s_x = [kb.sem("s_x0"), kb.sem("s_x1")]
    s_st = kb.sem("s_st")
    C = make_consts(kb)
    ident, ones128 = C["ident"], C["ones128"]

    utonesf = kb.sbuf("utonesf", [128, 128], F32)
    maskA = kb.sbuf("maskA", [128, 128], F32)
    maskB = kb.sbuf("maskB", [128, 128], F32)
    zerof = kb.sbuf("zerof", [128, 128], F32)
    onef = kb.sbuf("onef", [128, 1], F32)
    identf = kb.sbuf("identfL", [128, 128], F32)
    gcst = kb.sem("s_gcst")
    for tile, fill, op, val in (
        (utonesf, 0.0, ALU.is_gt, 1.0),
        (maskA, 0.0, ALU.is_gt, NEG),
        (maskB, NEG, ALU.is_gt, 0.0),
    ):
        t0 = kb.op("gpsimd", lambda E, t=tile, f=fill: E.memset(t[:, :], f), gcst)
        kb.wait("gpsimd", t0)
        kb.op("gpsimd", lambda E, t=tile, o=op, v=val: E.affine_select(
            out=t[:, :], in_=t[:, :], compare_op=o, fill=v,
            base=0, pattern=[[-1, 128]], channel_multiplier=1), gcst)
    kb.op("gpsimd", lambda E: E.memset(zerof[:, :], 0.0), gcst)
    kb.op("gpsimd", lambda E: E.memset(onef[:, :], 1.0), gcst)
    t0i = kb.op("gpsimd", lambda E: E.memset(identf[:, :], 0.0), gcst)
    kb.wait("gpsimd", t0i)
    kb.op("gpsimd", lambda E: E.affine_select(
        out=identf[:, :], in_=identf[:, :], compare_op=ALU.not_equal,
        fill=1.0, base=0, pattern=[[-1, 128]], channel_multiplier=1), gcst)
    gconst_tok = kb.eauto("gpsimd").tok()

    xs = kb.sbuf("xs", [128, 2, DC, 512], BF16)
    wq_t = kb.sbuf("wq_t", [128, DC, 128], BF16)
    wk_t = kb.sbuf("wk_t", [128, DC, 128], BF16)
    wv_t = kb.sbuf("wv_t", [128, DC, 128], BF16)
    wg_t = kb.sbuf("wg_t", [128, DC, 128], BF16)
    wa_t = kb.sbuf("wa_t", [128, DC, 4], BF16)
    wb_t = kb.sbuf("wb_t", [128, DC, 4], BF16)
    bacol = kb.sbuf("bacol", [4, 1], F32)
    bbcol = kb.sbuf("bbcol", [4, 1], F32)
    qcw = kb.sbuf("qcw", [128, 4, 4], F32)
    qcb = kb.sbuf("qcb", [128, 4], F32)
    kcw = kb.sbuf("kcw", [128, 4, 4], F32)
    kcb = kb.sbuf("kcb", [128, 4], F32)
    vcw = kb.sbuf("vcw", [128, 4, 4], F32)
    vcb = kb.sbuf("vcb", [128, 4], F32)
    lngc = kb.sbuf("lngc", [128, 1], F32)
    lnbc = kb.sbuf("lnbc", [128, 1], F32)
    zq = kb.sbuf("zq", [4, 512], F32)
    zb = kb.sbuf("zb", [4, 512], F32)
    zs1 = kb.sbuf("zs1", [4, 512], F32)
    zs2 = kb.sbuf("zs2", [4, 512], F32)
    la_sp = kb.sbuf("la_sp", [4, 512], F32)
    b_sp = kb.sbuf("b_sp", [4, 512], F32)
    labtm = kb.sbuf("labtm", [128, 32, 8], F32)
    gtm = kb.sbuf("gtm", [128, 32, 4], F32)
    rawq = kb.sbuf("rawq", [128, S + 3], BF16)
    rawk = kb.sbuf("rawk", [128, S + 3], BF16)
    rawv = kb.sbuf("rawv", [128, S + 3], BF16)
    cq = kb.sbuf("cq", [128, S], BF16)
    ck = kb.sbuf("ck", [128, S], BF16)
    cv_ = kb.sbuf("cv_", [128, S], BF16)
    gateh = kb.sbuf("gateh", [128, S], BF16)
    ogsb = kb.sbuf("ogsb", [128, 2, S], BF16)
    cvs = kb.sbuf("cvs", [128, 2, 512], F32)
    kss = kb.sbuf("kss", [128, 512], BF16)
    gbc_s = kb.sbuf("gbc_s", [128, 128], F32)
    escr = kb.sbuf("escr", [128, 128], F32)
    dn_bf = kb.sbuf("dn_bf", [128, 128], BF16)
    nt_bf = kb.sbuf("nt_bf", [128, 128], BF16)
    t1n = kb.sbuf("t1n", [128, 2, 128], BF16)
    t2n = kb.sbuf("t2n", [128, 2, 128], BF16)
    xsol = kb.sbuf("xsol", [128, 2, 128], BF16)
    dft_bf = kb.sbuf("dft_bf", [128, 128], BF16)
    sht_bf = kb.sbuf("sht_bf", [128, 128], BF16)
    vtm_bf = kb.sbuf("vtm_bf", [128, 128], BF16)
    ktm_bf = kb.sbuf("ktm_bf", [128, 128], BF16)
    qg_bf = kb.sbuf("qg_bf", [128, 128], BF16)
    gambc = kb.sbuf("gambc", [128, 128], BF16)
    uyl_bf = kb.sbuf("uyl_bf", [128, 128], BF16)
    r_bf = kb.sbuf("r_bf", [128, 128], BF16)
    scrR = kb.sbuf("scrR", [128, 128], F32)
    colv = kb.sbuf("colv", [128, 8], F32)
    labc = kb.sbuf("labc", [128, 128], F32)
    st_f = kb.sbuf("st_f", [128, 4, 2, 128], F32)
    st_bf = kb.sbuf("st_bf", [128, 128], BF16)
    o_s = kb.sbuf("o_s", [128, 128], F32)
    sqo = kb.sbuf("sqo", [128, 128], BF16)
    o_bf = kb.sbuf("o_bf", [128, 128], BF16)
    muo = kb.sbuf("muo", [128, 128], F32)
    mso = kb.sbuf("mso", [128, 128], F32)
    varo = kb.sbuf("varo", [128, 128], F32)
    rstdo = kb.sbuf("rstdo", [128, 128], F32)
    oscr = kb.sbuf("oscr", [128, 128], F32)
    ps_pj = kb.psum("ps_pj", [128, 2, 512], F32)
    ps_c = kb.psum("ps_c", [128, 1, 512], F32)
    ps_d = kb.psum("ps_d", [128, 1, 512], F32)
    ps_e = kb.psum("ps_e", [128, 1, 512], F32)
    ps_tp = kb.psum("ps_tpL", [128, 2, 1024], BF16)
    ps_x = kb.psum("ps_x", [128, 1, 512], F32)

    load_wT(kb, wa_t, wa_in, D, s_ldw)
    load_wT(kb, wb_t, wb_in, D, s_ldw)
    kb.dma("gpsimd", bacol[:, :], ba_in[:][:, None], s_ldw)
    kb.dma("gpsimd", bbcol[:, :], bb_in[:][:, None], s_ldw)
    for cw, cb, cwi, cbi in ((qcw, qcb, qcw_in, qcb_in),
                             (kcw, kcb, kcw_in, kcb_in),
                             (vcw, vcb, vcw_in, vcb_in)):
        for h in range(4):
            kb.dma("gpsimd", cw[:, h, :], cwi[h * 128:(h + 1) * 128, :], s_ldw)
            kb.dma("gpsimd", cb[:, h:h + 1], cbi[h * 128:(h + 1) * 128][:, None],
                   s_ldw)
    kb.dma("gpsimd", lngc[:, :], lng_in[:][:, None], s_ldw)
    kb.dma("gpsimd", lnbc[:, :], lnb_in[:][:, None], s_ldw)
    ldw_tok = s_ldw.tok()

    bank = {}

    for sp in range(n_spans):
        slot = sp % 2
        prev = bank.get(("xuse", slot))
        if prev is not None:
            kb.wait("sync", prev)
        for k in range(DC):
            kb.dma("sync", xs[:, slot, k, :],
                   x_in[k * 128:(k + 1) * 128, sp // 2,
                        (sp % 2) * 512:(sp % 2) * 512 + 512], s_x[slot])
        x_tok = s_x[slot].tok()
        kb.wait("tensor", [x_tok, ldw_tok, gconst_tok, C["tok"]])
        for which, wt, zt in ((0, wa_t, zq), (1, wb_t, zb)):
            pp = which
            kb.wait("tensor", bank.get(("pj", pp)))
            def mmab(E, wt=wt, pp=pp, slot=slot):
                for k in range(DC - 1):
                    E.matmul(ps_pj[0:4, pp, :], wt[:, k, :], xs[:, slot, k, :],
                             start=(k == 0), stop=False)
                return E.matmul(ps_pj[0:4, pp, :], wt[:, DC - 1, :],
                                xs[:, slot, DC - 1, :], start=False, stop=True)
            tok_mm = kb.op("tensor", mmab)
            kb.wait("vector", tok_mm)
            ev = kb.op("vector", lambda E, zt=zt, pp=pp, bc=(bacol if which == 0
                                                            else bbcol):
                       E.tensor_single_scalar(zt[:, :], ps_pj[0:4, pp, :],
                                              bc[:, 0:1], ALU.add))
            bank[("pj", pp)] = [ev]
        kb.wait("scalar", kb.eauto("vector").tok())
        kb.op("scalar", lambda E: E.activation(zs1[:, :], zq[:, :], AF.Abs))
        kb.op("scalar", lambda E: E.activation(zs2[:, :], zs1[:, :], AF.Exp,
                                               scale=-1.0))
        tok_ln = kb.op("scalar", lambda E: E.activation(
            zs1[:, :], zs2[:, :], AF.Ln, bias=onef[0:4, 0:1]))
        kb.wait("vector", tok_ln)
        kb.op("vector", lambda E: E.tensor_scalar_min(zs2[:, :], zq[:, :], 0.0))
        tok_la = kb.op("vector", lambda E: E.tensor_sub(la_sp[:, :], zs2[:, :],
                                                        zs1[:, :]))
        tok_b = kb.op("scalar", lambda E: E.activation(
            b_sp[:, :], zb[:, :], AF.Sigmoid))
        kb.wait("tensor", [tok_la, tok_b])
        for u in range(4):
            ch = sp * 4 + u
            tok_t = kb.op("tensor", lambda E, u=u: E.transpose(
                ps_c[0:128, 0, 0:4], la_sp[0:4, u * 128:(u + 1) * 128],
                identf[0:4, 0:4]))
            kb.wait("vector", tok_t)
            ev = kb.op("vector", lambda E, ch=ch: E.tensor_copy(
                labtm[:, ch, 0:4], ps_c[0:128, 0, 0:4]))
            kb.wait("tensor", ev)
            tok_t = kb.op("tensor", lambda E, u=u: E.transpose(
                ps_c[0:128, 0, 0:4], b_sp[0:4, u * 128:(u + 1) * 128],
                identf[0:4, 0:4]))
            kb.wait("vector", tok_t)
            ev = kb.op("vector", lambda E, ch=ch: E.tensor_copy(
                labtm[:, ch, 4:8], ps_c[0:128, 0, 0:4]))
            kb.wait("tensor", ev)
            tok_g = kb.op("tensor", lambda E, ch=ch: E.matmul(
                ps_c[:, 0, 0:4], utonesf[:, :], labtm[:, ch, 0:4],
                start=True, stop=True))
            kb.wait("vector", tok_g)
            ev = kb.op("vector", lambda E, ch=ch: E.tensor_copy(
                gtm[:, ch, :], ps_c[:, 0, 0:4]))
            kb.wait("tensor", ev)
        bank[("xuse", slot)] = kb.eauto("tensor").tok()
    labtm_tok = kb.eauto("vector").tok()

    for h in range(n_heads):
        kb.wait("scalar", kb.eauto("vector").tok())
        kb.wait("gpsimd", bank.get(("whuse",)))
        for wt, wi in ((wq_t, wq_in), (wk_t, wk_in), (wv_t, wv_in), (wg_t, wg_in)):
            load_wT(kb, wt, wi, D, s_wh, cols=(h * 128, (h + 1) * 128))
        wh_tok = s_wh.tok()
        if h == 0:
            for raw in (rawq, rawk, rawv):
                kb.op("gpsimd", lambda E, raw=raw: E.memset(raw[:, 0:3], 0.0))
            pad_tok = kb.eauto("gpsimd").tok()
        for sp in range(n_spans):
            slot = sp % 2
            prev = bank.get(("xuse", slot))
            if prev is not None:
                kb.wait("sync", prev)
            for k in range(DC):
                kb.dma("sync", xs[:, slot, k, :],
                       x_in[k * 128:(k + 1) * 128, sp // 2,
                            (sp % 2) * 512:(sp % 2) * 512 + 512],
                       s_x[slot])
            x_tok = s_x[slot].tok()
            kb.wait("tensor", [x_tok, wh_tok, pad_tok])
            for wi, (wt, dst) in enumerate(((wq_t, rawq), (wk_t, rawk),
                                            (wv_t, rawv), (wg_t, None))):
                pp = wi % 2
                kb.wait("tensor", bank.get(("pj", pp)))
                def mmp(E, wt=wt, pp=pp, slot=slot):
                    for k in range(DC - 1):
                        E.matmul(ps_pj[:, pp, :], wt[:, k, :], xs[:, slot, k, :],
                                 start=(k == 0), stop=False)
                    return E.matmul(ps_pj[:, pp, :], wt[:, DC - 1, :],
                                    xs[:, slot, DC - 1, :],
                                    start=False, stop=True)
                tok_mm = kb.op("tensor", mmp)
                if dst is None:
                    kb.wait("scalar", tok_mm)
                    ev = kb.op("scalar", lambda E, sp=sp, pp=pp: E.activation(
                        gateh[:, sp * 512:(sp + 1) * 512], ps_pj[:, pp, :],
                        AF.Sigmoid))
                else:
                    kb.wait("vector", tok_mm)
                    ev = kb.op("vector", lambda E, dst=dst, sp=sp, pp=pp:
                               E.tensor_copy(dst[:, 3 + sp * 512:3 + (sp + 1) * 512],
                                             ps_pj[:, pp, :]))
                bank[("pj", pp)] = [ev]
            raw_tok = kb.eauto("vector").tok()
            for raw, cw, cb, cout in ((rawq, qcw, qcb, cq), (rawk, kcw, kcb, ck),
                                      (rawv, vcw, vcb, cv_)):
                kb.wait("vector", bank.get(("cvsilu",)))
                kb.op("vector", lambda E, raw=raw, cw=cw, sp=sp, h=h:
                      E.tensor_single_scalar(
                          cvs[:, 0, :], raw[:, sp * 512:sp * 512 + 512],
                          cw[:, h, 0:1], ALU.mult))
                for tau in range(1, 4):
                    src_pp, dst_pp = (tau - 1) % 2, tau % 2
                    kb.op("vector", lambda E, raw=raw, cw=cw, sp=sp, h=h,
                          tau=tau, a=src_pp, b2=dst_pp: E.scalar_tensor_tensor(
                              cvs[:, b2, :],
                              raw[:, sp * 512 + tau:sp * 512 + tau + 512],
                              cw[:, h, tau:tau + 1], cvs[:, a, :],
                              ALU.mult, ALU.add))
                tok_cv = kb.eauto("vector").tok()
                kb.wait("scalar", tok_cv)
                if cout is ck:
                    tok_sl = kb.op("scalar", lambda E, cb=cb, h=h: E.activation(
                        kss[:, :], cvs[:, 1, :], AF.Silu, bias=cb[:, h:h + 1]))
                    kb.wait("vector", tok_sl)
                    kb.op("vector", lambda E, sp=sp: E.tensor_scalar_mul(
                        ck[:, sp * 512:(sp + 1) * 512], kss[:, :],
                        1.0 / math.sqrt(128.0)))
                    bank[("cvsilu",)] = [tok_sl]
                else:
                    tok_sl = kb.op("scalar", lambda E, cout=cout, cb=cb, sp=sp, h=h:
                          E.activation(cout[:, sp * 512:(sp + 1) * 512],
                                       cvs[:, 1, :], AF.Silu,
                                       bias=cb[:, h:h + 1]))
                    bank[("cvsilu",)] = [tok_sl]
            bank[("xuse", slot)] = kb.eauto("tensor").tok()
        conv_tok = [kb.eauto("vector").tok(), kb.eauto("scalar").tok()]

        kb.wait("tensor", conv_tok + [labtm_tok])
        stpp = 0
        kb.op("vector", lambda E, h=h: E.memset(st_f[:, h, 0, :], 0.0))
        tok_st0 = kb.op("vector", lambda E: E.memset(st_bf[:, :], 0.0))
        st_tok = tok_st0
        for c in range(n_chunks):
            t0 = c * 128
            kb.wait("vector", bank.get(("colv",)))
            kb.op("vector", lambda E, c=c, h=h: E.tensor_copy(
                colv[:, 0:1], labtm[:, c, h:h + 1]))
            kb.op("vector", lambda E, c=c, h=h: E.tensor_copy(
                colv[:, 1:2], gtm[:, c, h:h + 1]))
            kb.op("vector", lambda E: E.tensor_sub(
                colv[:, 2:3], colv[:, 1:2], colv[:, 0:1]))
            tok_col = kb.eauto("vector").tok()
            kb.wait("scalar", tok_col)
            tok_gp = kb.op("scalar", lambda E: E.activation(
                colv[:, 3:4], colv[:, 2:3], AF.Exp))
            kb.wait("vector", tok_gp)
            tok_bgp = kb.op("vector", lambda E, c=c, h=h: E.tensor_mul(
                colv[:, 4:5], labtm[:, c, 4 + h:5 + h], colv[:, 3:4]))
            kb.wait("vector", bank.get(("labc",)))
            tok_labc = kb.op("vector", lambda E: E.tensor_single_scalar(
                labc[:, :], zerof[:, :], colv[:, 0:1], ALU.add))
            kb.wait("tensor", [tok_labc, bank.get(("ps_c",))])
            tok_gbc = kb.op("tensor", lambda E: E.matmul(
                ps_c[:, 0, 0:128], labc[:, :], utonesf[:, :],
                start=True, stop=True))
            bank[("labc",)] = [tok_gbc]
            kb.wait("vector", tok_gbc)
            tok_gbcs = kb.op("vector", lambda E: E.tensor_copy(
                gbc_s[:, :], ps_c[:, 0, 0:128]))
            bank[("ps_c",)] = [tok_gbcs]
            kb.op("vector", lambda E: E.tensor_sub(
                colv[:, 7:8], gbc_s[:, 127:128], colv[:, 1:2]))
            tok_l1 = kb.eauto("vector").tok()
            kb.wait("scalar", tok_l1)
            kb.op("scalar", lambda E: E.activation(colv[:, 5:6], colv[:, 7:8],
                                                   AF.Exp))
            tok_gamC = kb.op("scalar", lambda E: E.activation(
                colv[:, 6:7], gbc_s[:, 127:128], AF.Exp))
            bank[("colv",)] = [tok_gamC]
            kb.wait("scalar", bank.get(("gambc",)))
            tok_gam = kb.op("scalar", lambda E: E.activation(
                gambc[:, :], gbc_s[:, :], AF.Exp))
            kb.wait("tensor", bank.get(("pj", 0)))
            tok_gram = kb.op("tensor", lambda E, t0=t0: E.matmul(
                ps_pj[:, 0, 0:128], ck[:, t0:t0 + 128], ck[:, t0:t0 + 128],
                start=True, stop=True))
            kb.wait("tensor", bank.get(("pj", 1)))
            tok_kq = kb.op("tensor", lambda E, t0=t0: E.matmul(
                ps_pj[:, 1, 0:128], ck[:, t0:t0 + 128], cq[:, t0:t0 + 128],
                start=True, stop=True))
            kb.wait("vector", [tok_gbcs])
            kb.op("vector", lambda E: E.tensor_scalar(
                escr[:, :], gbc_s[:, :], -1.0, colv[:, 2:3],
                ALU.mult, ALU.add))
            tok_escr = kb.op("vector", lambda E: E.tensor_add(
                escr[:, :], escr[:, :], maskA[:, :]))
            kb.wait("scalar", tok_escr)
            tok_dn = kb.op("scalar", lambda E: E.activation(
                dn_bf[:, :], escr[:, :], AF.Exp))
            kb.wait("vector", [tok_dn, tok_gram, tok_bgp])
            tok_nt = kb.op("vector", lambda E: E.scalar_tensor_tensor(
                nt_bf[:, :], dn_bf[:, :], labtm[:, c, 4 + h:5 + h],
                ps_pj[:, 0, 0:128], ALU.mult, ALU.mult))
            bank[("pj", 0)] = [tok_nt]
            kb.wait("tensor", [tok_nt, bank.get(("tp", 0))])
            tok_tN = kb.op("tensor", lambda E: E.transpose(
                ps_tp[:, 0, 0:128], nt_bf[:, :], ident[:, :]))
            kb.wait("scalar", tok_tN)
            tok_t2 = kb.op("scalar", lambda E: E.activation(
                t2n[:, 0, :], ps_tp[:, 0, 0:128], AF.Identity, scale=-1.0))
            bank[("tp", 0)] = [tok_t2]
            kb.wait("vector", tok_nt)
            tok_t1 = kb.op("vector", lambda E: E.tensor_scalar_mul(
                t1n[:, 0, :], nt_bf[:, :], -1.0))
            kb.wait("vector", tok_dn)
            kb.op("vector", lambda E: E.tensor_scalar(
                escr[:, :], gbc_s[:, :], 1.0, colv[:, 1:2],
                ALU.mult, ALU.subtract))
            tok_es2 = kb.op("vector", lambda E: E.tensor_add(
                escr[:, :], escr[:, :], maskB[:, :]))
            kb.wait("scalar", tok_es2)
            tok_dft = kb.op("scalar", lambda E: E.activation(
                dft_bf[:, :], escr[:, :], AF.Exp))
            kb.wait("vector", [tok_dft, tok_kq])
            tok_sht = kb.op("vector", lambda E: E.tensor_mul(
                sht_bf[:, :], dft_bf[:, :], ps_pj[:, 1, 0:128]))
            bank[("pj", 1)] = [tok_sht]
            kb.wait("tensor", bank.get(("tp", 1)))
            tok_tV = kb.op("tensor", lambda E, t0=t0: E.transpose(
                ps_tp[:, 1, 0:128], cv_[:, t0:t0 + 128], ident[:, :]))
            kb.wait("vector", tok_tV)
            tok_vtm = kb.op("vector", lambda E: E.tensor_copy(
                vtm_bf[:, :], ps_tp[:, 1, 0:128]))
            bank[("tp", 1)] = [tok_vtm]
            kb.wait("tensor", bank.get(("tp", 0)))
            tok_tK = kb.op("tensor", lambda E, t0=t0: E.transpose(
                ps_tp[:, 0, 0:128], ck[:, t0:t0 + 128], ident[:, :]))
            kb.wait("vector", tok_tK)
            tok_ktm = kb.op("vector", lambda E: E.tensor_copy(
                ktm_bf[:, :], ps_tp[:, 0, 0:128]))
            bank[("tp", 0)] = [tok_ktm]
            kb.wait("tensor", [st_tok, bank.get(("ps_d",))])
            tok_kst = kb.op("tensor", lambda E, t0=t0: E.matmul(
                ps_d[:, 0, 0:128], ck[:, t0:t0 + 128], st_bf[:, :],
                start=True, stop=True))
            kb.wait("vector", tok_kst)
            kb.op("vector", lambda E: E.tensor_scalar_mul(
                scrR[:, :], ps_d[:, 0, 0:128], colv[:, 4:5]))
            bank[("ps_d",)] = [kb.eauto("vector").tok()]
            tok_rhs = kb.op("vector", lambda E, c=c, h=h: E.scalar_tensor_tensor(
                r_bf[:, :], vtm_bf[:, :], labtm[:, c, 4 + h:5 + h],
                scrR[:, :], ALU.mult, ALU.subtract))
            kb.op("vector", lambda E: E.tensor_copy(xsol[:, 0, :], r_bf[:, :]))
            x_cur, x_nxt = 0, 1
            p_cur, p_nxt = 0, 1
            kb.wait("tensor", kb.eauto("vector").tok())
            kb.wait("tensor", [tok_t2])
            tok_xa = None
            tok_e1 = None
            for lvl in range(7):
                kb.wait("tensor", tok_xa)
                kb.wait("tensor", tok_e1)
                tok_xu = kb.op("tensor", lambda E, p_cur=p_cur, x_cur=x_cur:
                               E.matmul(ps_x[:, 0, 0:128], t2n[:, p_cur, :],
                                        xsol[:, x_cur, :],
                                        start=True, stop=True))
                kb.wait("vector", tok_xu)
                tok_xa = kb.op("vector", lambda E, x_cur=x_cur, x_nxt=x_nxt:
                               E.tensor_add(xsol[:, x_nxt, :],
                                            xsol[:, x_cur, :],
                                            ps_x[:, 0, 0:128]))
                if lvl < 6:
                    kb.wait("tensor", bank.get(("ps_d",)))
                    tok_s1 = kb.op("tensor", lambda E, p_cur=p_cur: E.matmul(
                        ps_d[:, 0, 0:128], t1n[:, p_cur, :], t2n[:, p_cur, :],
                        start=True, stop=True))
                    kb.wait("scalar", tok_s1)
                    tok_e1 = kb.op("scalar", lambda E, p_nxt=p_nxt: E.copy(
                        t2n[:, p_nxt, :], ps_d[:, 0, 0:128]))
                    kb.wait("tensor", tok_e1)
                    tok_s2 = kb.op("tensor", lambda E, p_cur=p_cur: E.matmul(
                        ps_d[:, 0, 0:128], t2n[:, p_cur, :], t1n[:, p_cur, :],
                        start=True, stop=True))
                    kb.wait("scalar", tok_s2)
                    tok_e2 = kb.op("scalar", lambda E, p_nxt=p_nxt: E.copy(
                        t1n[:, p_nxt, :], ps_d[:, 0, 0:128]))
                    bank[("ps_d",)] = [tok_e2]
                    p_cur, p_nxt = p_nxt, p_cur
                x_cur, x_nxt = x_nxt, x_cur
            uy_tok = kb.eauto("vector").tok()
            kb.wait("vector", [uy_tok, tok_gamC])
            tok_uyl = kb.op("vector", lambda E, x_cur=x_cur: E.tensor_scalar_mul(
                uyl_bf[:, :], xsol[:, x_cur, :], colv[:, 5:6]))
            kb.wait("vector", [tok_gam])
            tok_qg = kb.op("vector", lambda E, t0=t0: E.tensor_mul(
                qg_bf[:, :], cq[:, t0:t0 + 128], gambc[:, :]))
            bank[("gambc",)] = [tok_qg]
            kb.wait("tensor", [tok_qg, tok_sht, bank.get(("ps_c",)),
                               kb.eauto("vector").tok()])
            def mmo(E, x_cur=x_cur):
                E.matmul(ps_c[:, 0, 0:128], st_bf[:, :], qg_bf[:, :],
                         start=True, stop=False)
                return E.matmul(ps_c[:, 0, 0:128], xsol[:, x_cur, :],
                                sht_bf[:, :], start=False, stop=True)
            tok_o = kb.op("tensor", mmo)
            kb.wait("scalar", tok_o)
            tok_oev = kb.op("scalar", lambda E: E.copy(o_s[:, :],
                                                       ps_c[:, 0, 0:128]))
            bank[("ps_c",)] = [tok_oev]
            kb.wait("tensor", [tok_uyl, tok_ktm, bank.get(("ps_e",))])
            tok_smm = kb.op("tensor", lambda E: E.matmul(
                ps_e[:, 0, 0:128], ktm_bf[:, :], uyl_bf[:, :],
                start=True, stop=True))
            kb.wait("vector", [tok_smm, tok_gamC])
            tok_stn = kb.op("vector", lambda E, h=h, stpp=stpp: E.scalar_tensor_tensor(
                st_f[:, h, 1 - stpp, :], st_f[:, h, stpp, :], colv[:, 6:7],
                ps_e[:, 0, 0:128], ALU.mult, ALU.add))
            bank[("ps_e",)] = [tok_stn]
            kb.wait("vector", tok_o)
            tok_stbf = kb.op("vector", lambda E, h=h, stpp=stpp: E.tensor_copy(
                st_bf[:, :], st_f[:, h, 1 - stpp, :]))
            stpp = 1 - stpp
            st_tok = tok_stbf
            kb.wait("scalar", tok_oev)
            tok_sqo = kb.op("scalar", lambda E: E.activation(
                sqo[:, :], o_s[:, :], AF.Square))
            kb.wait("vector", tok_oev)
            tok_obf = kb.op("vector", lambda E: E.tensor_copy(o_bf[:, :],
                                                              o_s[:, :]))
            kb.wait("tensor", [tok_sqo, tok_obf, bank.get(("ps_d",)),
                               bank.get(("ps_e",))])
            tok_mA = kb.op("tensor", lambda E: E.matmul(
                ps_d[:, 0, 0:128], ones128[:, :], o_bf[:, :],
                start=True, stop=True))
            tok_mB = kb.op("tensor", lambda E: E.matmul(
                ps_e[:, 0, 128:256], ones128[:, :], sqo[:, :],
                start=True, stop=True))
            kb.wait("vector", tok_mA)
            kb.op("vector", lambda E: E.tensor_scalar_mul(
                muo[:, :], ps_d[:, 0, 0:128], 1.0 / 128.0))
            bank[("ps_d",)] = [kb.eauto("vector").tok()]
            kb.wait("vector", tok_mB)
            kb.op("vector", lambda E: E.tensor_scalar_mul(
                mso[:, :], ps_e[:, 0, 128:256], 1.0 / 128.0))
            bank[("ps_e",)] = [kb.eauto("vector").tok()]
            kb.op("vector", lambda E: E.tensor_mul(oscr[:, :], muo[:, :],
                                                   muo[:, :]))
            tok_var = kb.op("vector", lambda E: E.tensor_sub(
                varo[:, :], mso[:, :], oscr[:, :]))
            kb.wait("scalar", tok_var)
            kb.op("scalar", lambda E: E.activation(mso[:, :], varo[:, :],
                                                   AF.Ln, bias=C["eps"][:, 0:1]))
            tok_rstd = kb.op("scalar", lambda E: E.activation(
                rstdo[:, :], mso[:, :], AF.Exp, scale=-0.5))
            kb.wait("vector", tok_rstd)
            kb.op("vector", lambda E: E.tensor_sub(oscr[:, :], o_s[:, :],
                                                   muo[:, :]))
            kb.op("vector", lambda E: E.tensor_mul(varo[:, :], oscr[:, :],
                                                   rstdo[:, :]))
            kb.op("vector", lambda E: E.tensor_scalar(
                oscr[:, :], varo[:, :], lngc[:, 0:1], lnbc[:, 0:1],
                ALU.mult, ALU.add))
            tok_og = kb.op("vector", lambda E, t0=t0, h=h: E.tensor_mul(
                ogsb[:, h % 2, t0:t0 + 128], oscr[:, :],
                gateh[:, t0:t0 + 128]))
        kb.wait("sync", kb.eauto("vector").tok())
        kb.dma("sync", og_out[h * 128:(h + 1) * 128, 0:n_chunks * 128],
               ogsb[:, h % 2, 0:n_chunks * 128], s_st)
        bank[("whuse",)] = kb.eauto("tensor").tok()
        kb.wait("vector", s_st.tok())
    kb.wait("sync", s_st.tok())
    return kb




def _run(kb, in_maps, **kw):
    nc = kb.build()
    return run_bass_kernel_spmd(nc, in_maps, core_ids=list(range(8)), **kw)



def emit_projpart(kb, io, name):
    w_in = io["w_sl"]
    in_loc = io["inT_loc"]
    out_p = _v3(io["partial"])
    s_w = kb.sem(f"s_w_{name}")
    s_i = [kb.sem(f"s_i0_{name}"), kb.sem(f"s_i1_{name}")]
    s_st = kb.sem(f"s_st_{name}")
    wT = kb.sbuf("ppw", [128, 4, D], BF16)
    isb = kb.sbuf("ppi", [128, 2, 4, 512], BF16)
    stg = kb.sbuf("ppstg", [128, 2, 512], BF16)
    ps = kb.psum("pp_ps", [128, 2, 512], F32)
    load_wT(kb, wT, w_in, 512, s_w)
    w_tok = s_w.tok()
    bank = {}
    for sp in range(8):
        slot = sp % 2
        prev = bank.get(("iuse", slot))
        if prev is not None:
            kb.wait("sync", prev)
        for k in range(4):
            kb.dma("sync", isb[:, slot, k, :],
                   in_loc[k * 128:(k + 1) * 128, sp * 512:(sp + 1) * 512],
                   s_i[slot])
        i_tok = s_i[slot].tok()
        kb.wait("tensor", [w_tok, i_tok])
        for j in range(DC):
            pp = j % 2
            kb.wait("tensor", bank.get(("mm", pp)))
            def mmp(E, j=j, pp=pp, slot=slot):
                for k in range(3):
                    E.matmul(ps[:, pp, :], wT[:, k, j * 128:(j + 1) * 128],
                             isb[:, slot, k, :], start=(k == 0), stop=False)
                return E.matmul(ps[:, pp, :], wT[:, 3, j * 128:(j + 1) * 128],
                                isb[:, slot, 3, :], start=False, stop=True)
            tok_mm = kb.op("tensor", mmp)
            kb.wait("vector", [tok_mm, s_st.tok()])
            ev = kb.op("vector", lambda E, pp=pp: E.tensor_copy(
                stg[:, pp, :], ps[:, pp, :]))
            bank[("mm", pp)] = [ev]
            kb.wait("sync", ev)
            kb.dma("sync", out_p[j * 128:(j + 1) * 128, sp // 2,
                                 (sp % 2) * 512:(sp % 2) * 512 + 512],
                   stg[:, pp, :], s_st)
        bank[("iuse", slot)] = kb.eauto("tensor").tok()
    kb.wait("sync", s_st.tok())


RG4 = [[0, 1, 2, 3], [4, 5, 6, 7]]


def build_fused():
    kb = LB("fused")
    d = kb.din
    ins = {
        "x_sl": d("x_sl", [T_SL, D]),
        "w_cq": d("w_cq", [D, 1536]), "w_ckv": d("w_ckv", [D, 512]),
        "w_kr": d("w_kr", [D, 64]),
        "q_norm_w": d("q_norm_w", [1536]), "kv_norm_w": d("kv_norm_w", [512]),
        "wdqn_sl": d("wdqn_sl", [1536, 512]), "wdqr_sl": d("wdqr_sl", [1536, 256]),
        "wdkn_sl": d("wdkn_sl", [512, 512]), "wdv_sl": d("wdv_sl", [512, 512]),
        "rc": d("rc", [4, 64, 64]), "rsp": d("rsp", [4, 64, 64]),
        "rsn": d("rsn", [4, 64, 64]),
        "wproj_sl": d("wproj_sl", [512, D]),
        "gate_w": d("gate_w", [8, D]),
        "w1_sl": d("w1_sl", [2, 2816, D]), "w2_sl": d("w2_sl", [2, D, 1408]),
        "esel": d("esel", [8, 2]),
        "wq_sl": d("wq_sl", [D, 512]), "wk_sl": d("wk_sl", [D, 512]),
        "wv_sl": d("wv_sl", [D, 512]), "wg_sl": d("wg_sl", [D, 512]),
        "wa_sl": d("wa_sl", [D, 4]), "wb_sl": d("wb_sl", [D, 4]),
        "ba_sl": d("ba_sl", [4]), "bb_sl": d("bb_sl", [4]),
        "qcw_sl": d("qcw_sl", [512, 4]), "qcb_sl": d("qcb_sl", [512]),
        "kcw_sl": d("kcw_sl", [512, 4]), "kcb_sl": d("kcb_sl", [512]),
        "vcw_sl": d("vcw_sl", [512, 4]), "vcb_sl": d("vcb_sl", [512]),
        "ln_g": d("ln_g", [128]), "ln_b": d("ln_b", [128]),
        "wo_sl": d("wo_sl", [512, D]),
        "wgate_sl": d("wgate_sl", [D, D]), "wup_sl": d("wup_sl", [D, D]),
        "wdown_sl": d("wdown_sl", [D, D]),
    }
    x_out = kb.dout("x_out", [T_SL, D])
    di = kb.dram
    ag1_in = di("ag1_in", [2112, T_SL], BF16)
    ag1_out = di("ag1_out", [4 * 2112, T_SL], BF16)
    attn_loc = di("attn_loc", [512, S], BF16)
    rs1_in = di("rs1_in", [4 * D, T_SL], BF16)
    rs1_out = di("rs1_out", [D, T_SL], BF16)
    x1 = di("x1", [D, T_SL], F32)
    ag2_in = di("ag2_in", [D, T_SL], BF16)
    ag2_out = di("ag2_out", [4 * D, T_SL], BF16)
    cb_ag_in = di("cb_ag_in", [T_SL, 8], F32)
    cb_ag_out = di("cb_ag_out", [S, 8], F32)
    rs2_in = di("rs2_in", [4 * D, T_SL], BF16)
    rs2_out = di("rs2_out", [D, T_SL], BF16)
    x2 = di("x2", [D, T_SL], F32)
    ag3_in = di("ag3_in", [D, T_SL], BF16)
    ag3_out = di("ag3_out", [4 * D, T_SL], BF16)
    og_loc = di("og_loc", [512, S], BF16)
    rs3_in = di("rs3_in", [4 * D, T_SL], BF16)
    rs3_out = di("rs3_out", [D, T_SL], BF16)
    x3 = di("x3", [D, T_SL], F32)
    ag4_in = di("ag4_in", [D, T_SL], BF16)
    ag4_out = di("ag4_out", [4 * D, T_SL], BF16)
    rs4_in = di("rs4_in", [4 * D, T_SL], BF16)
    rs4_out = di("rs4_out", [D, T_SL], BF16)

    w1_bf = di("w1_bfF", [2, 2816, D], BF16)
    w2_bf = di("w2_bfF", [2, D, 1408], BF16)
    wgf_bf = di("wg_bfF", [D, D], BF16)
    wuf_bf = di("wu_bfF", [D, D], BF16)
    wdf_bf = di("wd_bfF", [D, D], BF16)
    s_cast0 = kb.sem("s_cast0")
    s_cast0.barrier_exempt = True
    for e in range(2):
        for o4 in range(4):
            kb.dma("gpsimd", w1_bf[e, o4 * 704:(o4 + 1) * 704, :],
                   ins["w1_sl"][e, o4 * 704:(o4 + 1) * 704, :], s_cast0)
        for o4 in range(4):
            kb.dma("gpsimd", w2_bf[e, o4 * 512:(o4 + 1) * 512, :],
                   ins["w2_sl"][e, o4 * 512:(o4 + 1) * 512, :], s_cast0)
    for w_i, w_b in ((ins["wgate_sl"], wgf_bf), (ins["wup_sl"], wuf_bf),
                     (ins["wdown_sl"], wdf_bf)):
        for o4 in range(4):
            kb.dma("gpsimd", w_b[o4 * 512:(o4 + 1) * 512, :],
                   w_i[o4 * 512:(o4 + 1) * 512, :], s_cast0)
    cast0_tok = s_cast0.tok()

    def coll(kind, op, i, o):
        kb.barrier()
        kb.op("gpsimd", lambda E, i=i, o=o, kind=kind, op=op: E.collective_compute(
            kind, op, replica_groups=RG4, ins=[i[:, :].opt()],
            outs=[o[:, :].opt()]))
        kb.barrier()

    with kb.scope():
        build_pre0(kb=kb, io={
            "x_sl": ins["x_sl"], "w_cq": ins["w_cq"], "w_ckv": ins["w_ckv"],
            "w_kr": ins["w_kr"], "q_norm_w": ins["q_norm_w"],
            "kv_norm_w": ins["kv_norm_w"],
            "nqT": ag1_in[0:1536, :], "nkvT": ag1_in[1536:2048, :],
            "kropeT": ag1_in[2048:2112, :]})
    coll("AllGather", ALU.bypass, ag1_in, ag1_out)
    with kb.scope():
        build_l0attn(kb=kb, io={
            "nqT_full": _v3(ag1_out)[0:1536], "nkvT_full": _v3(ag1_out)[1536:2048],
            "kropeT_full": _v3(ag1_out)[2048:2112],
            "wdqn_sl": ins["wdqn_sl"], "wdqr_sl": ins["wdqr_sl"],
            "wdkn_sl": ins["wdkn_sl"], "wdv_sl": ins["wdv_sl"],
            "rc": ins["rc"], "rsp": ins["rsp"], "rsn": ins["rsn"],
            "attnT": attn_loc})
    kb.barrier()
    with kb.scope():
        emit_projpart(kb, {"w_sl": ins["wproj_sl"], "inT_loc": attn_loc,
                           "partial": rs1_in}, "p1")
    coll("ReduceScatter", ALU.add, rs1_in, rs1_out)
    with kb.scope():
        build_post("l0post", False, True, gating=True, n_parts=1, kb=kb, io={
            "parts": rs1_out, "x_sl": ins["x_sl"], "gate_w": ins["gate_w"],
            "combine": cb_ag_in, "xT_new": x1, "xT_new_bf": ag2_in})
    coll("AllGather", ALU.bypass, ag2_in, ag2_out)
    coll("AllGather", ALU.bypass, cb_ag_in, cb_ag_out)
    with kb.scope():
        build_moe(kb=kb, io={
            "xT_bf_full": ag2_out, "w1_sl": ins["w1_sl"], "w2_sl": ins["w2_sl"],
            "combine_sl": cb_ag_out, "esel": ins["esel"], "moeP": rs2_in,
            "w1_bf": w1_bf, "w2_bf": w2_bf, "cast_tok": cast0_tok})
    coll("ReduceScatter", ALU.add, rs2_in, rs2_out)
    with kb.scope():
        build_post("moepost", False, False, n_parts=1, kb=kb, io={
            "parts": rs2_out, "xT_prev": x1, "xT_new": x2, "xT_new_bf": ag3_in})
    coll("AllGather", ALU.bypass, ag3_in, ag3_out)
    with kb.scope():
        build_l1attn(kb=kb, io={
            "xT_bf_full": ag3_out,
            "wq_sl": ins["wq_sl"], "wk_sl": ins["wk_sl"], "wv_sl": ins["wv_sl"],
            "wg_sl": ins["wg_sl"], "wa_sl": ins["wa_sl"], "wb_sl": ins["wb_sl"],
            "ba_sl": ins["ba_sl"], "bb_sl": ins["bb_sl"],
            "qcw_sl": ins["qcw_sl"], "qcb_sl": ins["qcb_sl"],
            "kcw_sl": ins["kcw_sl"], "kcb_sl": ins["kcb_sl"],
            "vcw_sl": ins["vcw_sl"], "vcb_sl": ins["vcb_sl"],
            "ln_g": ins["ln_g"], "ln_b": ins["ln_b"], "ogT": og_loc})
    kb.barrier()
    with kb.scope():
        emit_projpart(kb, {"w_sl": ins["wo_sl"], "inT_loc": og_loc,
                           "partial": rs3_in}, "p3")
    coll("ReduceScatter", ALU.add, rs3_in, rs3_out)
    with kb.scope():
        build_post("l1post", False, False, n_parts=1, kb=kb, io={
            "parts": rs3_out, "xT_prev": x2, "xT_new": x3, "xT_new_bf": ag4_in})
    coll("AllGather", ALU.bypass, ag4_in, ag4_out)
    with kb.scope():
        build_l1ffn(kb=kb, io={
            "xT_bf_full": ag4_out, "wg_sl": ins["wgate_sl"],
            "wu_sl": ins["wup_sl"], "wd_sl": ins["wdown_sl"], "ffnP": rs4_in,
            "wg_bf": wgf_bf, "wu_bf": wuf_bf, "wd_bf": wdf_bf,
            "cast_tok": cast0_tok})
    coll("ReduceScatter", ALU.add, rs4_in, rs4_out)
    with kb.scope():
        build_post("finpost", False, False, final=True, n_parts=1, kb=kb, io={
            "parts": rs4_out, "xT_prev": x3, "x_out": x_out})
    kb.barrier()
    return kb


def _esel(r):
    m = np.zeros((8, 2), np.float32)
    m[2 * r, 0] = 1.0
    m[2 * r + 1, 1] = 1.0
    return m


def kernel_fused(**inputs):
    inp = inputs
    ca = np.ascontiguousarray
    x = inp["L0_attn_x"]
    fc, fs = inp["L0_attn_freqs_cos"], inp["L0_attn_freqs_sin"]
    nc = _prog("fused", build_fused)
    maps = []
    for c in range(8):
        g, r = c // 4, c % 4
        rcm, rsp, rsn = _rope_mats_host(fc, fs, [4 * r + i for i in range(4)])
        sl5 = slice(512 * r, 512 * (r + 1))
        sl4 = slice(4 * r, 4 * (r + 1))
        maps.append({
            "x_sl": ca(x[g, r * 1024:(r + 1) * 1024]),
            "w_cq": inp["L0_attn_w_cq"], "w_ckv": inp["L0_attn_w_ckv"],
            "w_kr": inp["L0_attn_w_k_rope"],
            "q_norm_w": inp["L0_attn_q_norm_w"],
            "kv_norm_w": inp["L0_attn_kv_norm_w"],
            "wdqn_sl": ca(inp["L0_attn_w_dq_nope"][:, sl5]),
            "wdqr_sl": ca(inp["L0_attn_w_dq_rope"][:, 256 * r:256 * (r + 1)]),
            "wdkn_sl": ca(inp["L0_attn_w_dk_nope"][:, sl5]),
            "wdv_sl": ca(inp["L0_attn_w_dv"][:, sl5]),
            "rc": rcm, "rsp": rsp, "rsn": rsn,
            "wproj_sl": ca(inp["L0_attn_w_proj"][sl5, :]),
            "gate_w": inp["L0_ffn_gate_w"],
            "w1_sl": ca(inp["L0_ffn_w1"][2 * r:2 * r + 2]),
            "w2_sl": ca(inp["L0_ffn_w2"][2 * r:2 * r + 2]),
            "esel": _esel(r),
            "wq_sl": ca(inp["L1_attn_Wq"][:, sl5]),
            "wk_sl": ca(inp["L1_attn_Wk"][:, sl5]),
            "wv_sl": ca(inp["L1_attn_Wv"][:, sl5]),
            "wg_sl": ca(inp["L1_attn_Wg"][:, sl5]),
            "wa_sl": ca(inp["L1_attn_Wa"][:, sl4]),
            "wb_sl": ca(inp["L1_attn_Wb"][:, sl4]),
            "ba_sl": ca(inp["L1_attn_ba"][sl4]),
            "bb_sl": ca(inp["L1_attn_bb"][sl4]),
            "qcw_sl": ca(inp["L1_attn_qconv_w"][sl5, 0, :]),
            "qcb_sl": ca(inp["L1_attn_qconv_b"][sl5]),
            "kcw_sl": ca(inp["L1_attn_kconv_w"][sl5, 0, :]),
            "kcb_sl": ca(inp["L1_attn_kconv_b"][sl5]),
            "vcw_sl": ca(inp["L1_attn_vconv_w"][sl5, 0, :]),
            "vcb_sl": ca(inp["L1_attn_vconv_b"][sl5]),
            "ln_g": inp["L1_attn_ln_g"], "ln_b": inp["L1_attn_ln_b"],
            "wo_sl": ca(inp["L1_attn_Wo"][sl5, :]),
            "wgate_sl": ca(inp["L1_ffn_w_gate"][2048 * r:2048 * (r + 1), :]),
            "wup_sl": ca(inp["L1_ffn_w_up"][2048 * r:2048 * (r + 1), :]),
            "wdown_sl": ca(inp["L1_ffn_w_down"][:, 2048 * r:2048 * (r + 1)]),
        })
    res = _spmd(nc, maps)
    out = np.empty((2, 4096, 2048), np.float32)
    for c in range(8):
        g, r = c // 4, c % 4
        out[g, r * 1024:(r + 1) * 1024] = res[c]["x_out"]
    return out


_PROG_CACHE = {}


def _prog(name, builder, *a, **kw):
    if name not in _PROG_CACHE:
        _PROG_CACHE[name] = builder(*a, **kw).build()
    return _PROG_CACHE[name]


def _rope_mats_host(fc, fs, heads):
    rc = np.zeros((4, 64, 64), np.float32)
    rsp = np.zeros((4, 64, 64), np.float32)
    rsn = np.zeros((4, 64, 64), np.float32)
    for i, gh in enumerate(heads):
        for j in range(32):
            rc[i, 2 * j, 2 * j] = fc[gh, j]
            rc[i, 2 * j + 1, 2 * j + 1] = fc[gh, j]
            rsp[i, 2 * j, 2 * j + 1] = fs[gh, j]
            rsn[i, 2 * j + 1, 2 * j] = fs[gh, j]
    return rc, rsp, rsn


def _spmd(nc, in_maps):
    return run_bass_kernel_spmd(nc, in_maps, core_ids=list(range(8))).results


def kernel(**inputs):
    _ = inputs.get("L0_attn_mask")
    return kernel_fused(**inputs)


def kernel_unfused(**inputs):
    inp = inputs
    ca = np.ascontiguousarray
    x = inp["L0_attn_x"]
    fc, fs = inp["L0_attn_freqs_cos"], inp["L0_attn_freqs_sin"]

    nc = _prog("pre0", build_pre0)
    maps = []
    for c in range(8):
        g, r = c // 4, c % 4
        maps.append({
            "x_sl": ca(x[g, r * 1024:(r + 1) * 1024]),
            "w_cq": inp["L0_attn_w_cq"], "w_ckv": inp["L0_attn_w_ckv"],
            "w_kr": inp["L0_attn_w_k_rope"],
            "q_norm_w": inp["L0_attn_q_norm_w"],
            "kv_norm_w": inp["L0_attn_kv_norm_w"],
        })
    r1 = _spmd(nc, maps)

    nq_full = [np.concatenate([r1[4 * g + i]["nqT"] for i in range(4)], axis=1)
               for g in range(2)]
    nkv_full = [np.concatenate([r1[4 * g + i]["nkvT"] for i in range(4)], axis=1)
                for g in range(2)]
    kr_full = [np.concatenate([r1[4 * g + i]["kropeT"] for i in range(4)], axis=1)
               for g in range(2)]
    nc = _prog("l0attn", build_l0attn)
    maps = []
    for c in range(8):
        g, r = c // 4, c % 4
        rc, rsp, rsn = _rope_mats_host(fc, fs, [4 * r + i for i in range(4)])
        maps.append({
            "nqT_full": ca(nq_full[g]), "nkvT_full": ca(nkv_full[g]),
            "kropeT_full": ca(kr_full[g]),
            "wdqn_sl": ca(inp["L0_attn_w_dq_nope"][:, 512 * r:512 * (r + 1)]),
            "wdqr_sl": ca(inp["L0_attn_w_dq_rope"][:, 256 * r:256 * (r + 1)]),
            "wdkn_sl": ca(inp["L0_attn_w_dk_nope"][:, 512 * r:512 * (r + 1)]),
            "wdv_sl": ca(inp["L0_attn_w_dv"][:, 512 * r:512 * (r + 1)]),
            "rc": rc, "rsp": rsp, "rsn": rsn,
        })
    r2 = _spmd(nc, maps)

    attn_full = [np.concatenate([r2[4 * g + i]["attnT"] for i in range(4)], axis=0)
                 for g in range(2)]
    nc = _prog("l0post", build_post, "l0post", True, True, gating=True)
    maps = []
    for c in range(8):
        g, r = c // 4, c % 4
        maps.append({
            "inT_sl": ca(attn_full[g][:, r * 1024:(r + 1) * 1024]),
            "w": inp["L0_attn_w_proj"],
            "x_sl": ca(x[g, r * 1024:(r + 1) * 1024]),
            "gate_w": inp["L0_ffn_gate_w"],
        })
    r3 = _spmd(nc, maps)

    x1bf_full = [np.concatenate([r3[4 * g + i]["xT_new_bf"] for i in range(4)],
                                axis=1) for g in range(2)]
    comb_full = [np.concatenate([r3[4 * g + i]["combine"] for i in range(4)],
                                axis=0) for g in range(2)]
    nc = _prog("moe", build_moe)
    maps = []
    for c in range(8):
        g, r = c // 4, c % 4
        maps.append({
            "xT_bf_full": ca(x1bf_full[g]),
            "w1_sl": ca(inp["L0_ffn_w1"][2 * r:2 * r + 2]),
            "w2_sl": ca(inp["L0_ffn_w2"][2 * r:2 * r + 2]),
            "combine_sl": ca(comb_full[g][:, 2 * r:2 * r + 2]),
        })
    r4 = _spmd(nc, maps)

    nc = _prog("sumpost", build_post, "sumpost", False, False)
    maps = []
    for c in range(8):
        g, r = c // 4, c % 4
        parts = np.stack([r4[4 * g + i]["moeP"][:, r * 1024:(r + 1) * 1024]
                          for i in range(4)], axis=0)
        maps.append({"parts": ca(parts), "xT_prev": ca(r3[c]["xT_new"])})
    r5 = _spmd(nc, maps)

    x2bf_full = [np.concatenate([r5[4 * g + i]["xT_new_bf"] for i in range(4)],
                                axis=1) for g in range(2)]
    nc = _prog("l1attn", build_l1attn)
    maps = []
    for c in range(8):
        g, r = c // 4, c % 4
        sl5 = slice(512 * r, 512 * (r + 1))
        sl4 = slice(4 * r, 4 * (r + 1))
        maps.append({
            "xT_bf_full": ca(x2bf_full[g]),
            "wq_sl": ca(inp["L1_attn_Wq"][:, sl5]),
            "wk_sl": ca(inp["L1_attn_Wk"][:, sl5]),
            "wv_sl": ca(inp["L1_attn_Wv"][:, sl5]),
            "wg_sl": ca(inp["L1_attn_Wg"][:, sl5]),
            "wa_sl": ca(inp["L1_attn_Wa"][:, sl4]),
            "wb_sl": ca(inp["L1_attn_Wb"][:, sl4]),
            "ba_sl": ca(inp["L1_attn_ba"][sl4]),
            "bb_sl": ca(inp["L1_attn_bb"][sl4]),
            "qcw_sl": ca(inp["L1_attn_qconv_w"][sl5, 0, :]),
            "qcb_sl": ca(inp["L1_attn_qconv_b"][sl5]),
            "kcw_sl": ca(inp["L1_attn_kconv_w"][sl5, 0, :]),
            "kcb_sl": ca(inp["L1_attn_kconv_b"][sl5]),
            "vcw_sl": ca(inp["L1_attn_vconv_w"][sl5, 0, :]),
            "vcb_sl": ca(inp["L1_attn_vconv_b"][sl5]),
            "ln_g": inp["L1_attn_ln_g"], "ln_b": inp["L1_attn_ln_b"],
        })
    r6 = _spmd(nc, maps)

    og_full = [np.concatenate([r6[4 * g + i]["ogT"] for i in range(4)], axis=0)
               for g in range(2)]
    nc = _prog("l1post", build_post, "l1post", True, False)
    maps = []
    for c in range(8):
        g, r = c // 4, c % 4
        maps.append({
            "inT_sl": ca(og_full[g][:, r * 1024:(r + 1) * 1024]),
            "w": inp["L1_attn_Wo"],
            "xT_prev": ca(r5[c]["xT_new"]),
        })
    r7 = _spmd(nc, maps)

    x3bf_full = [np.concatenate([r7[4 * g + i]["xT_new_bf"] for i in range(4)],
                                axis=1) for g in range(2)]
    nc = _prog("l1ffn", build_l1ffn)
    maps = []
    for c in range(8):
        g, r = c // 4, c % 4
        maps.append({
            "xT_bf_full": ca(x3bf_full[g]),
            "wg_sl": ca(inp["L1_ffn_w_gate"][2048 * r:2048 * (r + 1), :]),
            "wu_sl": ca(inp["L1_ffn_w_up"][2048 * r:2048 * (r + 1), :]),
            "wd_sl": ca(inp["L1_ffn_w_down"][:, 2048 * r:2048 * (r + 1)]),
        })
    r8 = _spmd(nc, maps)

    nc = _prog("finpost", build_post, "finpost", False, False, final=True)
    maps = []
    for c in range(8):
        g, r = c // 4, c % 4
        parts = np.stack([r8[4 * g + i]["ffnP"][:, r * 1024:(r + 1) * 1024]
                          for i in range(4)], axis=0)
        maps.append({"parts": ca(parts), "xT_prev": ca(r7[c]["xT_new"])})
    r9 = _spmd(nc, maps)

    out = np.empty((2, 4096, 2048), np.float32)
    for c in range(8):
        g, r = c // 4, c % 4
        out[g, r * 1024:(r + 1) * 1024] = r9[c]["x_out"]
    return out
